# version 11
# baseline (speedup 1.0000x reference)
import numpy as np
import ml_dtypes
from contextlib import ExitStack

import concourse.bass as bass
import concourse.mybir as mybir
from concourse.bass_utils import run_bass_kernel_spmd

F32 = mybir.dt.float32
BF16 = mybir.dt.bfloat16
I32 = mybir.dt.int32
AF = mybir.ActivationFunctionType
ALU = mybir.AluOpType
AX = mybir.AxisListType
BF = ml_dtypes.bfloat16

NCORE = 8
TOK = 2048
D = 1024
NEG = -1.0e30

ENGINES = ("sync", "tensor", "vector", "scalar", "gpsimd")


class Prog:

    def __init__(self, nc, stack):
        self.nc = nc
        self.stack = stack
        self.streams = {e: [] for e in ENGINES}
        self._sems = {}
        self.counts = {}

    def _sem(self, name):
        if name not in self._sems:
            self._sems[name] = self.stack.enter_context(self.nc.semaphore(name))
            self.counts[name] = 0
        return self._sems[name]

    def op(self, engine, fn, waits=(), inc=None, inc_k=1):
        tok = None
        if inc is not None:
            self._sem(inc)
            self.counts[inc] += inc_k
            tok = (inc, self.counts[inc])
        ws = []
        for w in waits:
            if w is not None:
                self._sem(w[0])
                ws.append(w)
        self.streams[engine].append((fn, ws, (inc, inc_k) if inc else None))
        return tok

    def dma(self, queue, out, in_, waits=(), sem=None):
        sem = sem or ("Lg" if queue == "gpsimd" else "L")
        return self.op(queue, lambda e, o=out, i=in_: e.dma_start(out=o, in_=i),
                       waits=waits, inc=sem, inc_k=16)

    def last(self, sem):
        return (sem, self.counts.get(sem, 0))

    def barrier(self):
        toks = [(s, v) for s, v in self.counts.items() if v > 0]
        for eng in ENGINES:
            self.op(eng, lambda e: e.nop(), waits=toks)

    def build(self):
        block = self.stack.enter_context(self.nc.Block())
        for eng_name in ENGINES:
            stream = self.streams[eng_name]
            if not stream:
                continue

            def body(eng, stream=stream):
                for fn, waits, inc in stream:
                    for (s, v) in waits:
                        if v > 0:
                            eng.wait_ge(self._sems[s], v)
                    inst = fn(eng)
                    if inc is not None:
                        inst.then_inc(self._sems[inc[0]], inc[1])
            getattr(block, eng_name)(body)


def sb(nc, stack, name, shape, dtype):
    return stack.enter_context(nc.sbuf_tensor(name, shape, dtype))


def ps(nc, stack, name, shape, dtype=F32):
    return stack.enter_context(nc.psum_tensor(name, shape, dtype))



def emit_ln(p, c, hsb, t_ready, out_bf=None, var_save=None):
    TC = hsb.shape[2] if hasattr(hsb, "shape") else 1024
    hb, hsq, srows, PA, PB = c["hb"], c["hsq"], c["srows"], c["PA"], c["PB"]
    lncol, onesk1 = c["lncol"], c["onesk1"]
    toks = []
    for kt in range(8):
        t1 = p.op("scalar", lambda e, kt=kt: e.copy(out=hb(kt), in_=hsb[:, kt, :]),
                  waits=[t_ready] if kt == 0 else (), inc="SC")
        t2 = p.op("vector", lambda e, kt=kt: e.tensor_mul(
            out=hsq(kt), in0=hb(kt), in1=hb(kt)),
            waits=[t1], inc="DV")
        toks.append(t2)
    guard = toks[-1]
    apl_last = None
    for n2 in range(TC // 512):
        s1 = s2 = None
        for kt in range(8):
            s1 = p.op("tensor", lambda e, kt=kt, n2=n2: e.matmul(
                out=PA[0:1, 0:512], lhsT=lncol[:],
                rhs=hb(kt)[:, n2 * 512:(n2 + 1) * 512],
                start=(kt == 0), stop=(kt == 7)),
                waits=[guard] if kt == 0 else (), inc="PE" if kt == 7 else None)
        for kt in range(8):
            s2 = p.op("tensor", lambda e, kt=kt, n2=n2: e.matmul(
                out=PB[0:1, 0:512], lhsT=lncol[:],
                rhs=hsq(kt)[:, n2 * 512:(n2 + 1) * 512],
                start=(kt == 0), stop=(kt == 7)),
                inc="PE" if kt == 7 else None)
        rm = p.op("vector", lambda e: e.tensor_copy(out=srows[:, 0:512], in_=PA[0:1, 0:512]),
                  waits=[s1], inc="DV")
        rv1 = p.op("vector", lambda e: e.tensor_mul(
            out=srows[:, 1536:2048], in0=srows[:, 0:512], in1=srows[:, 0:512]),
            waits=[rm], inc="DV")
        rv2 = p.op("vector", lambda e: e.scalar_tensor_tensor(
            out=srows[:, 1536:2048], in0=srows[:, 1536:2048], scalar=-1.0,
            in1=PB[0:1, 0:512], op0=ALU.mult, op1=ALU.add),
            waits=[s2, rv1], inc="DV")
        if var_save is not None:
            p.op("vector", lambda e, n2=n2: e.tensor_copy(
                out=var_save[:, n2 * 512:(n2 + 1) * 512], in_=srows[:, 1536:2048]),
                waits=[rv2], inc="DV")
        rv3 = p.op("vector", lambda e: e.tensor_scalar_add(
            out=srows[:, 1536:2048], in0=srows[:, 1536:2048], scalar1=1e-5),
            waits=[rv2], inc="DV")
        rs = p.op("scalar", lambda e: e.activation(
            out=srows[:, 512:1024], in_=srows[:, 1536:2048],
            func=AF.Sqrt),
            waits=[rv3], inc="SC")
        rr = p.op("vector", lambda e: e.reciprocal(
            out=srows[:, 512:1024], in_=srows[:, 512:1024]),
            waits=[rs], inc="DV")
        rmr = p.op("vector", lambda e: e.tensor_mul(
            out=srows[:, 1024:1536], in0=srows[:, 0:512], in1=srows[:, 512:1024]),
            waits=[rr], inc="DV")
        bc1 = p.op("tensor", lambda e: e.matmul(
            out=PA[:, 0:512], lhsT=onesk1[:], rhs=srows[0:1, 512:1024],
            start=True, stop=True), waits=[rmr], inc="PE")
        bc2 = p.op("tensor", lambda e: e.matmul(
            out=PB[:, 0:512], lhsT=onesk1[:], rhs=srows[0:1, 1024:1536],
            start=True, stop=True), inc="PE")
        for kt in range(8):
            a1 = p.op("vector", lambda e, kt=kt, n2=n2: e.tensor_mul(
                out=hsb[:, kt, n2 * 512:(n2 + 1) * 512],
                in0=hsb[:, kt, n2 * 512:(n2 + 1) * 512], in1=PA[:, 0:512]),
                waits=[bc1] if kt == 0 else (), inc="DV")
            a2 = p.op("vector", lambda e, kt=kt, n2=n2: e.tensor_sub(
                out=hsb[:, kt, n2 * 512:(n2 + 1) * 512],
                in0=hsb[:, kt, n2 * 512:(n2 + 1) * 512], in1=PB[:, 0:512]),
                waits=[bc2] if kt == 0 else (), inc="DV")
            if out_bf is not None:
                p.op("scalar", lambda e, kt=kt, n2=n2: e.copy(
                    out=out_bf[:, kt, n2 * 512:(n2 + 1) * 512],
                    in_=hsb[:, kt, n2 * 512:(n2 + 1) * 512]),
                    waits=[a2], inc="SC")
            apl_last = a2
        guard = apl_last
    return apl_last



def emit_l0attn(p, nc, stack, io):
    TC = 1024
    NCH = TOK // TC
    sx = sb(nc, stack, "sx", [128, 8, TC], F32)
    sxb = sb(nc, stack, "sxb", [128, 8, TC], BF16)
    swq = sb(nc, stack, "swq", [128, 8, 1024], BF16)
    swk = sb(nc, stack, "swk", [128, 8, 1024], BF16)
    swv = sb(nc, stack, "swv", [128, 8, 1024], BF16)
    swo = sb(nc, stack, "swo", [128, 8, 1024], BF16)
    qb = sb(nc, stack, "qb", [128, 8, TC], BF16)
    kb = sb(nc, stack, "kb", [128, 8, TC], BF16)
    vt = sb(nc, stack, "vt", [128, 8, TC], BF16)
    ob = sb(nc, stack, "ob", [128, 8, TC], BF16)
    smA = sb(nc, stack, "smA", [128, 512], BF16)
    smB = sb(nc, stack, "smB", [128, 1024], BF16)
    sident = sb(nc, stack, "sident", [128, 128], BF16)
    slncol = sb(nc, stack, "slncol", [128, 1], BF16)
    sonesk1 = sb(nc, stack, "sonesk1", [1, 128], F32)
    pf = sb(nc, stack, "pf", [128, 1024], F32)
    sexpB = sb(nc, stack, "sexpB", [128, 1024], F32)
    pbA = sb(nc, stack, "pbA", [128, 512], BF16)
    pbB = sb(nc, stack, "pbB", [128, 1024], BF16)
    scol = [sb(nc, stack, "scol0", [128, 8, 1], F32)]
    pT = sb(nc, stack, "pT", [128, 768], BF16)

    srows = sb(nc, stack, "srows", [1, 2048], F32)

    P0 = ps(nc, stack, "P0", [128, 512])
    P1 = ps(nc, stack, "P1", [128, 512])
    P2 = ps(nc, stack, "P2", [128, 512])
    P3 = ps(nc, stack, "P3", [128, 512])
    P4 = ps(nc, stack, "P4", [128, 512])
    PT = ps(nc, stack, "PT", [128, 1024], BF16)
    PT2 = ps(nc, stack, "PT2", [128, 1024], BF16)

    p.dma("gpsimd", swq[:], io["wq"].rearrange("(kt p) m -> p kt m", p=128))
    p.dma("gpsimd", swk[:], io["wk"].rearrange("(kt p) m -> p kt m", p=128))
    p.dma("gpsimd", swv[:], io["wv"].rearrange("(kt p) m -> p kt m", p=128))
    wall = p.dma("gpsimd", swo[:], io["wo"].rearrange("(kt p) m -> p kt m", p=128))
    p.dma("sync", smA[:], io["maskA"], sem="C")
    p.dma("sync", smB[:], io["maskB"], sem="C")
    p.dma("sync", sident[:], io["ident"], sem="C")
    p.dma("sync", slncol[:], io["lncol"], sem="C")
    cC = p.dma("sync", sonesk1[:], io["onesk1"], sem="C")
    cA = cB = cI = cL = cK = cC

    lnctx = dict(lncol=slncol, onesk1=sonesk1,
                 hb=lambda kt: qb[:, kt, :], hsq=lambda kt: kb[:, kt, :],
                 srows=srows, PA=P2, PB=P3)

    prev_store = None
    prev_apply = None
    for tc in range(NCH):
        t0 = tc * TC
        lx = p.dma("sync", sx[:],
                   io["xT"].rearrange("(kt p) t -> p kt t", p=128)[:, :, t0:t0 + TC],
                   waits=[prev_store] if prev_store else (), sem="Lx")
        cast_done = None
        for kt in range(8):
            cast_done = p.op("vector", lambda e, kt=kt: e.tensor_copy(
                out=sxb[:, kt, :], in_=sx[:, kt, :]),
                waits=[lx, prev_apply] if kt == 0 else (), inc="DV")

        groups = ([("q", m, n2) for m in range(8) for n2 in range(TC // 512)]
                  + [("k", m, n2) for m in range(8) for n2 in range(TC // 512)]
                  + [("v", tt, n2) for tt in range(8) for n2 in range(2)])
        prev_evac = {}
        for gi, (kind, m, n2) in enumerate(groups):
            pp = (P0, P1)[gi % 2]
            w0 = [cast_done, wall]
            if gi - 2 >= 0:
                w0 = [prev_evac[gi - 2]] if gi > 2 else w0 + [prev_evac[gi - 2]]
            tok = None
            for kt in range(8):
                if kind == "q":
                    fn = (lambda e, kt=kt, m=m, n2=n2, pp=pp: e.matmul(
                        out=pp[:], lhsT=swq[:, kt, m * 128:(m + 1) * 128],
                        rhs=sxb[:, kt, n2 * 512:(n2 + 1) * 512],
                        start=(kt == 0), stop=(kt == 7)))
                elif kind == "k":
                    fn = (lambda e, kt=kt, m=m, n2=n2, pp=pp: e.matmul(
                        out=pp[:], lhsT=swk[:, kt, m * 128:(m + 1) * 128],
                        rhs=sxb[:, kt, n2 * 512:(n2 + 1) * 512],
                        start=(kt == 0), stop=(kt == 7)))
                else:
                    fn = (lambda e, kt=kt, tt=m, n2=n2, pp=pp: e.matmul(
                        out=pp[:], lhsT=sxb[:, kt, tt * 128:(tt + 1) * 128],
                        rhs=swv[:, kt, n2 * 512:(n2 + 1) * 512],
                        start=(kt == 0), stop=(kt == 7)))
                tok = p.op("tensor", fn, waits=w0 if kt == 0 else (),
                           inc="PE" if kt == 7 else None)
            dst = {"q": qb, "k": kb, "v": vt}[kind]
            ev = p.op("vector", lambda e, dst=dst, m=m, n2=n2, pp=pp: e.tensor_copy(
                out=dst[:, m, n2 * 512:(n2 + 1) * 512], in_=pp[:]),
                waits=[tok], inc="DV")
            prev_evac[gi] = ev
        qkv_done = prev_evac[len(groups) - 1]

        stSC = {}
        stN = {}
        stB2 = {}

        def quad(hh):
            return dict(
                hh=hh, base=(hh % 2) * 64, mt=hh // 2,
                PA=P0, PBs=(P2, P3),
                seA=pf[:, 0:512], seB=sexpB[:, 0:1024],
                pbAs=pbA[:, 0:512], pbBs=pbB[:, 0:1024], sc=scol[0])

        def emit_scores(hh):
            r = quad(hh)
            base, mt, PA, PBs = r["base"], r["mt"], r["PA"], r["PBs"]
            w0 = [qkv_done, cC] if hh == 0 else []
            wprev = [stSC[hh - 1]] if hh >= 1 else []
            first = True
            for c in range(4):
                q_ap = qb[base:base + 64, mt, c * 256:(c + 1) * 256]
                k_ap = kb[base:base + 64, mt, c * 256:(c + 1) * 256]
                p.op("tensor", lambda e, q_ap=q_ap, k_ap=k_ap, PA=PA, c=c: e.matmul(
                    out=PA[:, c * 128:(c + 1) * 128], lhsT=q_ap[:, 0:128],
                    rhs=k_ap[:, 0:128], start=(c == 0), stop=(c == 3)),
                    waits=(w0 + wprev) if first else ())
                first = False
            tok = None
            for c in range(4):
                q_ap = qb[base:base + 64, mt, c * 256:(c + 1) * 256]
                k_ap = kb[base:base + 64, mt, c * 256:(c + 1) * 256]
                pb_ = PBs[c // 2]
                cc = c % 2
                tok = p.op("tensor", lambda e, q_ap=q_ap, k_ap=k_ap, pb_=pb_, cc=cc:
                           e.matmul(out=pb_[:, cc * 256:(cc + 1) * 256],
                                    lhsT=q_ap[:, 128:256], rhs=k_ap[:, 0:256],
                                    start=(cc == 0), stop=(cc == 1)),
                           inc="PE" if c == 3 else None)
            wn = [stN[hh - 1][0], stN[hh - 1][1]] if hh >= 1 else []
            x1 = p.op("scalar", lambda e, r=r: e.activation(
                out=r["seA"], in_=r["PA"][:, 0:512], func=AF.Exp, scale=0.125),
                waits=[tok] + wn, inc="SC")
            p.op("scalar", lambda e, r=r: e.activation(
                out=r["seB"][:, 0:512], in_=r["PBs"][0][:, 0:512], func=AF.Exp,
                scale=0.125), inc="SC")
            x3 = p.op("scalar", lambda e, r=r: e.activation(
                out=r["seB"][:, 512:1024], in_=r["PBs"][1][:, 0:512], func=AF.Exp,
                scale=0.125), inc="SC")
            stSC[hh] = x3

        def emit_mid(hh):
            r = quad(hh)
            w0 = [stSC[hh]]
            wre = [stB2[hh - 1][0]] if hh >= 1 else []
            m1 = p.op("gpsimd", lambda e, r=r: e.tensor_mul(
                out=r["seA"], in0=r["seA"], in1=smA[:]), waits=w0, inc="GP")
            m2 = p.op("gpsimd", lambda e, r=r: e.tensor_mul(
                out=r["seB"], in0=r["seB"], in1=smB[:]), inc="GP")
            r1 = p.op("vector", lambda e, r=r: e.tensor_reduce(
                out=r["sc"][:, 0:4, :],
                in_=r["seA"].rearrange("p (c k) -> p c k", c=4),
                axis=AX.X, op=ALU.add), waits=[m1], inc="DV")
            r2 = p.op("vector", lambda e, r=r: e.tensor_reduce(
                out=r["sc"][:, 4:8, :],
                in_=r["seB"].rearrange("p (c k) -> p c k", c=4),
                axis=AX.X, op=ALU.add), waits=[m2], inc="DV")
            t = p.op("vector", lambda e, r=r: e.reciprocal(
                out=r["sc"][:, :, 0], in_=r["sc"][:, :, 0]), waits=[r1, r2], inc="DV")
            ntd = nts = None
            for c in range(4):
                ntd = p.op("vector", lambda e, r=r, c=c: e.tensor_scalar_mul(
                    out=r["pbAs"][:, c * 128:(c + 1) * 128],
                    in0=r["seA"][:, c * 128:(c + 1) * 128],
                    scalar1=r["sc"][:, c, :]),
                    waits=([t] + wre) if c == 0 else (), inc="DV")
                nts = p.op("scalar", lambda e, r=r, c=c: e.mul(
                    out=r["pbBs"][:, c * 256:(c + 1) * 256],
                    in_=r["seB"][:, c * 256:(c + 1) * 256],
                    mul=r["sc"][:, 4 + c, :]),
                    waits=([t] + wre) if c == 0 else (), inc="SC")
            stN[hh] = (ntd, nts)

        def emit_tb(hh):
            r = quad(hh)
            base, mt = r["base"], r["mt"]
            ev = None
            ecop = {}
            for c in range(4):
                PTs = (PT, PT2)[c % 2]
                pTs = pT[:, (c % 2) * 384:(c % 2) * 384 + 384]
                wT = [stN[hh][0], stN[hh][1]]
                if c >= 2:
                    wT.append(ecop[c - 2])
                elif hh >= 1:
                    wT.append(stB2[hh - 1][0])
                tA = p.op("tensor", lambda e, r=r, PTs=PTs, c=c: e.transpose(
                    out=PTs[:, 0:128], in_=r["pbAs"][:, c * 128:(c + 1) * 128],
                    identity=sident[:]), waits=wT, inc="PE")
                p.op("tensor", lambda e, r=r, PTs=PTs, c=c: e.transpose(
                    out=PTs[:, 128:256], in_=r["pbBs"][:, c * 256:c * 256 + 128],
                    identity=sident[:]))
                tB1 = p.op("tensor", lambda e, r=r, PTs=PTs, c=c: e.transpose(
                    out=PTs[:, 256:384], in_=r["pbBs"][:, c * 256 + 128:c * 256 + 256],
                    identity=sident[:]), inc="PE")
                o1 = p.op("scalar", lambda e, pTs=pTs, PTs=PTs: e.copy(
                    out=pTs[:, 0:128], in_=PTs[:, 0:128]), waits=[tA], inc="SC")
                e2last = p.op("scalar", lambda e, pTs=pTs, PTs=PTs: e.copy(
                    out=pTs[:, 128:384], in_=PTs[:, 128:384]), waits=[tB1], inc="SC")
                ecop[c] = e2last
                wpv = [o1, e2last]
                if ev is not None:
                    wpv.append(ev)
                elif hh >= 1:
                    wpv.append(stB2[hh - 1][1])
                h2 = 2 * mt + (base // 64)
                p.op("tensor", lambda e, c=c, h2=h2, base=base, pTs=pTs: e.matmul(
                    out=P4[base:base + 64, 0:128],
                    lhsT=vt[:, 2 * c, h2 * 64:(h2 + 1) * 64],
                    rhs=pTs[:, 0:128], start=True, stop=False), waits=wpv)
                p.op("tensor", lambda e, c=c, h2=h2, base=base, pTs=pTs: e.matmul(
                    out=P4[base:base + 64, 128:256],
                    lhsT=vt[:, 2 * c, h2 * 64:(h2 + 1) * 64],
                    rhs=pTs[:, 128:256], start=False, stop=False))
                pB1 = p.op("tensor", lambda e, c=c, h2=h2, base=base, pTs=pTs: e.matmul(
                    out=P4[base:base + 64, 128:256],
                    lhsT=vt[:, 2 * c + 1, h2 * 64:(h2 + 1) * 64],
                    rhs=pTs[:, 256:384], start=False, stop=True), inc="PE")
                ev = p.op("vector", lambda e, c=c, mt=mt, base=base: e.tensor_copy(
                    out=ob[base:base + 64, mt, c * 256:(c + 1) * 256],
                    in_=P4[base:base + 64, 0:256]), waits=[pB1], inc="DV")
            stB2[hh] = (ecop[3], ev)

        for hh in range(16):
            emit_scores(hh)
            if hh >= 1:
                emit_tb(hh - 1)
            emit_mid(hh)
        emit_tb(15)
        attn_done = stB2[15][1]

        prev_res = {}
        h_done = None
        for gi2 in range(16):
            m, n2 = gi2 // 2, gi2 % 2
            pp = (P0, P1)[gi2 % 2]
            w0 = [attn_done] if gi2 < 2 else [prev_res[gi2 - 2]]
            tok = None
            for kt in range(8):
                tok = p.op("tensor", lambda e, kt=kt, m=m, n2=n2, pp=pp: e.matmul(
                    out=pp[:], lhsT=swo[:, kt, m * 128:(m + 1) * 128],
                    rhs=ob[:, kt, n2 * 512:(n2 + 1) * 512],
                    start=(kt == 0), stop=(kt == 7)),
                    waits=w0 if kt == 0 else (), inc="PE" if kt == 7 else None)
            ev = p.op("vector", lambda e, m=m, n2=n2, pp=pp: e.tensor_add(
                out=sx[:, m, n2 * 512:(n2 + 1) * 512],
                in0=sx[:, m, n2 * 512:(n2 + 1) * 512], in1=pp[:]),
                waits=[tok], inc="DV")
            prev_res[gi2] = ev
            h_done = ev

        prev_apply = emit_ln(p, lnctx, sx, h_done)

        prev_store = p.dma(
            "sync",
            io["outT"].rearrange("(kt p) t -> p kt t", p=128)[:, :, t0:t0 + TC],
            sx[:], waits=[prev_apply], sem="ST")

    p.op("sync", lambda e: e.nop(), waits=[p.last("ST")])


def build_l0attn():
    nc = bass.Bass("TRN2", target_bir_lowering=False, debug=False)
    io = {}
    io["xT"] = nc.dram_tensor("xT", [D, TOK], F32, kind="ExternalInput").ap()
    for w in ("wq", "wk", "wv", "wo"):
        io[w] = nc.dram_tensor(w, [D, D], F32, kind="ExternalInput").ap()
    io["maskA"] = nc.dram_tensor("maskA", [128, 512], BF16, kind="ExternalInput").ap()
    io["maskB"] = nc.dram_tensor("maskB", [128, 1024], BF16, kind="ExternalInput").ap()
    io["ident"] = nc.dram_tensor("ident", [128, 128], BF16, kind="ExternalInput").ap()
    io["lncol"] = nc.dram_tensor("lncol", [128, 1], BF16, kind="ExternalInput").ap()
    io["onesk1"] = nc.dram_tensor("onesk1", [1, 128], F32, kind="ExternalInput").ap()
    io["outT"] = nc.dram_tensor("outT", [D, TOK], F32, kind="ExternalOutput").ap()
    stack = ExitStack()
    with stack:
        p = Prog(nc, stack)
        emit_l0attn(p, nc, stack, io)
        p.build()
    return nc


def common_consts():
    i = np.arange(128)[:, None]
    jA = np.arange(128)[None, :]
    mA = np.where((jA <= i) & (jA >= i - 64), 1.0, 0.0).astype(BF)
    i2 = (np.arange(128) + 128)[:, None]
    jB = np.arange(256)[None, :]
    mB = np.where((jB <= i2) & (jB >= i2 - 64), 1.0, 0.0).astype(BF)
    return dict(
        maskA=np.tile(mA, (1, 4)).astype(BF), maskB=np.tile(mB, (1, 4)).astype(BF),
        ident=np.eye(128, dtype=BF),
        lncol=np.full((128, 1), 1.0 / 1024.0, dtype=BF),
        onesk1=np.ones((1, 128), dtype=np.float32),
    )





def emit_l0ffn(p, nc, stack, io):
    S = 1.0 / 127.0
    TC = 1024
    sx = sb(nc, stack, "fx", [128, 8, TOK], F32)
    xb = sb(nc, stack, "fxb", [128, 8, TOK], BF16)
    hact = sb(nc, stack, "fhact", [128, 24, TC], BF16)
    wgs = sb(nc, stack, "fwgs", [128, 2, 8, 128], BF16)
    wus = sb(nc, stack, "fwus", [128, 2, 8, 128], BF16)
    wds = sb(nc, stack, "fwds", [128, 2, 24, 128], BF16)
    slncol = sb(nc, stack, "flncol", [128, 1], BF16)
    sonesk1 = sb(nc, stack, "fonesk1", [1, 128], F32)
    srows = sb(nc, stack, "fsrows", [1, 2048], F32)
    t1 = sb(nc, stack, "ft1", [128, 512], F32)

    P0 = ps(nc, stack, "fP0", [128, 512])
    P1 = ps(nc, stack, "fP1", [128, 512])
    P2 = ps(nc, stack, "fP2", [128, 512])
    P3 = ps(nc, stack, "fP3", [128, 512])

    p.dma("sync", slncol[:], io["lncol"], sem="C")
    cK = p.dma("sync", sonesk1[:], io["onesk1"], sem="C")
    cL = cK
    lx = p.dma("sync", sx[:], io["xT"].rearrange("(kt p) t -> p kt t", p=128), sem="Lx")
    cast_done = None
    for kt in range(8):
        cast_done = p.op("vector", lambda e, kt=kt: e.tensor_copy(
            out=xb[:, kt, :], in_=sx[:, kt, :]), waits=[lx] if kt == 0 else (),
            inc="DV")

    res_last = None
    for tc in range(TOK // TC):
        t0 = tc * TC
        dma_toks = {}
        fuse_tok = res_last
        for mh in range(2):
            bi = mh % 2
            p.dma("gpsimd", wgs[:, bi, :, :],
                  io["wgT"][:, mh * 128:(mh + 1) * 128]
                  .rearrange("(kt p) m -> p kt m", p=128),
                  waits=[res_last] if res_last else ())
            dma_toks[mh] = p.dma("gpsimd", wus[:, bi, :, :],
                                 io["wuT"][:, mh * 128:(mh + 1) * 128]
                                 .rearrange("(kt p) m -> p kt m", p=128))
        for mh in range(24):
            bi = mh % 2
            for n in range(TC // 512):
                w0 = [dma_toks[max(mh, 1)] if mh < 2 else dma_toks[mh],
                      cast_done, fuse_tok]
                last_pe = None
                for kt in range(8):
                    p.op("tensor", lambda e, kt=kt, bi=bi, n=n, t0=t0: e.matmul(
                        out=P0[:], lhsT=wgs[:, bi, kt, :],
                        rhs=xb[:, kt, t0 + n * 512:t0 + (n + 1) * 512],
                        start=(kt == 0), stop=(kt == 7)),
                        waits=w0 if kt == 0 else ())
                for kt in range(8):
                    last_pe = p.op("tensor", lambda e, kt=kt, bi=bi, n=n, t0=t0: e.matmul(
                        out=P1[:], lhsT=wus[:, bi, kt, :],
                        rhs=xb[:, kt, t0 + n * 512:t0 + (n + 1) * 512],
                        start=(kt == 0), stop=(kt == 7)),
                        inc="PE" if kt == 7 else None)
                a1 = p.op("scalar", lambda e: e.activation(
                    out=t1[:], in_=P0[:], func=AF.Silu, scale=S),
                    waits=[last_pe], inc="SC")
                fuse_tok = p.op("vector", lambda e, mh=mh, n=n: e.tensor_mul(
                    out=hact[:, mh, n * 512:(n + 1) * 512], in0=t1[:], in1=P1[:]),
                    waits=[a1], inc="DV")
            gtok = p.last("PE")
            nxt = mh + 2
            if nxt < 24:
                p.dma("gpsimd", wgs[:, nxt % 2, :, :],
                      io["wgT"][:, nxt * 128:(nxt + 1) * 128]
                      .rearrange("(kt p) m -> p kt m", p=128), waits=[gtok])
                dma_toks[nxt] = p.dma("gpsimd", wus[:, nxt % 2, :, :],
                                      io["wuT"][:, nxt * 128:(nxt + 1) * 128]
                                      .rearrange("(kt p) m -> p kt m", p=128))
        hact_done = fuse_tok

        ddma = {}
        for m in range(2):
            ddma[m] = p.dma("gpsimd", wds[:, m % 2, :, :],
                            io["wdT"][:, m * 128:(m + 1) * 128]
                            .rearrange("(kt p) mm -> p kt mm", p=128),
                            waits=[hact_done] if tc > 0 else ())
        for m in range(8):
            bi = m % 2
            for n in range(TC // 512):
                w0 = [ddma[max(m, 1)] if m < 2 else ddma[m], hact_done, res_last]
                tok = None
                for kt in range(24):
                    tok = p.op("tensor", lambda e, kt=kt, bi=bi, n=n: e.matmul(
                        out=P2[:], lhsT=wds[:, bi, kt, :],
                        rhs=hact[:, kt, n * 512:(n + 1) * 512],
                        start=(kt == 0), stop=(kt == 23)),
                        waits=w0 if kt == 0 else (), inc="PE" if kt == 23 else None)
                res_last = p.op("vector", lambda e, m=m, n=n, t0=t0: e.scalar_tensor_tensor(
                    out=sx[:, m, t0 + n * 512:t0 + (n + 1) * 512], in0=P2[:], scalar=S * S,
                    in1=sx[:, m, t0 + n * 512:t0 + (n + 1) * 512],
                    op0=ALU.mult, op1=ALU.add),
                    waits=[tok], inc="DV")
            dtok = p.last("PE")
            nxt = m + 2
            if nxt < 8:
                ddma[nxt] = p.dma("gpsimd", wds[:, nxt % 2, :, :],
                                  io["wdT"][:, nxt * 128:(nxt + 1) * 128]
                                  .rearrange("(kt p) mm -> p kt mm", p=128),
                                  waits=[dtok])

    lnctx = dict(lncol=slncol, onesk1=sonesk1,
                 hb=lambda kt: hact[:, 2 * kt:2 * kt + 2, :].rearrange("p a t -> p (a t)"),
                 hsq=lambda kt: xb[:, kt, :],
                 srows=srows, PA=P2, PB=P3)
    var_save = io.get("var_sb")
    apl = emit_ln(p, lnctx, sx, res_last, var_save=var_save)
    stt = p.dma("sync", io["outT"].rearrange("(kt p) t -> p kt t", p=128), sx[:],
                waits=[apl], sem="ST")
    if io.get("varT") is not None:
        p.dma("sync", io["varT"], io["var_sb"][:], waits=[p.last("DV")], sem="ST")
    p.op("sync", lambda e: e.nop(), waits=[p.last("ST")])


def build_l0ffn(save_var=False):
    nc = bass.Bass("TRN2", target_bir_lowering=False, debug=False)
    io = {}
    io["xT"] = nc.dram_tensor("xT", [D, TOK], F32, kind="ExternalInput").ap()
    io["wgT"] = nc.dram_tensor("wgT", [D, 3072], I32, kind="ExternalInput").ap()
    io["wuT"] = nc.dram_tensor("wuT", [D, 3072], I32, kind="ExternalInput").ap()
    io["wdT"] = nc.dram_tensor("wdT", [3072, D], I32, kind="ExternalInput").ap()
    io["lncol"] = nc.dram_tensor("lncol", [128, 1], BF16, kind="ExternalInput").ap()
    io["onesk1"] = nc.dram_tensor("onesk1", [1, 128], F32, kind="ExternalInput").ap()
    io["outT"] = nc.dram_tensor("outT", [D, TOK], F32, kind="ExternalOutput").ap()
    stack = ExitStack()
    with stack:
        p = Prog(nc, stack)
        if save_var:
            io["varT"] = nc.dram_tensor("varT", [1, TOK], F32, kind="ExternalOutput").ap()
            io["var_sb"] = sb(nc, stack, "var_sb", [1, TOK], F32)
        emit_l0ffn(p, nc, stack, io)
        p.build()
    return nc




F32EPS = float(np.finfo(np.float32).eps)


def emit_l1a(p, nc, stack, io):
    xb = sb(nc, stack, "axb", [128, 8, TOK], BF16)
    xsq = sb(nc, stack, "axsq", [128, 8, TOK], BF16)
    w1b = sb(nc, stack, "aw1b", [128, 8, 256], BF16)
    w2b = sb(nc, stack, "aw2b", [128, 2, 6144], BF16)
    h2 = sb(nc, stack, "ah2", [128, 2, TOK], BF16)
    vp = sb(nc, stack, "avp", [128, 16, 512], BF16)
    qsq = sb(nc, stack, "aqsq", [128, 16, 512], BF16)
    _xsqf = xsq[:, :, :].rearrange("p a t -> p (a t)")
    qp = _xsqf[:, 0:8192].rearrange("p (r t) -> p r t", r=16)
    kp = _xsqf[:, 8192:16384].rearrange("p (r t) -> p r t", r=16)
    sblkd = sb(nc, stack, "ablkd", [128, 16], BF16)
    sexp16 = sb(nc, stack, "aexp16", [16, 128], F32)
    slncol = sb(nc, stack, "alncol", [128, 1], BF16)
    sonesk1 = sb(nc, stack, "aonesk1", [1, 128], F32)
    r2row = sb(nc, stack, "ar2row", [1, TOK], F32)
    srow = sb(nc, stack, "asrow", [1, 512], F32)
    rbc = sb(nc, stack, "arbc", [128, 512], F32)
    ztmp = sb(nc, stack, "aztmp", [128, 512], F32)
    zr = sb(nc, stack, "azr", [128, 512], BF16)
    rqsc = sb(nc, stack, "arqsc", [16, 512], F32)
    rksc = sb(nc, stack, "arksc", [16, 512], F32)
    cosc = sb(nc, stack, "acosc", [128, 4, 512], BF16)
    sinc = sb(nc, stack, "asinc", [128, 4, 512], BF16)
    tp1 = sb(nc, stack, "atp1", [128, 4, 512], BF16)
    tp2 = sb(nc, stack, "atp2", [128, 4, 512], BF16)
    tp4 = sb(nc, stack, "atp4", [128, 4, 512], BF16)
    ksg = qsq
    scarry = sb(nc, stack, "acarry", [128, 16], F32)
    scred = sb(nc, stack, "acred", [128, 16, 1], F32)

    P0 = ps(nc, stack, "aP0", [128, 512])
    P1 = ps(nc, stack, "aP1", [128, 512])
    Pss = ps(nc, stack, "aPss", [16, 512])
    Pss2 = ps(nc, stack, "aPss2", [16, 512])
    Pbq = ps(nc, stack, "aPbq", [128, 512])
    Pbk = ps(nc, stack, "aPbk", [128, 512])

    p.dma("gpsimd", w1b[:], io["w1T"].rearrange("(kt p) m -> p kt m", p=128))
    p.dma("gpsimd", w2b[:], io["w2Tp"].rearrange("(kt p) m -> p kt m", p=128))
    p.dma("sync", sblkd[:], io["blkd"], sem="C")
    p.dma("sync", sexp16[:], io["exp16"], sem="C")
    p.dma("sync", slncol[:], io["lncol"], sem="C")
    cC = p.dma("sync", sonesk1[:], io["onesk1"], sem="C")
    lb = le = cL = cK = cC
    lx = p.dma("gpsimd", xb[:], io["xT"].rearrange("(kt p) t -> p kt t", p=128))
    lw1 = lw2 = lx

    p.op("vector", lambda e: e.memset(scarry[:], 0.0), inc="DV")
    sq_toks = []
    for kt in range(8):
        sq_toks.append(p.op("vector", lambda e, kt=kt: e.tensor_mul(
            out=xsq[:, kt, :], in0=xb[:, kt, :], in1=xb[:, kt, :]),
            waits=[lx] if kt == 0 else (), inc="DV"))
    guard = sq_toks[-1]
    r2_toks = []
    for n2 in range(4):
        s2 = None
        for kt in range(8):
            s2 = p.op("tensor", lambda e, kt=kt, n2=n2: e.matmul(
                out=P0[0:1, :], lhsT=slncol[:],
                rhs=xsq[:, kt, n2 * 512:(n2 + 1) * 512],
                start=(kt == 0), stop=(kt == 7)),
                waits=[guard, cL] if kt == 0 else (), inc="PE" if kt == 7 else None)
        e1 = p.op("vector", lambda e, n2=n2: e.tensor_scalar_add(
            out=srow[:], in0=P0[0:1, :], scalar1=F32EPS), waits=[s2], inc="DV")
        e2 = p.op("scalar", lambda e: e.activation(out=srow[:], in_=srow[:], func=AF.Sqrt),
                  waits=[e1], inc="SC")
        e3 = p.op("vector", lambda e, n2=n2: e.reciprocal(
            out=r2row[:, n2 * 512:(n2 + 1) * 512], in_=srow[:]), waits=[e2], inc="DV")
        guard = e3
        r2_toks.append(e3)

    h2_tok = None
    for m in range(2):
        for n in range(4):
            tok = None
            for kt in range(8):
                tok = p.op("tensor", lambda e, kt=kt, m=m, n=n: e.matmul(
                    out=P1[:], lhsT=w1b[:, kt, m * 128:(m + 1) * 128],
                    rhs=xb[:, kt, n * 512:(n + 1) * 512],
                    start=(kt == 0), stop=(kt == 7)),
                    waits=[lw1, r2_toks[-1], h2_tok] if kt == 0 else (),
                    inc="PE" if kt == 7 else None)
            b1 = p.op("tensor", lambda e, n=n: e.matmul(
                out=P0[:], lhsT=sonesk1[:], rhs=r2row[0:1, n * 512:(n + 1) * 512],
                start=True, stop=True), waits=[cK], inc="PE")
            v1 = p.op("vector", lambda e: e.tensor_copy(out=rbc[:], in_=P0[:]),
                      waits=[b1], inc="DV")
            v2 = p.op("vector", lambda e: e.tensor_mul(out=ztmp[:], in0=rbc[:], in1=P1[:]),
                      waits=[v1, tok], inc="DV")
            v3 = p.op("vector", lambda e: e.tensor_scalar_max(out=zr[:], in0=ztmp[:], scalar1=0.0),
                      waits=[v2], inc="DV")
            h2_tok = p.op("vector", lambda e, m=m, n=n: e.tensor_mul(
                out=h2[:, m, n * 512:(n + 1) * 512], in0=zr[:], in1=zr[:]),
                waits=[v3], inc="DV")

    prev_chunk_out = None
    prev_rope = None
    for c in range(4):
        t0 = c * 512
        lcos = p.dma("sync", cosc[:], io["cosT"][:, :, t0:t0 + 512],
                     waits=[prev_rope] if prev_rope else (), sem="Lx")
        lsin = p.dma("sync", sinc[:], io["sinT"][:, :, t0:t0 + 512], sem="Lx")
        ev_tok = None
        evs = {}
        gj = 0
        for ti, dst in ((0, qp), (1, kp), (2, vp)):
            for r in range(16):
                j = ti * 16 + r
                pp = (P0, P1)[gj % 2]
                tok = None
                w0 = [lw2, h2_tok, prev_chunk_out] if gj == 0 else []
                if gj >= 2:
                    w0 = [evs[gj - 2]]
                for kt in range(2):
                    tok = p.op("tensor", lambda e, kt=kt, j=j, t0=t0, pp=pp: e.matmul(
                        out=pp[:], lhsT=w2b[:, kt, j * 128:(j + 1) * 128],
                        rhs=h2[:, kt, t0:t0 + 512],
                        start=(kt == 0), stop=(kt == 1)),
                        waits=w0 if kt == 0 else (),
                        inc="PE" if kt == 1 else None)
                ev_tok = p.op("vector", lambda e, dst=dst, r=r, pp=pp: e.tensor_copy(
                    out=dst[:, r, :], in_=pp[:]), waits=[tok], inc="DV")
                evs[gj] = ev_tok
                gj += 1
                if ti < 2:
                    p.op("scalar", lambda e, dst=dst, r=r, sq=qsq: e.activation(
                        out=qsq[:, r, :], in_=dst[:, r, :], func=AF.Square),
                        waits=[ev_tok], inc="SC")
                    if ti == 0:
                        sqtok = p.last("SC")
            if ti < 2:
                pss = Pss if ti == 0 else Pss2
                sst = None
                for r in range(16):
                    sst = p.op("tensor", lambda e, r=r, pss=pss: e.matmul(
                        out=pss[:], lhsT=sblkd[:], rhs=qsq[:, r, :],
                        start=(r == 0), stop=(r == 15)),
                        waits=[p.last("SC"), lb] if r == 0 else (),
                        inc="PE" if r == 15 else None)
                dstc = rqsc if ti == 0 else rksc
                u1 = p.op("vector", lambda e, pss=pss, dstc=dstc: e.tensor_scalar_add(
                    out=dstc[:], in0=pss[:], scalar1=F32EPS), waits=[sst], inc="DV")
                u2 = p.op("scalar", lambda e, dstc=dstc: e.activation(
                    out=dstc[:], in_=dstc[:], func=AF.Sqrt), waits=[u1], inc="SC")
                u3 = p.op("vector", lambda e, dstc=dstc: e.reciprocal(
                    out=dstc[:], in_=dstc[:]), waits=[u2], inc="DV")
                bq = p.op("tensor", lambda e, dstc=dstc, pb=(Pbq if ti == 0 else Pbk):
                          e.matmul(out=pb[:], lhsT=sexp16[:], rhs=dstc[:],
                                   start=True, stop=True),
                          waits=[u3, le], inc="PE")
                for r in range(16):
                    ev_tok = p.op("vector", lambda e, dst=dst, r=r,
                                  pb=(Pbq if ti == 0 else Pbk): e.tensor_mul(
                        out=dst[:, r, :], in0=dst[:, r, :], in1=pb[:]),
                        waits=[bq] if r == 0 else (), inc="DV")
        rope_tok = None
        for dst in (qp, kp):
            z1 = dst[:, 0:4, :]
            z2 = dst[:, 8:12, :]
            w0 = [lcos, lsin, p.last("DV")]
            a = p.op("vector", lambda e, z1=z1: e.tensor_mul(out=tp1[:], in0=z1, in1=cosc[:]),
                     waits=w0, inc="DV")
            b = p.op("vector", lambda e, z2=z2: e.tensor_mul(out=tp2[:], in0=z2, in1=sinc[:]),
                     inc="DV")
            d = p.op("vector", lambda e, z1=z1: e.tensor_mul(out=tp4[:], in0=z1, in1=sinc[:]),
                     inc="DV")
            f = p.op("vector", lambda e, z1=z1: e.tensor_add(out=z1, in0=tp1[:], in1=tp2[:]),
                     waits=[b], inc="DV")
            g = p.op("vector", lambda e, z2=z2: e.tensor_mul(out=tp1[:], in0=z2, in1=cosc[:]),
                     waits=[f], inc="DV")
            rope_tok = p.op("vector", lambda e, z2=z2: e.tensor_tensor(
                out=z2, in0=tp1[:], in1=tp4[:], op=ALU.subtract),
                waits=[g], inc="DV")
            prev_rope = rope_tok
        qf = qp[:, :, :].rearrange("p r t -> p (r t)")
        rq1 = p.op("vector", lambda e, qf=qf: e.tensor_scalar_max(
            out=qf, in0=qf, scalar1=0.0), waits=[rope_tok], inc="DV")
        st1 = p.dma("sync", io["rqT"][:, 0:8, t0:t0 + 512], qp[:, 0:8, :],
                    waits=[rq1], sem="ST")
        st2 = p.dma("sync", io["rqT"][:, 8:16, t0:t0 + 512], qp[:, 8:16, :],
                    waits=[rq1], sem="ST")
        kf = kp[:, :, :].rearrange("p r t -> p (r t)")
        ks1 = p.op("scalar", lambda e, kf=kf: e.activation(
            out=ksg[:, :, :].rearrange("p r t -> p (r t)"), in_=kf, func=AF.Sigmoid),
            waits=[rope_tok, prev_chunk_out], inc="SC")
        sv1 = p.op("vector", lambda e: e.tensor_mul(
            out=ksg[:, :, :].rearrange("p r t -> p (r t)"),
            in0=ksg[:, :, :].rearrange("p r t -> p (r t)"),
            in1=vp[:, :, :].rearrange("p r t -> p (r t)")),
            waits=[ks1], inc="DV")
        sc_tok = sv1
        for r in range(16):
            sc_tok = p.op("vector", lambda e, r=r: e.tensor_tensor_scan(
                out=vp[:, r, :], data0=ksg[:, r, :], data1=ksg[:, r, :],
                initial=scarry[:, r:r + 1], op0=ALU.add, op1=ALU.bypass),
                waits=[sc_tok] if r == 0 else (), inc="DV")
        cr1 = p.op("vector", lambda e: e.tensor_reduce(
            out=scred[:, :, 0:1], in_=ksg[:, :, :], axis=AX.X, op=ALU.add),
            waits=[sc_tok], inc="DV")
        cr2 = p.op("vector", lambda e: e.tensor_add(
            out=scarry[:], in0=scarry[:], in1=scred[:, :, 0]),
            waits=[cr1], inc="DV")
        yp1 = p.op("vector", lambda e: e.tensor_mul(
            out=ksg[:, :, :].rearrange("p r t -> p (r t)"),
            in0=qp[:, :, :].rearrange("p r t -> p (r t)"),
            in1=vp[:, :, :].rearrange("p r t -> p (r t)")),
            waits=[cr2], inc="DV")
        st3 = p.dma("sync", io["ypT"][:, 0:8, t0:t0 + 512], ksg[:, 0:8, :],
                    waits=[yp1], sem="ST")
        st4 = p.dma("sync", io["ypT"][:, 8:16, t0:t0 + 512], ksg[:, 8:16, :],
                    waits=[yp1], sem="ST")
        prev_chunk_out = p.last("ST")
    stc = p.dma("sync", io["TcT"], scarry[:], waits=[p.last("DV")], sem="ST")
    p.op("sync", lambda e: e.nop(), waits=[p.last("ST")])


def build_l1a():
    nc = bass.Bass("TRN2", target_bir_lowering=False, debug=False)
    io = {}
    io["xT"] = nc.dram_tensor("xT", [D, TOK], F32, kind="ExternalInput").ap()
    io["w1T"] = nc.dram_tensor("w1T", [D, 256], F32, kind="ExternalInput").ap()
    io["w2Tp"] = nc.dram_tensor("w2Tp", [256, 6144], F32, kind="ExternalInput").ap()
    io["cosT"] = nc.dram_tensor("cosT", [128, 4, TOK], BF16, kind="ExternalInput").ap()
    io["sinT"] = nc.dram_tensor("sinT", [128, 4, TOK], BF16, kind="ExternalInput").ap()
    io["blkd"] = nc.dram_tensor("blkd", [128, 16], BF16, kind="ExternalInput").ap()
    io["exp16"] = nc.dram_tensor("exp16", [16, 128], F32, kind="ExternalInput").ap()
    io["lncol"] = nc.dram_tensor("lncol", [128, 1], BF16, kind="ExternalInput").ap()
    io["onesk1"] = nc.dram_tensor("onesk1", [1, 128], F32, kind="ExternalInput").ap()
    io["ypT"] = nc.dram_tensor("ypT", [128, 16, TOK], BF16, kind="ExternalOutput").ap()
    io["rqT"] = nc.dram_tensor("rqT", [128, 16, TOK], BF16, kind="ExternalOutput").ap()
    io["TcT"] = nc.dram_tensor("TcT", [128, 16], F32, kind="ExternalOutput").ap()
    stack = ExitStack()
    with stack:
        p = Prog(nc, stack)
        emit_l1a(p, nc, stack, io)
        p.build()
    return nc



def emit_l1b(p, nc, stack, io):
    sv = sb(nc, stack, "bsv", [128, 4, 8192], BF16)
    so = sb(nc, stack, "bso", [128, 4, 8192], BF16)
    lx = p.dma("sync", sv[:], io["svF"])
    toks = []
    for r in range(4):
        toks.append(p.op("vector", lambda e, r=r: e.tensor_tensor_scan(
            out=so[:, r, :], data0=sv[:, r, :], data1=sv[:, r, :],
            initial=0.0, op0=ALU.add, op1=ALU.bypass),
            waits=[lx] if r == 0 else (), inc="DV"))
    st = p.dma("sync", io["SF"], so[:], waits=[toks[-1]], sem="ST")
    p.op("sync", lambda e: e.nop(), waits=[p.last("ST")])


def build_l1b():
    nc = bass.Bass("TRN2", target_bir_lowering=False, debug=False)
    io = {}
    io["svF"] = nc.dram_tensor("svF", [128, 4, 8192], BF16, kind="ExternalInput").ap()
    io["SF"] = nc.dram_tensor("SF", [128, 4, 8192], BF16, kind="ExternalOutput").ap()
    stack = ExitStack()
    with stack:
        p = Prog(nc, stack)
        emit_l1b(p, nc, stack, io)
        p.build()
    return nc



def emit_l1c(p, nc, stack, io):
    sx = sb(nc, stack, "cx", [128, 8, TOK], F32)
    sS = sb(nc, stack, "cS", [128, 16, 512], BF16)
    srq = sb(nc, stack, "crq", [128, 16, 512], BF16)
    sag = sb(nc, stack, "cag", [128, 8, 16], F32)
    smk = sb(nc, stack, "cmk", [128, 8], F32)
    cw1s = sb(nc, stack, "cw1s", [128, 16, 256], BF16)
    cw1b = sb(nc, stack, "cw1b", [128, 16, 256], BF16)
    cw2b = sb(nc, stack, "cw2b", [128, 2, 2048], BF16)
    uv1 = sb(nc, stack, "cuv1", [128, 2, TOK], BF16)
    sg = sb(nc, stack, "csg", [128, 512], F32)
    tu = sb(nc, stack, "ctu", [128, 512], F32)
    hqs = sb(nc, stack, "chqs", [128, 8, TOK], BF16)
    _cSf = None
    slncol = sb(nc, stack, "clncol", [128, 1], BF16)
    sonesk1 = sb(nc, stack, "conesk1", [1, 128], F32)
    srows = sb(nc, stack, "csrows", [1, 2048], F32)
    sWr = sb(nc, stack, "cWr", [128, 8, 8], F32)
    sbr = sb(nc, stack, "cbr", [1, 8], F32)
    sbrb = sb(nc, stack, "cbrb", [128, 8], F32)
    sident = sb(nc, stack, "cident", [128, 128], BF16)
    lg = sb(nc, stack, "clg", [128, 8], F32)
    mxv = sb(nc, stack, "cmxv", [128, 1], F32)
    nmx = sb(nc, stack, "cnmx", [128, 1], F32)
    sexr = sb(nc, stack, "csexr", [128, 8], F32)
    sse = sb(nc, stack, "csse", [128, 1], F32)
    ohb = sb(nc, stack, "cohb", [128, 8], BF16)
    sohT = sb(nc, stack, "csohT", [8, TOK], BF16)
    scs = sb(nc, stack, "cscs", [8, TOK], F32)
    scnt = sb(nc, stack, "ccnt", [8, 1], F32)
    sgate = sb(nc, stack, "cgate", [128, 16], F32)

    P0 = ps(nc, stack, "cP0", [128, 512])
    P1 = ps(nc, stack, "cP1", [128, 512])
    P2 = ps(nc, stack, "cP2", [128, 512])
    P3 = ps(nc, stack, "cP3", [128, 512])
    POH = ps(nc, stack, "cPOH", [8, 128], BF16)

    lx = p.dma("sync", sx[:], io["xT"].rearrange("(kt p) t -> p kt t", p=128), sem="Lx")
    p.dma("gpsimd", cw1b[:], io["cw1p"].rearrange("(r p) m -> p r m", p=128))
    p.dma("gpsimd", cw2b[:], io["cw2T"].rearrange("(kt p) m -> p kt m", p=128))
    lwg = p.dma("gpsimd", sWr[:], io["Wr"].rearrange("(kt p) e -> p kt e", p=128))
    lw1 = lw2 = lwr = lwg
    p.dma("sync", sbr[:], io["br"], sem="C")
    p.dma("sync", sident[:], io["ident"], sem="C")
    p.dma("sync", slncol[:], io["lncol"], sem="C")
    cC = p.dma("sync", sonesk1[:], io["onesk1"], sem="C")
    lbr = cI = cL = cK = cC

    lag = p.dma("sync", sag[:], io["TcAll"].rearrange("(r p) f -> p r f", p=128),
                sem="C2")
    lmk = p.dma("sync", smk[:], io["carrymask"], sem="C2")
    cm0 = None
    lvl = []
    for r in range(8):
        lvl.append(p.op("vector", lambda e, r=r: e.tensor_scalar(
            out=sag[:, r, :], in0=sag[:, r, :], scalar1=smk[:, r:r + 1],
            scalar2=None, op0=ALU.mult), waits=[lag, lmk] if r == 0 else (),
            inc="DV"))
    t4 = []
    for r in range(4):
        t4.append(p.op("vector", lambda e, r=r: e.tensor_add(
            out=sag[:, r, :], in0=sag[:, r, :], in1=sag[:, r + 4, :]),
            waits=[lvl[-1]] if r == 0 else (), inc="DV"))
    t2 = []
    for r in range(2):
        t2.append(p.op("vector", lambda e, r=r: e.tensor_add(
            out=sag[:, r, :], in0=sag[:, r, :], in1=sag[:, r + 2, :]),
            waits=[t4[-1]] if r == 0 else (), inc="DV"))
    carry_tok = p.op("vector", lambda e: e.tensor_add(
        out=sag[:, 0, :], in0=sag[:, 0, :], in1=sag[:, 1, :]),
        waits=[t2[-1]], inc="DV")
    cw1s_tok = None
    for r in range(16):
        cw1s_tok = p.op("vector", lambda e, r=r: e.tensor_scalar(
            out=cw1s[:, r, :], in0=cw1b[:, r, :], scalar1=sag[:, 0, r:r + 1],
            scalar2=None, op0=ALU.mult),
            waits=[carry_tok, lw1] if r == 0 else (), inc="DV")

    uv1_tok = None
    for c in range(4):
        t0 = c * 512
        lS1 = p.dma("sync", sS[:, 0:8, :], io["ypT"][:, 0:8, t0:t0 + 512],
                    waits=[uv1_tok], sem="Lx")
        lS2 = p.dma("sync", sS[:, 8:16, :], io["ypT"][:, 8:16, t0:t0 + 512], sem="Lx")
        lq1 = p.dma("sync", srq[:, 0:8, :], io["rqT"][:, 0:8, t0:t0 + 512], sem="Lx")
        lq2 = p.dma("sync", srq[:, 8:16, :], io["rqT"][:, 8:16, t0:t0 + 512], sem="Lx")
        for m2 in range(2):
            tok = None
            for r in range(16):
                p.op("tensor", lambda e, r=r, m2=m2: e.matmul(
                    out=P0[:], lhsT=cw1b[:, r, m2 * 128:(m2 + 1) * 128],
                    rhs=sS[:, r, :], start=(r == 0), stop=False),
                    waits=[lS1, lS2, lq1, lq2, cw1s_tok, uv1_tok] if r == 0 else ())
            for r in range(16):
                tok = p.op("tensor", lambda e, r=r, m2=m2: e.matmul(
                    out=P0[:], lhsT=cw1s[:, r, m2 * 128:(m2 + 1) * 128],
                    rhs=srq[:, r, :], start=False, stop=(r == 15)),
                    inc="PE" if r == 15 else None)
            uv1_tok = p.op("vector", lambda e, m2=m2, t0=t0: e.tensor_copy(
                out=uv1[:, m2, t0:t0 + 512], in_=P0[:]), waits=[tok], inc="DV")

    res_tok = None
    for n in range(4):
        for m in range(8):
            tokg = None
            toku = None
            for kt in range(2):
                tokg = p.op("tensor", lambda e, kt=kt, m=m, n=n: e.matmul(
                    out=P1[:], lhsT=cw2b[:, kt, (m + 8) * 128:(m + 9) * 128],
                    rhs=uv1[:, kt, n * 512:(n + 1) * 512],
                    start=(kt == 0), stop=(kt == 1)),
                    waits=[uv1_tok, lw2, res_tok] if kt == 0 else (),
                    inc="PE" if kt == 1 else None)
            for kt in range(2):
                toku = p.op("tensor", lambda e, kt=kt, m=m, n=n: e.matmul(
                    out=P0[:], lhsT=cw2b[:, kt, m * 128:(m + 1) * 128],
                    rhs=uv1[:, kt, n * 512:(n + 1) * 512],
                    start=(kt == 0), stop=(kt == 1)),
                    inc="PE" if kt == 1 else None)
            a1 = p.op("scalar", lambda e: e.activation(out=sg[:], in_=P1[:], func=AF.Silu),
                      waits=[tokg], inc="SC")
            a2 = p.op("vector", lambda e: e.tensor_mul(out=tu[:], in0=sg[:], in1=P0[:]),
                      waits=[a1, toku], inc="DV")
            res_tok = p.op("vector", lambda e, m=m, n=n: e.scalar_tensor_tensor(
                out=sx[:, m, n * 512:(n + 1) * 512],
                in0=sx[:, m, n * 512:(n + 1) * 512], scalar=2.0, in1=tu[:],
                op0=ALU.mult, op1=ALU.add),
                waits=[a2, lx] if (n == 0 and m == 0) else [a2], inc="DV")

    _cSf = sS[:, :, :].rearrange("p r t -> p (r t)")
    _crqf = srq[:, :, :].rearrange("p r t -> p (r t)")

    def _hb(kt):
        base = _cSf if kt < 4 else _crqf
        return base[:, (kt % 4) * 2048:(kt % 4 + 1) * 2048]
    lnctx = dict(lncol=slncol, onesk1=sonesk1,
                 hb=_hb, hsq=lambda kt: hqs[:, kt, :],
                 srows=srows, PA=P2, PB=P3)
    apl = emit_ln(p, lnctx, sx, res_tok)
    p.dma("sync", io["outT"].rearrange("(kt p) t -> p kt t", p=128), sx[:],
          waits=[apl], sem="ST")

    bb = p.op("tensor", lambda e: e.matmul(
        out=P2[:, 0:8], lhsT=sonesk1[:], rhs=sbr[:], start=True, stop=True),
        waits=[lbr, cK, apl], inc="PE")
    bbv = p.op("vector", lambda e: e.tensor_copy(out=sbrb[:], in_=P2[:, 0:8]),
               waits=[bb], inc="DV")
    oh_toks = []
    for tt in range(16):
        tok = None
        for kt in range(8):
            tok = p.op("tensor", lambda e, kt=kt, tt=tt: e.matmul(
                out=P3[:, 0:8], lhsT=sx[:, kt, tt * 128:(tt + 1) * 128],
                rhs=sWr[:, kt, :], start=(kt == 0), stop=(kt == 7)),
                waits=[lwr, bbv] if (kt == 0 and tt == 0) else (),
                inc="PE" if kt == 7 else None)
        g1 = p.op("vector", lambda e: e.tensor_add(out=lg[:], in0=sbrb[:], in1=P3[:, 0:8]),
                  waits=[tok], inc="DV")
        g2 = p.op("vector", lambda e: e.reduce_max(out=mxv[:], in_=lg[:], axis=AX.X),
                  waits=[g1], inc="DV")
        g3 = p.op("vector", lambda e: e.tensor_scalar(
            out=ohb[:], in0=lg[:], scalar1=mxv[:], scalar2=None, op0=ALU.is_equal),
            waits=[g2], inc="DV")
        g4 = p.op("vector", lambda e: e.tensor_scalar_mul(out=nmx[:], in0=mxv[:], scalar1=-1.0),
                  waits=[g2], inc="DV")
        g5 = p.op("scalar", lambda e: e.activation(
            out=sexr[:], in_=lg[:], func=AF.Exp, bias=nmx[:], accum_out=sse[:]),
            waits=[g4, g3], inc="SC")
        g6 = p.op("vector", lambda e, tt=tt: e.reciprocal(
            out=sgate[:, tt:tt + 1], in_=sse[:]), waits=[g5], inc="DV")
        tr = p.op("tensor", lambda e: e.transpose(out=POH[:], in_=ohb[:], identity=sident[:]),
                  waits=[g3, cI], inc="PE")
        oh_toks.append(p.op("vector", lambda e, tt=tt: e.tensor_copy(
            out=sohT[:, tt * 128:(tt + 1) * 128], in_=POH[:]), waits=[tr], inc="DV"))
        p.op("tensor", lambda e: e.nop(), waits=[p.last("DV")])

    sc1 = p.op("vector", lambda e: e.tensor_tensor_scan(
        out=scs[:], data0=sohT[:], data1=sohT[:], initial=0.0,
        op0=ALU.add, op1=ALU.bypass), waits=[oh_toks[-1]], inc="DV")
    sc2 = p.op("vector", lambda e: e.tensor_copy(out=scnt[:], in_=scs[:, TOK - 1:TOK]),
               waits=[sc1], inc="DV")
    p.dma("sync", io["ohT"], sohT[:], waits=[oh_toks[-1]], sem="ST")
    p.dma("sync", io["csumT"], scs[:], waits=[sc2], sem="ST")
    p.dma("sync", io["counts"], scnt[:], waits=[sc2], sem="ST")
    p.dma("sync", io["gateT"], sgate[:], waits=[p.last("DV")], sem="ST")
    p.op("sync", lambda e: e.nop(), waits=[p.last("ST")])


def build_l1c():
    nc = bass.Bass("TRN2", target_bir_lowering=False, debug=False)
    io = {}
    io["xT"] = nc.dram_tensor("xT", [D, TOK], F32, kind="ExternalInput").ap()
    io["ypT"] = nc.dram_tensor("ypT", [128, 16, TOK], BF16, kind="ExternalInput").ap()
    io["rqT"] = nc.dram_tensor("rqT", [128, 16, TOK], BF16, kind="ExternalInput").ap()
    io["TcAll"] = nc.dram_tensor("TcAll", [8 * 128, 16], F32, kind="ExternalInput").ap()
    io["carrymask"] = nc.dram_tensor("carrymask", [128, 8], F32, kind="ExternalInput").ap()
    io["cw1p"] = nc.dram_tensor("cw1p", [2048, 256], F32, kind="ExternalInput").ap()
    io["cw2T"] = nc.dram_tensor("cw2T", [256, 2048], F32, kind="ExternalInput").ap()
    io["Wr"] = nc.dram_tensor("Wr", [D, 8], F32, kind="ExternalInput").ap()
    io["br"] = nc.dram_tensor("br", [1, 8], F32, kind="ExternalInput").ap()
    io["ident"] = nc.dram_tensor("ident", [128, 128], BF16, kind="ExternalInput").ap()
    io["lncol"] = nc.dram_tensor("lncol", [128, 1], BF16, kind="ExternalInput").ap()
    io["onesk1"] = nc.dram_tensor("onesk1", [1, 128], F32, kind="ExternalInput").ap()
    io["outT"] = nc.dram_tensor("outT", [D, TOK], F32, kind="ExternalOutput").ap()
    io["ohT"] = nc.dram_tensor("ohT", [8, TOK], BF16, kind="ExternalOutput").ap()
    io["csumT"] = nc.dram_tensor("csumT", [8, TOK], F32, kind="ExternalOutput").ap()
    io["counts"] = nc.dram_tensor("counts", [8, 1], F32, kind="ExternalOutput").ap()
    io["gateT"] = nc.dram_tensor("gateT", [128, 16], F32, kind="ExternalOutput").ap()
    stack = ExitStack()
    with stack:
        p = Prog(nc, stack)
        emit_l1c(p, nc, stack, io)
        p.build()
    return nc





def emit_l4(p, nc, stack, io):
    sx = sb(nc, stack, "m4x", [128, 8, TOK], F32)
    xinb = sb(nc, stack, "mxin", [128, 8, TOK], BF16)
    w1b = sb(nc, stack, "mw1", [128, 2, 8, 1024], BF16)
    w2b = sb(nc, stack, "mw2", [128, 2, 8, 1024], BF16)
    hcb = sb(nc, stack, "mhc", [128, 8, 512], BF16)
    soh = sb(nc, stack, "moh", [8, TOK], BF16)
    scs = sb(nc, stack, "mcs", [8, TOK], F32)
    skept = sb(nc, stack, "mkept", [8, TOK], BF16)
    sWT = sb(nc, stack, "mWT", [8, TOK], BF16)
    sgate = sb(nc, stack, "mgate", [128, 16], F32)
    sgateb = sb(nc, stack, "mgateb", [128, 16], BF16)
    sdiag = sb(nc, stack, "mdiag", [128, 512], BF16)
    sbcdump = sb(nc, stack, "mbcdump", [8, 512], F32)
    bc_dump_tok = None
    scnt = sb(nc, stack, "mcnt", [8, 8], F32)
    sprior = sb(nc, stack, "mprior", [8, 8], F32)
    soffs = sb(nc, stack, "moffs", [8, 1], F32)
    sb1 = sb(nc, stack, "mb1", [128, 8, 8], F32)
    sb2 = sb(nc, stack, "mb2", [8, 1024], BF16)
    ssel = sb(nc, stack, "msel", [8, 1024], BF16)
    sident = sb(nc, stack, "mident", [128, 128], BF16)
    sones128 = sb(nc, stack, "mones128", [128, 128], BF16)
    slncol = sb(nc, stack, "mlncol", [128, 1], BF16)
    sonesk1 = sb(nc, stack, "monesk1", [1, 128], F32)
    srows = sb(nc, stack, "msrows", [1, 2048], F32)

    P0 = ps(nc, stack, "mP0", [128, 512])
    P1 = ps(nc, stack, "mP1", [128, 512])
    P2 = ps(nc, stack, "mP2", [128, 512])
    P3 = ps(nc, stack, "mP3", [128, 512])

    lx = p.dma("sync", sx[:], io["xT"].rearrange("(kt p) t -> p kt t", p=128), sem="Lx")
    p.dma("sync", soh[:], io["ohT"], sem="C")
    p.dma("sync", scs[:], io["csumT"], sem="C")
    p.dma("sync", sgate[:], io["gateT"], sem="C")
    if io.get("cntall") is not None:
        p.dma("sync", scnt[:], io["cntall"], sem="C")
    else:
        for r in range(8):
            p.dma("sync", scnt[:, r:r + 1],
                  io["cntAllCol"][r * 8:(r + 1) * 8, 0:1], sem="C")
    p.dma("sync", sprior[:], io["prior"], sem="C")
    p.dma("sync", sb1[:], io["b1T"].rearrange("(m p) e -> p m e", p=128), sem="C")
    lb2 = p.dma("gpsimd", sb2[:], io["b2"])
    p.dma("sync", ssel[:], io["sel8"], sem="C")
    p.dma("sync", sident[:], io["ident"], sem="C")
    p.dma("sync", sones128[:], io["ones128"], sem="C")
    p.dma("sync", slncol[:], io["lncol"], sem="C")
    cC = p.dma("sync", sonesk1[:], io["onesk1"], sem="C")
    loh = lcs = lg = lcn = lpr = lb1 = lsel = lid = l128 = cL = cK = cC

    of1 = p.op("vector", lambda e: e.tensor_mul(
        out=scnt[:], in0=scnt[:], in1=sprior[:]), waits=[lcn, lpr], inc="DV")
    of2 = p.op("vector", lambda e: e.reduce_sum(
        out=soffs[:], in_=scnt[:], axis=AX.X), waits=[of1], inc="DV")
    k1 = p.op("vector", lambda e: e.tensor_scalar(
        out=skept[:], in0=scs[:], scalar1=soffs[:], scalar2=2048.0,
        op0=ALU.add, op1=ALU.is_lt), waits=[of2, lcs], inc="DV")
    k3 = p.op("vector", lambda e: e.tensor_mul(out=skept[:], in0=skept[:], in1=soh[:]),
              waits=[k1, loh], inc="DV")
    gb = p.op("vector", lambda e: e.tensor_copy(out=sgateb[:], in_=sgate[:]),
              waits=[lg], inc="DV")
    wt_tok = None
    xin_tok = None
    dg_tok = None
    for n in range(4):
        dgs = []
        for j in range(4):
            tt = n * 4 + j
            dgs.append(p.op("vector", lambda e, tt=tt: e.tensor_scalar(
                out=sdiag[:, (tt % 4) * 128:(tt % 4 + 1) * 128], in0=sident[:],
                scalar1=sgate[:, tt:tt + 1], scalar2=None, op0=ALU.mult),
                waits=[lid, lg, dg_tok] if (tt % 4 == 0) else (), inc="DV"))
        bc8s = []
        bc128s = []
        for j in range(4):
            bc8s.append(p.op("tensor", lambda e, j=j: e.matmul(
                out=P1[0:8, j * 128:(j + 1) * 128], lhsT=sones128[:, 0:8],
                rhs=sdiag[:, j * 128:(j + 1) * 128], start=True, stop=True),
                waits=[dgs[-1], l128] if j == 0 else (), inc="PE"))
            bc128s.append(p.op("tensor", lambda e, j=j: e.matmul(
                out=P2[:, j * 128:(j + 1) * 128], lhsT=sones128[:],
                rhs=sdiag[:, j * 128:(j + 1) * 128], start=True, stop=True),
                inc="PE"))
        if n == 0 and io.get("dbgBC") is not None:
            bc_dump_tok = p.op("vector", lambda e: e.tensor_copy(
                out=sbcdump[:], in_=P1[0:8, 0:512]), waits=[bc8s[-1]], inc="DV")
        wt_tok = p.op("vector", lambda e, n=n: e.scalar_tensor_tensor(
            out=sWT[:, n * 512:(n + 1) * 512], in0=skept[:, n * 512:(n + 1) * 512],
            scalar=1.0, in1=P1[0:8, 0:512], op0=ALU.mult, op1=ALU.mult),
            waits=[k3, bc8s[-1]], inc="DV")
        for kt in range(8):
            xin_tok = p.op("vector", lambda e, kt=kt, n=n: e.tensor_mul(
                out=xinb[:, kt, n * 512:(n + 1) * 512],
                in0=sx[:, kt, n * 512:(n + 1) * 512], in1=P2[:, 0:512]),
                waits=[bc128s[-1], lx] if kt == 0 else (), inc="DV")
        dg_tok = xin_tok

    wdma = {}
    for ei in range(2):
        p.dma("gpsimd", w1b[:, ei % 2, :, :],
              io["W1"][ei].rearrange("(kt p) m -> p kt m", p=128))
        t_ = p.dma("gpsimd", w2b[:, ei % 2, :, :],
                   io["W2"][ei].rearrange("(kt p) m -> p kt m", p=128))
        wdma[ei] = (t_, t_)
    wdma[0] = wdma[1]
    acc_tok = None
    for ei in range(8):
        bi = ei % 2
        for n in range(4):
            h_toks = []
            for m in range(8):
                pp = (P0, P1)[m % 2]
                tok = None
                for kt in range(8):
                    tok = p.op("tensor", lambda e, kt=kt, m=m, n=n, bi=bi, pp=pp: e.matmul(
                        out=pp[:], lhsT=w1b[:, bi, kt, m * 128:(m + 1) * 128],
                        rhs=xinb[:, kt, n * 512:(n + 1) * 512],
                        start=(kt == 0), stop=(kt == 7)),
                        waits=([wdma[ei][0], wdma[ei][1], xin_tok, acc_tok]
                               + ([h_toks[m - 2]] if m >= 2 else [])) if kt == 0 else (),
                        inc="PE" if kt == 7 else None)
                ht = p.op("vector", lambda e, m=m, ei=ei, pp=pp: e.tensor_scalar(
                    out=hcb[:, m, :], in0=pp[:], scalar1=sb1[:, m, ei:ei + 1],
                    scalar2=0.0, op0=ALU.add, op1=ALU.max),
                    waits=[tok, lb1], inc="DV")
                h_toks.append(ht)
            wb1 = p.op("tensor", lambda e, ei=ei, n=n: e.matmul(
                out=P2[:, 0:512], lhsT=ssel[:, ei * 128:(ei + 1) * 128],
                rhs=sWT[:, n * 512:(n + 1) * 512], start=True, stop=True),
                waits=[lsel, wt_tok, h_toks[-1]], inc="PE")
            hh_tok = None
            for m in range(8):
                hh_tok = p.op("vector", lambda e, m=m: e.tensor_mul(
                    out=hcb[:, m, :], in0=hcb[:, m, :], in1=P2[:, 0:512]),
                    waits=[wb1] if m == 0 else (), inc="DV")
            accs = []
            for m in range(8):
                pp = (P0, P1)[m % 2]
                tok = None
                w2w = [hh_tok] + ([accs[m - 2]] if m >= 2 else [])
                if ei == 0:
                    p.op("tensor", lambda e, m=m, n=n, pp=pp: e.matmul(
                        out=pp[:], lhsT=sb2[:, m * 128:(m + 1) * 128],
                        rhs=sWT[:, n * 512:(n + 1) * 512],
                        start=True, stop=False),
                        waits=w2w + ([lb2] if m < 2 else []))
                for kt in range(8):
                    tok = p.op("tensor", lambda e, kt=kt, m=m, n=n, bi=bi, pp=pp: e.matmul(
                        out=pp[:], lhsT=w2b[:, bi, kt, m * 128:(m + 1) * 128],
                        rhs=hcb[:, kt, :],
                        start=(kt == 0 and ei != 0), stop=(kt == 7)),
                        waits=(w2w if (kt == 0 and ei != 0) else ()),
                        inc="PE" if kt == 7 else None)
                acc_tok = p.op("vector", lambda e, m=m, n=n, pp=pp: e.tensor_add(
                    out=sx[:, m, n * 512:(n + 1) * 512],
                    in0=sx[:, m, n * 512:(n + 1) * 512], in1=pp[:]),
                    waits=[tok], inc="DV")
                accs.append(acc_tok)
        nxt = ei + 2
        if nxt < 8:
            wdma[nxt] = (p.dma("gpsimd", w1b[:, nxt % 2, :, :],
                               io["W1"][nxt].rearrange("(kt p) m -> p kt m", p=128),
                               waits=[p.last("PE")]),
                         p.dma("gpsimd", w2b[:, nxt % 2, :, :],
                               io["W2"][nxt].rearrange("(kt p) m -> p kt m", p=128)))

    if io.get("dbgDiag") is not None:
        p.dma("sync", io["dbgDiag"], sdiag[:], waits=[p.last("DV")], sem="ST")
        p.dma("sync", io["dbgBC"], sbcdump[:], waits=[bc_dump_tok], sem="ST")
    if io.get("dbgWT") is not None:
        p.dma("sync", io["dbgWT"], sWT[:], waits=[wt_tok], sem="ST")
        p.dma("sync", io["dbgKept"], skept[:], waits=[wt_tok], sem="ST")
        p.dma("sync", io["dbgOffs"], soffs[:], waits=[wt_tok], sem="ST")
        p.dma("sync", io["dbgXin"], xinb[:], waits=[xin_tok], sem="ST")
    _xf = xinb[:, :, :].rearrange("p a t -> p (a t)")
    _wf = w1b[:, :, :, :].rearrange("p a b t -> p (a b t)")
    lnctx = dict(lncol=slncol, onesk1=sonesk1,
                 hb=lambda kt: _xf[:, kt * 2048:(kt + 1) * 2048],
                 hsq=lambda kt: _wf[:, kt * 2048:(kt + 1) * 2048],
                 srows=srows, PA=P2, PB=P3)
    apl = emit_ln(p, lnctx, sx, acc_tok)
    p.dma("sync", io["outT"].rearrange("(kt p) t -> p kt t", p=128), sx[:],
          waits=[apl], sem="ST")
    p.op("sync", lambda e: e.nop(), waits=[p.last("ST")])


def build_l4(debug=False):
    nc = bass.Bass("TRN2", target_bir_lowering=False, debug=False)
    io = {}
    io["xT"] = nc.dram_tensor("xT", [D, TOK], F32, kind="ExternalInput").ap()
    io["ohT"] = nc.dram_tensor("ohT", [8, TOK], BF16, kind="ExternalInput").ap()
    io["csumT"] = nc.dram_tensor("csumT", [8, TOK], F32, kind="ExternalInput").ap()
    io["gateT"] = nc.dram_tensor("gateT", [128, 16], F32, kind="ExternalInput").ap()
    io["cntall"] = nc.dram_tensor("cntall", [8, 8], F32, kind="ExternalInput").ap()
    io["prior"] = nc.dram_tensor("prior", [8, 8], F32, kind="ExternalInput").ap()
    io["W1"] = [nc.dram_tensor(f"W1_{e}", [D, D], F32, kind="ExternalInput").ap()
                for e in range(8)]
    io["W2"] = [nc.dram_tensor(f"W2_{e}", [D, D], F32, kind="ExternalInput").ap()
                for e in range(8)]
    io["b1T"] = nc.dram_tensor("b1T", [D, 8], F32, kind="ExternalInput").ap()
    io["b2"] = nc.dram_tensor("b2", [8, D], F32, kind="ExternalInput").ap()
    io["sel8"] = nc.dram_tensor("sel8", [8, D], BF16, kind="ExternalInput").ap()
    io["ident"] = nc.dram_tensor("ident", [128, 128], BF16, kind="ExternalInput").ap()
    io["ones128"] = nc.dram_tensor("ones128", [128, 128], BF16, kind="ExternalInput").ap()
    io["lncol"] = nc.dram_tensor("lncol", [128, 1], BF16, kind="ExternalInput").ap()
    io["onesk1"] = nc.dram_tensor("onesk1", [1, 128], F32, kind="ExternalInput").ap()
    io["outT"] = nc.dram_tensor("outT", [D, TOK], F32, kind="ExternalOutput").ap()
    if debug:
        io["dbgWT"] = nc.dram_tensor("dbgWT", [8, TOK], BF16, kind="ExternalOutput").ap()
        io["dbgKept"] = nc.dram_tensor("dbgKept", [8, TOK], BF16, kind="ExternalOutput").ap()
        io["dbgOffs"] = nc.dram_tensor("dbgOffs", [8, 1], F32, kind="ExternalOutput").ap()
        io["dbgDiag"] = nc.dram_tensor("dbgDiag", [128, 512], BF16, kind="ExternalOutput").ap()
        io["dbgBC"] = nc.dram_tensor("dbgBC", [8, 512], F32, kind="ExternalOutput").ap()
        io["dbgXin"] = nc.dram_tensor("dbgXin", [128, 8, TOK], BF16, kind="ExternalOutput").ap()
    stack = ExitStack()
    with stack:
        p = Prog(nc, stack)
        emit_l4(p, nc, stack, io)
        p.build()
    return nc





def build_fused():
    nc = bass.Bass("TRN2", target_bir_lowering=False, debug=False)
    ext = {}
    def ein(name, shape, dtype):
        ext[name] = nc.dram_tensor(name, shape, dtype, kind="ExternalInput").ap()
        return ext[name]

    xT = ein("xT", [D, TOK], F32)
    for w in ("wq", "wk", "wv", "wo"):
        ein(w, [D, D], F32)
    ein("maskA", [128, 512], BF16)
    ein("maskB", [128, 1024], BF16)
    ein("ident", [128, 128], BF16)
    ein("lncol", [128, 1], BF16)
    ein("onesk1", [1, 128], F32)
    ein("wgT", [D, 3072], I32)
    ein("wuT", [D, 3072], I32)
    ein("wdT", [3072, D], I32)
    ein("w1T", [D, 256], F32)
    ein("w2Tp", [256, 6144], F32)
    ein("cosT", [128, 4, TOK], BF16)
    ein("sinT", [128, 4, TOK], BF16)
    ein("blkd", [128, 16], BF16)
    ein("exp16", [16, 128], F32)
    ein("cw1p", [2048, 256], F32)
    ein("cw2T", [256, 2048], F32)
    ein("Wr", [D, 8], F32)
    ein("br", [1, 8], F32)
    ein("carrymask", [128, 8], F32)
    ein("prior", [8, 8], F32)
    ein("W1own", [D, D], F32)
    ein("W2own", [D, D], F32)
    ein("b1own", [128, 8], F32)
    ein("b2own", [128, 8], F32)
    ein("e8col", [8, 1], F32)
    ein("ones81", [8, 1], F32)
    ein("ones128", [128, 128], BF16)
    outT = nc.dram_tensor("outT", [D, TOK], F32, kind="ExternalOutput").ap()

    x1T = nc.dram_tensor("x1T", [D, TOK], F32, kind="Internal").ap()
    x2T = nc.dram_tensor("x2T", [D, TOK], F32, kind="Internal").ap()
    x3T = nc.dram_tensor("x3T", [D, TOK], F32, kind="Internal").ap()
    ypT = nc.dram_tensor("ypTi", [128, 16, TOK], BF16, kind="Internal").ap()
    rqT = nc.dram_tensor("rqTi", [128, 16, TOK], BF16, kind="Internal").ap()
    TcT = nc.dram_tensor("TcTi", [128, 16], F32, kind="Internal").ap()
    TcAll = nc.dram_tensor("TcAlli", [8 * 128, 16], F32, kind="Internal").ap()
    ohT = nc.dram_tensor("ohTi", [8, TOK], BF16, kind="Internal").ap()
    csumT = nc.dram_tensor("csumTi", [8, TOK], F32, kind="Internal").ap()
    cntT = nc.dram_tensor("cntTi", [8, 1], F32, kind="Internal").ap()
    cntAll = nc.dram_tensor("cntAlli", [64, 1], F32, kind="Internal").ap()
    gateT = nc.dram_tensor("gateTi", [128, 16], F32, kind="Internal").ap()

    RG = [[0, 1, 2, 3, 4, 5, 6, 7]]
    dram4 = l4r_drams(nc)

    with ExitStack() as outer:
        p = Prog(nc, outer)

        with ExitStack() as ph:
            io = dict(xT=xT, wq=ext["wq"], wk=ext["wk"], wv=ext["wv"],
                      wo=ext["wo"], maskA=ext["maskA"], maskB=ext["maskB"],
                      ident=ext["ident"], lncol=ext["lncol"],
                      onesk1=ext["onesk1"], outT=x1T)
            emit_l4r_zero(p, nc, ph, dram4)
            emit_l0attn(p, nc, ph, io)
        p.barrier()

        with ExitStack() as ph:
            io = dict(xT=x1T, wgT=ext["wgT"], wuT=ext["wuT"], wdT=ext["wdT"],
                      lncol=ext["lncol"], onesk1=ext["onesk1"], outT=x2T)
            emit_l0ffn(p, nc, ph, io)
        p.barrier()

        with ExitStack() as ph:
            io = dict(xT=x2T, w1T=ext["w1T"], w2Tp=ext["w2Tp"],
                      cosT=ext["cosT"], sinT=ext["sinT"], blkd=ext["blkd"],
                      exp16=ext["exp16"], lncol=ext["lncol"],
                      onesk1=ext["onesk1"], ypT=ypT, rqT=rqT, TcT=TcT)
            emit_l1a(p, nc, ph, io)
        cc1 = p.op("gpsimd", lambda e: e.collective_compute(
            "AllGather", ALU.bypass, replica_groups=RG,
            ins=[TcT.opt()], outs=[TcAll.opt()]),
            waits=[p.last("ST")], inc="CC")
        p.barrier()

        with ExitStack() as ph:
            io = dict(xT=x2T, ypT=ypT, rqT=rqT, TcAll=TcAll,
                      carrymask=ext["carrymask"], cw1p=ext["cw1p"],
                      cw2T=ext["cw2T"], Wr=ext["Wr"], br=ext["br"],
                      ident=ext["ident"], lncol=ext["lncol"],
                      onesk1=ext["onesk1"], outT=x3T, ohT=ohT,
                      csumT=csumT, counts=cntT, gateT=gateT)
            emit_l1c(p, nc, ph, io)
        cc2 = p.op("gpsimd", lambda e: e.collective_compute(
            "AllGather", ALU.bypass, replica_groups=RG,
            ins=[cntT.opt()], outs=[cntAll.opt()]),
            waits=[p.last("ST")], inc="CC")
        p.barrier()

        with ExitStack() as ph:
            io = dict(xT=x3T, ohT=ohT, csumT=csumT, gateT=gateT,
                      cntAllCol=cntAll, prior=ext["prior"],
                      W1own=ext["W1own"], W2own=ext["W2own"],
                      b1own=ext["b1own"], b2own=ext["b2own"],
                      e8col=ext["e8col"], ones81=ext["ones81"],
                      ident=ext["ident"], ones128=ext["ones128"],
                      lncol=ext["lncol"], onesk1=ext["onesk1"], outT=outT)
            emit_l4r(p, nc, ph, io, dram4, emit_collectives=True)
        p.build()
    return nc




from concourse.bass_core_rust import IndirectOffsetOnAxis

RG8 = [[0, 1, 2, 3, 4, 5, 6, 7]]

def emit_l4r_zero(p, nc, stack, dram):
    szb = sb(nc, stack, "zzb", [128, 2048], BF16)
    z0 = p.op("vector", lambda e: e.memset(szb[:], 0.0), inc="DV")
    xdisp = dram["xdisp"]
    for k in range(64):
        p.dma("sync", xdisp[k * 256:(k + 1) * 256, :].rearrange(
            "(a p) d -> p a d", p=128),
            szb[:].rearrange("p (a d) -> p a d", a=2),
            waits=[z0] if k == 0 else (), sem="Z")


def emit_l4r(p, nc, stack, io, dram, emit_collectives=True):
    U32 = mybir.dt.uint32
    TOKL = 2048

    sx = sb(nc, stack, "r4x", [128, 8, TOKL], F32)
    xinb = sb(nc, stack, "r4xin", [128, 8, TOKL], BF16)
    hcb = sb(nc, stack, "r4hc", [128, 8, TOKL], BF16)
    stg = [sb(nc, stack, f"r4stg{i}", [128, 1024], BF16) for i in range(2)]
    sWT = sb(nc, stack, "r4WT", [8, TOKL], BF16)
    sve = sb(nc, stack, "r4sve", [128, 16, 3], F32)
    sA2 = sb(nc, stack, "r4A2", [128, 16], F32)
    sB2 = sb(nc, stack, "r4B2", [128, 16], F32)
    sD2 = sb(nc, stack, "r4D2", [128, 16], F32)
    sG2 = sb(nc, stack, "r4G2", [128, 16], F32)
    sslotu = sb(nc, stack, "r4slu", [128, 16], U32)
    sgslu = sb(nc, stack, "r4gsu", [128, 16], U32)
    sgate = sb(nc, stack, "r4gate", [128, 16], F32)
    sidentb = sb(nc, stack, "r4id", [128, 128], BF16)
    sones128 = sb(nc, stack, "r4ones", [128, 128], BF16)
    slncol = sb(nc, stack, "r4lnc", [128, 1], BF16)
    sonesk1 = sb(nc, stack, "r4k1", [1, 128], F32)
    scnt = sb(nc, stack, "r4cnt", [8, 8], F32)
    sprior = sb(nc, stack, "r4pr", [8, 8], F32)
    soffs = sb(nc, stack, "r4of", [8, 1], F32)
    se8 = sb(nc, stack, "r4e8", [8, 1], F32)
    s81 = sb(nc, stack, "r4o81", [8, 1], F32)
    sb1o = sb(nc, stack, "r4b1", [128, 8], F32)
    sb2o = sb(nc, stack, "r4b2", [128, 8], F32)

    P0 = ps(nc, stack, "r4P0", [128, 512])
    P1 = ps(nc, stack, "r4P1", [128, 512])
    P2 = ps(nc, stack, "r4P2", [128, 512])
    P3 = ps(nc, stack, "r4P3", [128, 512])
    PS4 = ps(nc, stack, "r4PS", [128, 2048])

    xdisp, xown, eoc, eoAll, svd = (
        dram["xdisp"], dram["xown"], dram["eoc"], dram["eoAll"], dram["svd"])
    z_total = p.last("Z")

    lx = p.dma("sync", sx[:], io["xT"].rearrange("(kt p) t -> p kt t", p=128), sem="Lx")
    p.dma("sync", sgate[:], io["gateT"], sem="C")
    if io.get("cntall") is not None:
        p.dma("sync", scnt[:], io["cntall"], sem="C")
    else:
        for r_ in range(8):
            p.dma("sync", scnt[:, r_:r_ + 1],
                  io["cntAllCol"][r_ * 8:(r_ + 1) * 8, 0:1], sem="C")
    p.dma("sync", sprior[:], io["prior"], sem="C")
    p.dma("sync", se8[:], io["e8col"], sem="C")
    p.dma("sync", s81[:], io["ones81"], sem="C")
    p.dma("sync", sb1o[:], io["b1own"], sem="C")
    p.dma("sync", sb2o[:], io["b2own"], sem="C")
    p.dma("sync", sidentb[:], io["ident"], sem="C")
    p.dma("sync", sones128[:], io["ones128"], sem="C")
    p.dma("sync", slncol[:], io["lncol"], sem="C")
    p.dma("sync", sonesk1[:], io["onesk1"], sem="C")

    w1b = None

    rs_ = ExitStack()
    soh = sb(nc, rs_, "r4oh", [8, TOKL], BF16)
    sohf = sb(nc, rs_, "r4ohf", [8, TOKL], F32)
    scs = sb(nc, rs_, "r4cs", [8, TOKL], F32)
    skept = sb(nc, rs_, "r4kp", [8, TOKL], F32)
    sDr = sb(nc, rs_, "r4Dr", [8, TOKL], F32)
    su32 = sb(nc, rs_, "r4u32", [8, TOKL], U32)
    sv32 = sb(nc, rs_, "r4v32", [8, TOKL], U32)
    srABD = sb(nc, rs_, "r4ABD", [1, 1536], F32)
    sdiag = sb(nc, rs_, "r4diag", [128, 512], BF16)

    p.dma("sync", soh[:], io["ohT"], sem="C")
    cC = p.dma("sync", scs[:], io["csumT"], sem="C")

    of1 = p.op("vector", lambda e: e.tensor_mul(
        out=scnt[:], in0=scnt[:], in1=sprior[:]), waits=[cC], inc="DV")
    of2 = p.op("vector", lambda e: e.reduce_sum(
        out=soffs[:], in_=scnt[:], axis=AX.X), waits=[of1], inc="DV")
    ohc = p.op("vector", lambda e: e.tensor_copy(out=sohf[:], in_=soh[:]),
               waits=[cC], inc="DV")
    k0 = p.op("vector", lambda e: e.tensor_scalar(
        out=scs[:], in0=scs[:], scalar1=soffs[:], scalar2=None, op0=ALU.add),
        waits=[of2], inc="DV")
    k1 = p.op("vector", lambda e: e.tensor_scalar(
        out=skept[:], in0=scs[:], scalar1=2048.0, scalar2=None, op0=ALU.is_lt),
        waits=[k0], inc="DV")
    k2 = p.op("vector", lambda e: e.tensor_mul(
        out=skept[:], in0=skept[:], in1=sohf[:]), waits=[k1, ohc], inc="DV")
    d0 = p.op("vector", lambda e: e.tensor_sub(
        out=sDr[:], in0=sohf[:], in1=skept[:]), waits=[k2], inc="DV")

    wt_tok = None
    xin_toks = {}
    dg_tok = None
    for n in range(4):
        dgs = None
        for j in range(4):
            tt = n * 4 + j
            dgs = p.op("vector", lambda e, tt=tt: e.tensor_scalar(
                out=sdiag[:, (tt % 4) * 128:(tt % 4 + 1) * 128], in0=sidentb[:],
                scalar1=sgate[:, tt:tt + 1], scalar2=None, op0=ALU.mult),
                waits=[cC, dg_tok] if (tt % 4 == 0) else (), inc="DV")
        bc8s = []
        bc128s = []
        for j in range(4):
            bc8s.append(p.op("tensor", lambda e, j=j: e.matmul(
                out=P1[0:8, j * 128:(j + 1) * 128], lhsT=sones128[:, 0:8],
                rhs=sdiag[:, j * 128:(j + 1) * 128], start=True, stop=True),
                waits=[dgs] if j == 0 else (), inc="PE"))
            bc128s.append(p.op("tensor", lambda e, j=j: e.matmul(
                out=P2[:, j * 128:(j + 1) * 128], lhsT=sones128[:],
                rhs=sdiag[:, j * 128:(j + 1) * 128], start=True, stop=True),
                inc="PE"))
        wt_tok = p.op("vector", lambda e, n=n: e.scalar_tensor_tensor(
            out=sWT[:, n * 512:(n + 1) * 512], in0=skept[:, n * 512:(n + 1) * 512],
            scalar=1.0, in1=P1[0:8, 0:512], op0=ALU.mult, op1=ALU.mult),
            waits=[k2, bc8s[-1]], inc="DV")
        tkx = None
        for kt in range(8):
            tkx = p.op("vector", lambda e, kt=kt, n=n: e.tensor_mul(
                out=xinb[:, kt, n * 512:(n + 1) * 512],
                in0=sx[:, kt, n * 512:(n + 1) * 512], in1=P2[:, 0:512]),
                waits=[bc128s[-1], lx] if kt == 0 else (), inc="DV")
        xin_toks[n] = tkx
        dg_tok = tkx

    c0 = p.op("vector", lambda e: e.tensor_scalar(
        out=scs[:], in0=scs[:], scalar1=2047.0, scalar2=None, op0=ALU.min),
        waits=[wt_tok, k2, d0], inc="DV")
    c1 = p.op("vector", lambda e: e.tensor_copy(out=su32[:], in_=scs[:]),
              waits=[c0], inc="DV")
    c2 = p.op("vector", lambda e: e.tensor_scalar(
        out=sv32[:], in0=su32[:], scalar1=255, scalar2=None,
        op0=ALU.bitwise_and), waits=[c1], inc="DV")
    c3 = p.op("vector", lambda e: e.tensor_copy(out=scs[:], in_=sv32[:]),
              waits=[c2], inc="DV")
    c4 = p.op("vector", lambda e: e.tensor_scalar(
        out=sv32[:], in0=su32[:], scalar1=8, scalar2=None,
        op0=ALU.logical_shift_right), waits=[c3], inc="DV")
    c5 = p.op("vector", lambda e: e.tensor_copy(out=skept[:], in_=sv32[:]),
              waits=[c4], inc="DV")
    c6 = p.op("vector", lambda e: e.tensor_scalar(
        out=skept[:], in0=skept[:], scalar1=se8[:], scalar2=None, op0=ALU.add),
        waits=[c5], inc="DV")
    c7 = p.op("vector", lambda e: e.tensor_mul(
        out=skept[:], in0=skept[:], in1=sohf[:]), waits=[c6], inc="DV")
    c8 = p.op("vector", lambda e: e.tensor_mul(
        out=scs[:], in0=scs[:], in1=sohf[:]), waits=[c7], inc="DV")

    chain = c8
    for cch in range(4):
        sl = slice(cch * 512, (cch + 1) * 512)
        p.op("tensor", lambda e, sl=sl: e.matmul(
            out=PS4[0:1, sl], lhsT=s81[:], rhs=skept[:, sl],
            start=True, stop=True), waits=[chain, cC], inc="PE")
        p.op("tensor", lambda e, sl=sl: e.matmul(
            out=PS4[32:33, sl], lhsT=s81[:], rhs=scs[:, sl],
            start=True, stop=True), inc="PE")
        mmE = p.op("tensor", lambda e, sl=sl: e.matmul(
            out=PS4[64:65, sl], lhsT=s81[:], rhs=sDr[:, sl],
            start=True, stop=True), inc="PE")
        vA = srABD.rearrange("a (t three) -> a t three", three=3)
        e0 = p.op("vector", lambda e, sl=sl, vA=vA: e.tensor_copy(
            out=vA[:, 0:512, 0:1],
            in_=PS4[0:1, sl].rearrange("a (t o) -> a t o", o=1)),
            waits=[mmE], inc="DV")
        e1 = p.op("vector", lambda e, sl=sl, vA=vA: e.tensor_copy(
            out=vA[:, 0:512, 1:2],
            in_=PS4[32:33, sl].rearrange("a (t o) -> a t o", o=1)),
            waits=[e0], inc="DV")
        e2 = p.op("vector", lambda e, sl=sl, vA=vA: e.tensor_copy(
            out=vA[:, 0:512, 2:3],
            in_=PS4[64:65, sl].rearrange("a (t o) -> a t o", o=1)),
            waits=[e1], inc="DV")
        chain = p.dma("sync", svd[cch * 512:(cch + 1) * 512, :].rearrange(
            "(a t) three -> a (t three)", a=1), srABD[:],
            waits=[e2], sem="S1")
    sv_done = chain

    lsv = p.dma("sync", sve[:], svd.rearrange("(a p) three -> p a three", p=128),
                waits=[sv_done], sem="S1")
    q0 = p.op("vector", lambda e: e.tensor_copy(
        out=sA2.rearrange("p (t o) -> p t o", o=1), in_=sve[:, :, 0:1]),
        waits=[lsv], inc="DV")
    q1 = p.op("vector", lambda e: e.tensor_copy(
        out=sB2.rearrange("p (t o) -> p t o", o=1), in_=sve[:, :, 1:2]),
        waits=[q0], inc="DV")
    q2 = p.op("vector", lambda e: e.tensor_copy(
        out=sD2.rearrange("p (t o) -> p t o", o=1), in_=sve[:, :, 2:3]),
        waits=[q1], inc="DV")
    q3 = p.op("vector", lambda e: e.tensor_scalar(
        out=sG2[:], in0=sA2[:], scalar1=256.0, scalar2=None, op0=ALU.mult),
        waits=[q2], inc="DV")
    q4 = p.op("vector", lambda e: e.tensor_add(
        out=sG2[:], in0=sG2[:], in1=sB2[:]), waits=[q3], inc="DV")
    cv_g = p.op("vector", lambda e: e.tensor_copy(out=sgslu[:], in_=sG2[:]),
                waits=[q4], inc="DV")
    q6 = p.op("vector", lambda e: e.tensor_scalar(
        out=sD2[:], in0=sD2[:], scalar1=16384.0, scalar2=None, op0=ALU.mult),
        waits=[cv_g], inc="DV")
    q7 = p.op("vector", lambda e: e.tensor_add(
        out=sG2[:], in0=sG2[:], in1=sD2[:]), waits=[q6], inc="DV")
    cv_sl = p.op("vector", lambda e: e.tensor_copy(out=sslotu[:], in_=sG2[:]),
                 waits=[q7], inc="DV")

    rs_.close()

    w1b = sb(nc, stack, "r4w1", [128, 8, 1024], BF16)
    w2b = sb(nc, stack, "r4w2", [128, 8, 1024], BF16)
    srows = sb(nc, stack, "r4srows", [1, 2048], F32)
    lw1 = p.dma("gpsimd", w1b[:], io["W1own"].rearrange("(kt p) m -> p kt m", p=128),
                waits=[cv_sl], sem="L")
    lw2 = p.dma("gpsimd", w2b[:], io["W2own"].rearrange("(kt p) m -> p kt m", p=128),
                sem="L")

    sc_toks = {}
    for t in range(16):
        si = t % 2
        xb_tok = None
        for dt in range(8):
            w = ()
            if dt == 0:
                w = [xin_toks[t // 4]] + ([sc_toks[t - 2]] if t >= 2 else [])
            xb_tok = p.op("sync", lambda e, t=t, dt=dt, si=si: e.dma_start(
                out=stg[si][:, dt * 128:(dt + 1) * 128],
                in_=xinb[:, dt, t * 128:(t + 1) * 128], transpose=True),
                waits=w, inc=("XA" if t % 2 == 0 else "XB"), inc_k=16)
        sc_toks[t] = p.op("gpsimd", lambda e, t=t, si=si: e.indirect_dma_start(
            out=xdisp, out_offset=IndirectOffsetOnAxis(sslotu[:, t:t + 1], 0),
            in_=stg[si][:], in_offset=None,
            bounds_check=16383, oob_is_err=False),
            waits=[xb_tok, cv_sl, z_total],
            inc=("GA" if t % 2 == 0 else "GB"), inc_k=16)
    sc_done = [sc_toks[14], sc_toks[15]]

    if emit_collectives:
        cc_x = p.op("gpsimd", lambda e: e.collective_compute(
            "ReduceScatter", ALU.add, replica_groups=RG8,
            ins=[xdisp.opt()], outs=[xown.opt()]),
            waits=sc_done, inc="CC")
    else:
        cc_x = p.op("gpsimd", lambda e: e.dma_start(out=xown, in_=xdisp[0:2048, :]),
                    waits=sc_done, inc="CC", inc_k=1)

    ld_toks = {}
    xe_toks = {}
    for t in range(16):
        si = t % 2
        w = [cc_x] if t < 2 else [xe_toks[t - 2]]
        ld_toks[t] = p.op("sync", lambda e, t=t, si=si: e.dma_start(
            out=stg[si][:], in_=xown[t * 128:(t + 1) * 128, :]),
            waits=w, inc=("SA" if t % 2 == 0 else "SB"), inc_k=16)
        xe_last = None
        for dt in range(8):
            xe_last = p.op("sync", lambda e, t=t, dt=dt, si=si: e.dma_start(
                out=xinb[:, dt, t * 128:(t + 1) * 128],
                in_=stg[si][:, dt * 128:(dt + 1) * 128], transpose=True),
                waits=[ld_toks[t]] if dt == 0 else (),
                inc=("XA" if t % 2 == 0 else "XB"), inc_k=16)
        xe_toks[t] = xe_last
    xe_done = [xe_toks[14], xe_toks[15]]

    acc = {}
    h_toks = {}
    cC2 = p.last("C")
    for n in range(4):
        for m in range(8):
            pp = (P0, P1)[m % 2]
            w0 = [lw1, lw2, cC2] + xe_done
            if m >= 2:
                w0.append(h_toks[(n, m - 2)])
            elif n > 0:
                w0.append(acc[(n - 1, 6 + m)])
            tok = None
            for kt in range(8):
                tok = p.op("tensor", lambda e, kt=kt, m=m, n=n, pp=pp: e.matmul(
                    out=pp[:], lhsT=w1b[:, kt, m * 128:(m + 1) * 128],
                    rhs=xinb[:, kt, n * 512:(n + 1) * 512],
                    start=(kt == 0), stop=(kt == 7)),
                    waits=w0 if kt == 0 else (), inc="PE" if kt == 7 else None)
            h_toks[(n, m)] = p.op("vector", lambda e, m=m, pp=pp, n=n: e.tensor_scalar(
                out=hcb[:, m, n * 512:(n + 1) * 512], in0=pp[:],
                scalar1=sb1o[:, m:m + 1], scalar2=0.0, op0=ALU.add, op1=ALU.max),
                waits=[tok], inc="DV")
        for m in range(8):
            pp = (P0, P1)[m % 2]
            w0 = [h_toks[(n, 7)]] + ([acc[(n, m - 2)]] if m >= 2 else [])
            tok = None
            for kt in range(8):
                tok = p.op("tensor", lambda e, kt=kt, m=m, n=n, pp=pp: e.matmul(
                    out=pp[:], lhsT=w2b[:, kt, m * 128:(m + 1) * 128],
                    rhs=hcb[:, kt, n * 512:(n + 1) * 512],
                    start=(kt == 0), stop=(kt == 7)),
                    waits=w0 if kt == 0 else (), inc="PE" if kt == 7 else None)
            acc[(n, m)] = p.op("vector", lambda e, m=m, n=n, pp=pp: e.tensor_scalar(
                out=xinb[:, m, n * 512:(n + 1) * 512], in0=pp[:],
                scalar1=sb2o[:, m:m + 1], scalar2=None, op0=ALU.add),
                waits=[tok], inc="DV")
    eo_done = acc[(3, 7)]

    st2 = {}
    for t in range(16):
        si = t % 2
        xb_tok = None
        for dt in range(8):
            w = ()
            if dt == 0:
                w = [eo_done] if t < 2 else [st2[t - 2]]
            xb_tok = p.op("sync", lambda e, t=t, dt=dt, si=si: e.dma_start(
                out=stg[si][:, dt * 128:(dt + 1) * 128],
                in_=xinb[:, dt, t * 128:(t + 1) * 128], transpose=True),
                waits=w, inc=("XA" if t % 2 == 0 else "XB"), inc_k=16)
        st2[t] = p.op("sync", lambda e, t=t, si=si: e.dma_start(
            out=eoc[t * 128:(t + 1) * 128, :], in_=stg[si][:]),
            waits=[xb_tok], inc=("SA" if t % 2 == 0 else "SB"), inc_k=16)
    eo_stored = [st2[14], st2[15]]

    if emit_collectives:
        cc_e = p.op("gpsimd", lambda e: e.collective_compute(
            "AllGather", ALU.bypass, replica_groups=RG8,
            ins=[eoc.opt()], outs=[eoAll.opt()]),
            waits=eo_stored, inc="CC")
    else:
        cc_e = p.op("gpsimd", lambda e: e.dma_start(out=eoAll[0:2048, :], in_=eoc),
                    waits=eo_stored, inc="CC", inc_k=1)

    g2_toks = {}
    eg_toks = {}
    for t in range(16):
        si = t % 2
        gw = [cv_g, cc_e] if t < 2 else [eg_toks[t - 2]]
        g2_toks[t] = p.op("gpsimd", lambda e, t=t, si=si: e.indirect_dma_start(
            out=stg[si][:], out_offset=None,
            in_=eoAll, in_offset=IndirectOffsetOnAxis(sgslu[:, t:t + 1], 0),
            bounds_check=16383, oob_is_err=False),
            waits=gw, inc=("GA" if t % 2 == 0 else "GB"), inc_k=16)
        eg_last = None
        for dt in range(8):
            eg_last = p.op("sync", lambda e, t=t, dt=dt, si=si: e.dma_start(
                out=hcb[:, dt, t * 128:(t + 1) * 128],
                in_=stg[si][:, dt * 128:(dt + 1) * 128], transpose=True),
                waits=[g2_toks[t]] if dt == 0 else (),
                inc=("XA" if t % 2 == 0 else "XB"), inc_k=16)
        eg_toks[t] = eg_last
    eg_done = [eg_toks[14], eg_toks[15]]

    acc2 = None
    for n in range(4):
        wb = p.op("tensor", lambda e, n=n: e.matmul(
            out=P2[:, 0:512], lhsT=sones128[0:8, :],
            rhs=sWT[:, n * 512:(n + 1) * 512], start=True, stop=True),
            waits=[wt_tok] + eg_done + ([acc2] if acc2 else []), inc="PE")
        for kt in range(8):
            tm1 = p.op("vector", lambda e, kt=kt, n=n: e.tensor_mul(
                out=xinb[:, kt, n * 512:(n + 1) * 512],
                in0=hcb[:, kt, n * 512:(n + 1) * 512], in1=P2[:, 0:512]),
                waits=[wb] if kt == 0 else (), inc="DV")
            acc2 = p.op("vector", lambda e, kt=kt, n=n: e.tensor_add(
                out=sx[:, kt, n * 512:(n + 1) * 512],
                in0=sx[:, kt, n * 512:(n + 1) * 512],
                in1=xinb[:, kt, n * 512:(n + 1) * 512]),
                waits=[tm1], inc="DV")

    lnctx = dict(lncol=slncol, onesk1=sonesk1,
                 hb=lambda kt: xinb[:, kt, :], hsq=lambda kt: hcb[:, kt, :],
                 srows=srows, PA=P2, PB=P3)
    apl = emit_ln(p, lnctx, sx, acc2)
    p.dma("sync", io["outT"].rearrange("(kt p) t -> p kt t", p=128), sx[:],
          waits=[apl], sem="ST")
    p.op("sync", lambda e: e.nop(), waits=[p.last("ST")])


def l4r_drams(nc):
    return dict(
        xdisp=nc.dram_tensor("r_xdisp", [16384, 1024], BF16, kind="Internal").ap(),
        xown=nc.dram_tensor("r_xown", [2048, 1024], BF16, kind="Internal").ap(),
        eoc=nc.dram_tensor("r_eoc", [2048, 1024], BF16, kind="Internal").ap(),
        eoAll=nc.dram_tensor("r_eoAll", [16384, 1024], BF16, kind="Internal").ap(),
        svd=nc.dram_tensor("r_svd", [2048, 3], F32, kind="Internal").ap(),
    )


def l4r_io(nc):
    io = {}
    io["xT"] = nc.dram_tensor("xT", [D, 2048], F32, kind="ExternalInput").ap()
    io["ohT"] = nc.dram_tensor("ohT", [8, 2048], BF16, kind="ExternalInput").ap()
    io["csumT"] = nc.dram_tensor("csumT", [8, 2048], F32, kind="ExternalInput").ap()
    io["gateT"] = nc.dram_tensor("gateT", [128, 16], F32, kind="ExternalInput").ap()
    io["cntall"] = nc.dram_tensor("cntall", [8, 8], F32, kind="ExternalInput").ap()
    io["prior"] = nc.dram_tensor("prior", [8, 8], F32, kind="ExternalInput").ap()
    io["e8col"] = nc.dram_tensor("e8col", [8, 1], F32, kind="ExternalInput").ap()
    io["ones81"] = nc.dram_tensor("ones81", [8, 1], F32, kind="ExternalInput").ap()
    io["b1own"] = nc.dram_tensor("b1own", [128, 8], F32, kind="ExternalInput").ap()
    io["b2own"] = nc.dram_tensor("b2own", [128, 8], F32, kind="ExternalInput").ap()
    io["W1own"] = nc.dram_tensor("W1own", [D, D], F32, kind="ExternalInput").ap()
    io["W2own"] = nc.dram_tensor("W2own", [D, D], F32, kind="ExternalInput").ap()
    io["ident"] = nc.dram_tensor("ident", [128, 128], BF16, kind="ExternalInput").ap()
    io["ones128"] = nc.dram_tensor("ones128", [128, 128], BF16, kind="ExternalInput").ap()
    io["lncol"] = nc.dram_tensor("lncol", [128, 1], BF16, kind="ExternalInput").ap()
    io["onesk1"] = nc.dram_tensor("onesk1", [1, 128], F32, kind="ExternalInput").ap()
    return io


def build_l4r():
    nc = bass.Bass("TRN2", target_bir_lowering=False, debug=False)
    io = l4r_io(nc)
    io["outT"] = nc.dram_tensor("outT", [D, 2048], F32, kind="ExternalOutput").ap()
    stack = ExitStack()
    with stack:
        p = Prog(nc, stack)
        dram = l4r_drams(nc)
        emit_l4r_zero(p, nc, stack, dram)
        emit_l4r(p, nc, stack, io, dram, emit_collectives=True)
        p.build()
    return nc

_BUILD_CACHE = {}


def _get(name, fn):
    if name not in _BUILD_CACHE:
        _BUILD_CACHE[name] = fn()
    return _BUILD_CACHE[name]


def _make_perm():
    pp = np.arange(128)
    perm = np.empty(6144, np.int64)
    for tau in range(3):
        for r in range(16):
            perm[tau * 2048 + r * 128 + pp] = (
                tau * 2048 + (pp % 16) * 128 + 8 * r + pp // 16)
    permy = np.empty(2048, np.int64)
    for r in range(16):
        permy[r * 128 + pp] = (pp % 16) * 128 + 8 * r + pp // 16
    return perm, permy


def _rope_tables(t0):
    lin = np.linspace(0.0, 1.0, 32, dtype=np.float32)
    fr = np.power(np.float32(1.0 / 1024.0), lin).astype(np.float32)
    t = (t0 + np.arange(2048)).astype(np.float32)
    p = np.arange(128)
    r = np.arange(4)
    d = 8 * r[None, :] + (p // 16)[:, None]
    theta = (t[None, None, :] * fr[d][:, :, None]).astype(np.float32).astype(np.float64)
    return (np.cos(theta).astype(np.float32).astype(BF),
            np.sin(theta).astype(np.float32).astype(BF))


def kernel_unfused(L0_attn_x, L0_attn_Wq, L0_attn_Wk, L0_attn_Wv, L0_attn_Wo,
           L0_ffn_w_gate, L0_ffn_w_up, L0_ffn_w_down,
           L1_attn_qkv_w1, L1_attn_qkv_w2, L1_attn_cproj_w1, L1_attn_cproj_w2,
           L1_ffn_Wr, L1_ffn_br, L1_ffn_W1, L1_ffn_b1, L1_ffn_W2, L1_ffn_b2):
    x = np.asarray(L0_attn_x)
    consts = common_consts()
    cores = list(range(NCORE))

    def xslices(arrT_list):
        return arrT_list

    def core_bt(c):
        return c // 4, (c % 4) * 2048

    nc = _get("l0attn", build_l0attn)
    in_maps = []
    for c in cores:
        b, t0 = core_bt(c)
        in_maps.append(dict(
            xT=np.ascontiguousarray(x[b, t0:t0 + TOK, :].T),
            wq=L0_attn_Wq, wk=L0_attn_Wk, wv=L0_attn_Wv, wo=L0_attn_Wo,
            **consts))
    res = run_bass_kernel_spmd(nc, in_maps, core_ids=cores)
    x1T = [np.asarray(res.results[c]["outT"]) for c in cores]

    nc = _get("l0ffn", build_l0ffn)
    wgT = np.ascontiguousarray(L0_ffn_w_gate.T)
    wuT = np.ascontiguousarray(L0_ffn_w_up.T)
    wdT = np.ascontiguousarray(L0_ffn_w_down.T)
    in_maps = [dict(xT=x1T[c], wgT=wgT, wuT=wuT, wdT=wdT,
                    lncol=consts["lncol"], onesk1=consts["onesk1"])
               for c in cores]
    res = run_bass_kernel_spmd(nc, in_maps, core_ids=cores)
    x2T = [np.asarray(res.results[c]["outT"]) for c in cores]

    nc = _get("l1a", build_l1a)
    perm, permy = _make_perm()
    w1T = np.ascontiguousarray(L1_attn_qkv_w1.T)
    w2Tp = np.ascontiguousarray(L1_attn_qkv_w2[perm].T)
    blkd = np.zeros((128, 16), dtype=BF)
    pp = np.arange(128)
    blkd[pp, pp % 16] = BF(1.0 / 128.0)
    exp16 = np.zeros((16, 128), np.float32)
    exp16[pp % 16, pp] = 1.0
    in_maps = []
    for c in cores:
        b, t0 = core_bt(c)
        cosT, sinT = _rope_tables(t0)
        in_maps.append(dict(
            xT=x2T[c], w1T=w1T, w2Tp=w2Tp, cosT=cosT, sinT=sinT,
            blkd=blkd, exp16=exp16,
            lncol=consts["lncol"], onesk1=consts["onesk1"]))
    res = run_bass_kernel_spmd(nc, in_maps, core_ids=cores)
    ypT = [np.asarray(res.results[c]["ypT"]) for c in cores]
    rqT = [np.asarray(res.results[c]["rqT"]) for c in cores]
    TcT = [np.asarray(res.results[c]["TcT"]) for c in cores]

    nc = _get("l1c", build_l1c)
    cw1p = np.ascontiguousarray(L1_attn_cproj_w1.T[permy])
    cw2T = np.ascontiguousarray(L1_attn_cproj_w2.T)
    TcAll = np.ascontiguousarray(np.concatenate(TcT, axis=0))
    in_maps = []
    for c in cores:
        b, t0 = core_bt(c)
        cmask = np.zeros((128, 8), np.float32)
        cmask[:, 4 * b:c] = 1.0
        in_maps.append(dict(
            xT=x2T[c], ypT=ypT[c], rqT=rqT[c], TcAll=TcAll, carrymask=cmask,
            cw1p=cw1p, cw2T=cw2T, Wr=L1_ffn_Wr,
            br=np.ascontiguousarray(L1_ffn_br.reshape(1, 8)),
            ident=consts["ident"], lncol=consts["lncol"],
            onesk1=consts["onesk1"]))
    res = run_bass_kernel_spmd(nc, in_maps, core_ids=cores)
    x3T = [np.asarray(res.results[c]["outT"]) for c in cores]
    ohT = [np.asarray(res.results[c]["ohT"]) for c in cores]
    csumT = [np.asarray(res.results[c]["csumT"]) for c in cores]
    cnt = [np.asarray(res.results[c]["counts"]) for c in cores]
    gateT = [np.asarray(res.results[c]["gateT"]) for c in cores]

    nc = _get("l4", build_l4)
    cnt_et = np.ascontiguousarray(np.concatenate(cnt, axis=1))
    sel8 = np.zeros((8, 1024), dtype=BF)
    for e in range(8):
        sel8[e, e * 128:(e + 1) * 128] = BF(1.0)
    b1T = np.ascontiguousarray(L1_ffn_b1.T)
    in_maps = []
    for c in cores:
        prior = np.zeros((8, 8), np.float32)
        prior[:, :c] = 1.0
        m = dict(xT=x3T[c], ohT=ohT[c], csumT=csumT[c], gateT=gateT[c],
                 cntall=cnt_et, prior=prior, b1T=b1T, b2=L1_ffn_b2,
                 sel8=sel8, ident=consts["ident"],
                 ones128=np.ones((128, 128), dtype=BF),
                 lncol=consts["lncol"], onesk1=consts["onesk1"])
        for e in range(8):
            m[f"W1_{e}"] = np.ascontiguousarray(L1_ffn_W1[e])
            m[f"W2_{e}"] = np.ascontiguousarray(L1_ffn_W2[e])
        in_maps.append(m)
    res = run_bass_kernel_spmd(nc, in_maps, core_ids=cores)

    out = np.empty((2, 8192, D), np.float32)
    for c in cores:
        b, t0 = core_bt(c)
        out[b, t0:t0 + TOK, :] = np.asarray(res.results[c]["outT"]).T
    return out



def kernel(L0_attn_x, L0_attn_Wq, L0_attn_Wk, L0_attn_Wv, L0_attn_Wo,
           L0_ffn_w_gate, L0_ffn_w_up, L0_ffn_w_down,
           L1_attn_qkv_w1, L1_attn_qkv_w2, L1_attn_cproj_w1, L1_attn_cproj_w2,
           L1_ffn_Wr, L1_ffn_br, L1_ffn_W1, L1_ffn_b1, L1_ffn_W2, L1_ffn_b2):
    x = np.asarray(L0_attn_x)
    consts = common_consts()
    cores = list(range(NCORE))
    perm, permy = _make_perm()
    shared = dict(
        wq=L0_attn_Wq, wk=L0_attn_Wk, wv=L0_attn_Wv, wo=L0_attn_Wo,
        wgT=np.ascontiguousarray(L0_ffn_w_gate.T),
        wuT=np.ascontiguousarray(L0_ffn_w_up.T),
        wdT=np.ascontiguousarray(L0_ffn_w_down.T),
        w1T=np.ascontiguousarray(L1_attn_qkv_w1.T),
        w2Tp=np.ascontiguousarray(L1_attn_qkv_w2[perm].T),
        cw1p=np.ascontiguousarray(L1_attn_cproj_w1.T[permy]),
        cw2T=np.ascontiguousarray(L1_attn_cproj_w2.T),
        Wr=L1_ffn_Wr,
        br=np.ascontiguousarray(L1_ffn_br.reshape(1, 8)),
        e8col=(np.arange(8, dtype=np.float32) * 8.0).reshape(8, 1),
        ones81=np.ones((8, 1), np.float32),
        ones128=np.ones((128, 128), dtype=BF),
        **consts)
    pp = np.arange(128)
    blkd = np.zeros((128, 16), dtype=BF)
    blkd[pp, pp % 16] = BF(1.0 / 128.0)
    exp16 = np.zeros((16, 128), np.float32)
    exp16[pp % 16, pp] = 1.0
    shared["blkd"] = blkd
    shared["exp16"] = exp16
    W1a = np.asarray(L1_ffn_W1)
    W2a = np.asarray(L1_ffn_W2)
    b1a = np.asarray(L1_ffn_b1)
    b2a = np.asarray(L1_ffn_b2)

    nc = _get("fused", build_fused)
    in_maps = []
    for c in cores:
        b, t0 = c // 4, (c % 4) * 2048
        cosT, sinT = _rope_tables(t0)
        cmask = np.zeros((128, 8), np.float32)
        cmask[:, 4 * b:c] = 1.0
        prior = np.zeros((8, 8), np.float32)
        prior[:, :c] = 1.0
        in_maps.append(dict(
            xT=np.ascontiguousarray(x[b, t0:t0 + TOK, :].T),
            cosT=cosT, sinT=sinT, carrymask=cmask, prior=prior,
            W1own=np.ascontiguousarray(W1a[c]),
            W2own=np.ascontiguousarray(W2a[c]),
            b1own=np.ascontiguousarray(b1a[c].reshape(8, 128).T),
            b2own=np.ascontiguousarray(b2a[c].reshape(8, 128).T),
            **shared))
    res = run_bass_kernel_spmd(nc, in_maps, core_ids=cores)
    out = np.empty((2, 8192, D), np.float32)
    for c in cores:
        b, t0 = c // 4, (c % 4) * 2048
        out[b, t0:t0 + TOK, :] = np.asarray(res.results[c]["outT"]).T
    return out


# version 12
# speedup vs baseline: 1.0853x; 1.0853x over previous
import numpy as np
import ml_dtypes
from contextlib import ExitStack

import concourse.bass as bass
import concourse.mybir as mybir
from concourse.bass_utils import run_bass_kernel_spmd

F32 = mybir.dt.float32
BF16 = mybir.dt.bfloat16
I32 = mybir.dt.int32
AF = mybir.ActivationFunctionType
ALU = mybir.AluOpType
AX = mybir.AxisListType
BF = ml_dtypes.bfloat16

NCORE = 8
TOK = 2048
D = 1024
NEG = -1.0e30

ENGINES = ("sync", "tensor", "vector", "scalar", "gpsimd")


class Prog:

    def __init__(self, nc, stack):
        self.nc = nc
        self.stack = stack
        self.streams = {e: [] for e in ENGINES}
        self._sems = {}
        self.counts = {}

    def _sem(self, name):
        if name not in self._sems:
            self._sems[name] = self.stack.enter_context(self.nc.semaphore(name))
            self.counts[name] = 0
        return self._sems[name]

    def op(self, engine, fn, waits=(), inc=None, inc_k=1):
        tok = None
        if inc is not None:
            self._sem(inc)
            self.counts[inc] += inc_k
            tok = (inc, self.counts[inc])
        ws = []
        for w in waits:
            if w is not None:
                self._sem(w[0])
                ws.append(w)
        self.streams[engine].append((fn, ws, (inc, inc_k) if inc else None))
        return tok

    def dma(self, queue, out, in_, waits=(), sem=None):
        sem = sem or ("Lg" if queue == "gpsimd" else "L")
        return self.op(queue, lambda e, o=out, i=in_: e.dma_start(out=o, in_=i),
                       waits=waits, inc=sem, inc_k=16)

    def last(self, sem):
        return (sem, self.counts.get(sem, 0))

    def barrier(self):
        toks = [(s, v) for s, v in self.counts.items() if v > 0]
        for eng in ENGINES:
            self.op(eng, lambda e: e.nop(), waits=toks)

    def build(self):
        block = self.stack.enter_context(self.nc.Block())
        for eng_name in ENGINES:
            stream = self.streams[eng_name]
            if not stream:
                continue

            def body(eng, stream=stream):
                for fn, waits, inc in stream:
                    for (s, v) in waits:
                        if v > 0:
                            eng.wait_ge(self._sems[s], v)
                    inst = fn(eng)
                    if inc is not None:
                        inst.then_inc(self._sems[inc[0]], inc[1])
            getattr(block, eng_name)(body)


def sb(nc, stack, name, shape, dtype):
    return stack.enter_context(nc.sbuf_tensor(name, shape, dtype))


def ps(nc, stack, name, shape, dtype=F32):
    return stack.enter_context(nc.psum_tensor(name, shape, dtype))



def emit_ln(p, c, hsb, t_ready, out_bf=None, var_save=None):
    TC = hsb.shape[2] if hasattr(hsb, "shape") else 1024
    hb, hsq, srows, PA, PB = c["hb"], c["hsq"], c["srows"], c["PA"], c["PB"]
    lncol, onesk1 = c["lncol"], c["onesk1"]
    toks = []
    for kt in range(8):
        t1 = p.op("scalar", lambda e, kt=kt: e.copy(out=hb(kt), in_=hsb[:, kt, :]),
                  waits=[t_ready] if kt == 0 else (), inc="SC")
        t2 = p.op("vector", lambda e, kt=kt: e.tensor_mul(
            out=hsq(kt), in0=hb(kt), in1=hb(kt)),
            waits=[t1], inc="DV")
        toks.append(t2)
    guard = toks[-1]
    apl_last = None
    for n2 in range(TC // 512):
        s1 = s2 = None
        for kt in range(8):
            s1 = p.op("tensor", lambda e, kt=kt, n2=n2: e.matmul(
                out=PA[0:1, 0:512], lhsT=lncol[:],
                rhs=hb(kt)[:, n2 * 512:(n2 + 1) * 512],
                start=(kt == 0), stop=(kt == 7)),
                waits=[guard] if kt == 0 else (), inc="PE" if kt == 7 else None)
        for kt in range(8):
            s2 = p.op("tensor", lambda e, kt=kt, n2=n2: e.matmul(
                out=PB[0:1, 0:512], lhsT=lncol[:],
                rhs=hsq(kt)[:, n2 * 512:(n2 + 1) * 512],
                start=(kt == 0), stop=(kt == 7)),
                inc="PE" if kt == 7 else None)
        rm = p.op("vector", lambda e: e.tensor_copy(out=srows[:, 0:512], in_=PA[0:1, 0:512]),
                  waits=[s1], inc="DV")
        rv1 = p.op("vector", lambda e: e.tensor_mul(
            out=srows[:, 1536:2048], in0=srows[:, 0:512], in1=srows[:, 0:512]),
            waits=[rm], inc="DV")
        rv2 = p.op("vector", lambda e: e.scalar_tensor_tensor(
            out=srows[:, 1536:2048], in0=srows[:, 1536:2048], scalar=-1.0,
            in1=PB[0:1, 0:512], op0=ALU.mult, op1=ALU.add),
            waits=[s2, rv1], inc="DV")
        if var_save is not None:
            p.op("vector", lambda e, n2=n2: e.tensor_copy(
                out=var_save[:, n2 * 512:(n2 + 1) * 512], in_=srows[:, 1536:2048]),
                waits=[rv2], inc="DV")
        rv3 = p.op("vector", lambda e: e.tensor_scalar_add(
            out=srows[:, 1536:2048], in0=srows[:, 1536:2048], scalar1=1e-5),
            waits=[rv2], inc="DV")
        rs = p.op("scalar", lambda e: e.activation(
            out=srows[:, 512:1024], in_=srows[:, 1536:2048],
            func=AF.Sqrt),
            waits=[rv3], inc="SC")
        rr = p.op("vector", lambda e: e.reciprocal(
            out=srows[:, 512:1024], in_=srows[:, 512:1024]),
            waits=[rs], inc="DV")
        rmr = p.op("vector", lambda e: e.tensor_mul(
            out=srows[:, 1024:1536], in0=srows[:, 0:512], in1=srows[:, 512:1024]),
            waits=[rr], inc="DV")
        bc1 = p.op("tensor", lambda e: e.matmul(
            out=PA[:, 0:512], lhsT=onesk1[:], rhs=srows[0:1, 512:1024],
            start=True, stop=True), waits=[rmr], inc="PE")
        bc2 = p.op("tensor", lambda e: e.matmul(
            out=PB[:, 0:512], lhsT=onesk1[:], rhs=srows[0:1, 1024:1536],
            start=True, stop=True), inc="PE")
        for kt in range(8):
            a1 = p.op("vector", lambda e, kt=kt, n2=n2: e.tensor_mul(
                out=hsb[:, kt, n2 * 512:(n2 + 1) * 512],
                in0=hsb[:, kt, n2 * 512:(n2 + 1) * 512], in1=PA[:, 0:512]),
                waits=[bc1] if kt == 0 else (), inc="DV")
            a2 = p.op("vector", lambda e, kt=kt, n2=n2: e.tensor_sub(
                out=hsb[:, kt, n2 * 512:(n2 + 1) * 512],
                in0=hsb[:, kt, n2 * 512:(n2 + 1) * 512], in1=PB[:, 0:512]),
                waits=[bc2] if kt == 0 else (), inc="DV")
            if out_bf is not None:
                p.op("scalar", lambda e, kt=kt, n2=n2: e.copy(
                    out=out_bf[:, kt, n2 * 512:(n2 + 1) * 512],
                    in_=hsb[:, kt, n2 * 512:(n2 + 1) * 512]),
                    waits=[a2], inc="SC")
            apl_last = a2
        guard = apl_last
    return apl_last



def emit_l0attn(p, nc, stack, io):
    TC = 1024
    NCH = TOK // TC
    sx = sb(nc, stack, "sx", [128, 8, TC], F32)
    sxb = sb(nc, stack, "sxb", [128, 8, TC], BF16)
    swq = sb(nc, stack, "swq", [128, 8, 1024], BF16)
    swk = sb(nc, stack, "swk", [128, 8, 1024], BF16)
    swv = sb(nc, stack, "swv", [128, 8, 1024], BF16)
    swo = sb(nc, stack, "swo", [128, 8, 1024], BF16)
    qb = sb(nc, stack, "qb", [128, 8, TC], BF16)
    kb = sb(nc, stack, "kb", [128, 8, TC], BF16)
    vt = sb(nc, stack, "vt", [128, 8, TC], BF16)
    ob = sb(nc, stack, "ob", [128, 8, TC], BF16)
    smA = sb(nc, stack, "smA", [128, 512], BF16)
    smB = sb(nc, stack, "smB", [128, 1024], BF16)
    sident = sb(nc, stack, "sident", [128, 128], BF16)
    slncol = sb(nc, stack, "slncol", [128, 1], BF16)
    sonesk1 = sb(nc, stack, "sonesk1", [1, 128], F32)
    pf = sb(nc, stack, "pf", [128, 1024], F32)
    sexpB = sb(nc, stack, "sexpB", [128, 1024], F32)
    pbA = sb(nc, stack, "pbA", [128, 512], BF16)
    pbB = sb(nc, stack, "pbB", [128, 1024], BF16)
    scol = [sb(nc, stack, "scol0", [128, 8, 1], F32)]
    pT = sb(nc, stack, "pT", [128, 768], BF16)

    srows = sb(nc, stack, "srows", [1, 2048], F32)

    P0 = ps(nc, stack, "P0", [128, 512])
    P1 = ps(nc, stack, "P1", [128, 512])
    P2 = ps(nc, stack, "P2", [128, 512])
    P3 = ps(nc, stack, "P3", [128, 512])
    P4 = ps(nc, stack, "P4", [128, 512])
    PT = ps(nc, stack, "PT", [128, 1024], BF16)
    PT2 = ps(nc, stack, "PT2", [128, 1024], BF16)

    p.dma("gpsimd", swq[:], io["wq"].rearrange("(kt p) m -> p kt m", p=128))
    p.dma("gpsimd", swk[:], io["wk"].rearrange("(kt p) m -> p kt m", p=128))
    p.dma("gpsimd", swv[:], io["wv"].rearrange("(kt p) m -> p kt m", p=128))
    wall = p.dma("gpsimd", swo[:], io["wo"].rearrange("(kt p) m -> p kt m", p=128))
    p.dma("sync", smA[:], io["maskA"], sem="C")
    p.dma("sync", smB[:], io["maskB"], sem="C")
    p.dma("sync", sident[:], io["ident"], sem="C")
    p.dma("sync", slncol[:], io["lncol"], sem="C")
    cC = p.dma("sync", sonesk1[:], io["onesk1"], sem="C")
    cA = cB = cI = cL = cK = cC

    lnctx = dict(lncol=slncol, onesk1=sonesk1,
                 hb=lambda kt: qb[:, kt, :], hsq=lambda kt: kb[:, kt, :],
                 srows=srows, PA=P2, PB=P3)

    prev_store = None
    prev_apply = None
    for tc in range(NCH):
        t0 = tc * TC
        lx = p.dma("sync", sx[:],
                   io["xT"].rearrange("(kt p) t -> p kt t", p=128)[:, :, t0:t0 + TC],
                   waits=[prev_store] if prev_store else (), sem="Lx")
        cast_done = None
        for kt in range(8):
            cast_done = p.op("vector", lambda e, kt=kt: e.tensor_copy(
                out=sxb[:, kt, :], in_=sx[:, kt, :]),
                waits=[lx, prev_apply] if kt == 0 else (), inc="DV")

        groups = ([("q", m, n2) for m in range(8) for n2 in range(TC // 512)]
                  + [("k", m, n2) for m in range(8) for n2 in range(TC // 512)]
                  + [("v", tt, n2) for tt in range(8) for n2 in range(2)])
        prev_evac = {}
        for gi, (kind, m, n2) in enumerate(groups):
            pp = (P0, P1)[gi % 2]
            w0 = [cast_done, wall]
            if gi - 2 >= 0:
                w0 = [prev_evac[gi - 2]] if gi > 2 else w0 + [prev_evac[gi - 2]]
            tok = None
            for kt in range(8):
                if kind == "q":
                    fn = (lambda e, kt=kt, m=m, n2=n2, pp=pp: e.matmul(
                        out=pp[:], lhsT=swq[:, kt, m * 128:(m + 1) * 128],
                        rhs=sxb[:, kt, n2 * 512:(n2 + 1) * 512],
                        start=(kt == 0), stop=(kt == 7)))
                elif kind == "k":
                    fn = (lambda e, kt=kt, m=m, n2=n2, pp=pp: e.matmul(
                        out=pp[:], lhsT=swk[:, kt, m * 128:(m + 1) * 128],
                        rhs=sxb[:, kt, n2 * 512:(n2 + 1) * 512],
                        start=(kt == 0), stop=(kt == 7)))
                else:
                    fn = (lambda e, kt=kt, tt=m, n2=n2, pp=pp: e.matmul(
                        out=pp[:], lhsT=sxb[:, kt, tt * 128:(tt + 1) * 128],
                        rhs=swv[:, kt, n2 * 512:(n2 + 1) * 512],
                        start=(kt == 0), stop=(kt == 7)))
                tok = p.op("tensor", fn, waits=w0 if kt == 0 else (),
                           inc="PE" if kt == 7 else None)
            dst = {"q": qb, "k": kb, "v": vt}[kind]
            ev = p.op("vector", lambda e, dst=dst, m=m, n2=n2, pp=pp: e.tensor_copy(
                out=dst[:, m, n2 * 512:(n2 + 1) * 512], in_=pp[:]),
                waits=[tok], inc="DV")
            prev_evac[gi] = ev
        qkv_done = prev_evac[len(groups) - 1]

        stSC = {}
        stN = {}
        stB2 = {}

        def quad(hh):
            return dict(
                hh=hh, base=(hh % 2) * 64, mt=hh // 2,
                PA=P0, PBs=(P2, P3),
                seA=pf[:, 0:512], seB=sexpB[:, 0:1024],
                pbAs=pbA[:, 0:512], pbBs=pbB[:, 0:1024], sc=scol[0])

        def emit_scores(hh):
            r = quad(hh)
            base, mt, PA, PBs = r["base"], r["mt"], r["PA"], r["PBs"]
            w0 = [qkv_done, cC] if hh == 0 else []
            wprev = [stSC[hh - 1]] if hh >= 1 else []
            first = True
            for c in range(4):
                q_ap = qb[base:base + 64, mt, c * 256:(c + 1) * 256]
                k_ap = kb[base:base + 64, mt, c * 256:(c + 1) * 256]
                p.op("tensor", lambda e, q_ap=q_ap, k_ap=k_ap, PA=PA, c=c: e.matmul(
                    out=PA[:, c * 128:(c + 1) * 128], lhsT=q_ap[:, 0:128],
                    rhs=k_ap[:, 0:128], start=(c == 0), stop=(c == 3)),
                    waits=(w0 + wprev) if first else ())
                first = False
            tok = None
            for c in range(4):
                q_ap = qb[base:base + 64, mt, c * 256:(c + 1) * 256]
                k_ap = kb[base:base + 64, mt, c * 256:(c + 1) * 256]
                pb_ = PBs[c // 2]
                cc = c % 2
                tok = p.op("tensor", lambda e, q_ap=q_ap, k_ap=k_ap, pb_=pb_, cc=cc:
                           e.matmul(out=pb_[:, cc * 256:(cc + 1) * 256],
                                    lhsT=q_ap[:, 128:256], rhs=k_ap[:, 0:256],
                                    start=(cc == 0), stop=(cc == 1)),
                           inc="PE" if c == 3 else None)
            wn = [stN[hh - 1][0], stN[hh - 1][1]] if hh >= 1 else []
            x1 = p.op("scalar", lambda e, r=r: e.activation(
                out=r["seA"], in_=r["PA"][:, 0:512], func=AF.Exp, scale=0.125),
                waits=[tok] + wn, inc="SC")
            p.op("scalar", lambda e, r=r: e.activation(
                out=r["seB"][:, 0:512], in_=r["PBs"][0][:, 0:512], func=AF.Exp,
                scale=0.125), inc="SC")
            x3 = p.op("scalar", lambda e, r=r: e.activation(
                out=r["seB"][:, 512:1024], in_=r["PBs"][1][:, 0:512], func=AF.Exp,
                scale=0.125), inc="SC")
            stSC[hh] = x3

        def emit_mid(hh):
            r = quad(hh)
            w0 = [stSC[hh]]
            wre = [stB2[hh - 1][0]] if hh >= 1 else []
            m1 = p.op("gpsimd", lambda e, r=r: e.tensor_mul(
                out=r["seA"], in0=r["seA"], in1=smA[:]), waits=w0, inc="GP")
            m2 = p.op("gpsimd", lambda e, r=r: e.tensor_mul(
                out=r["seB"], in0=r["seB"], in1=smB[:]), inc="GP")
            r1 = p.op("vector", lambda e, r=r: e.tensor_reduce(
                out=r["sc"][:, 0:4, :],
                in_=r["seA"].rearrange("p (c k) -> p c k", c=4),
                axis=AX.X, op=ALU.add), waits=[m1], inc="DV")
            r2 = p.op("vector", lambda e, r=r: e.tensor_reduce(
                out=r["sc"][:, 4:8, :],
                in_=r["seB"].rearrange("p (c k) -> p c k", c=4),
                axis=AX.X, op=ALU.add), waits=[m2], inc="DV")
            t = p.op("vector", lambda e, r=r: e.reciprocal(
                out=r["sc"][:, :, 0], in_=r["sc"][:, :, 0]), waits=[r1, r2], inc="DV")
            ntd = nts = None
            for c in range(4):
                ntd = p.op("vector", lambda e, r=r, c=c: e.tensor_scalar_mul(
                    out=r["pbAs"][:, c * 128:(c + 1) * 128],
                    in0=r["seA"][:, c * 128:(c + 1) * 128],
                    scalar1=r["sc"][:, c, :]),
                    waits=([t] + wre) if c == 0 else (), inc="DV")
                nts = p.op("scalar", lambda e, r=r, c=c: e.mul(
                    out=r["pbBs"][:, c * 256:(c + 1) * 256],
                    in_=r["seB"][:, c * 256:(c + 1) * 256],
                    mul=r["sc"][:, 4 + c, :]),
                    waits=([t] + wre) if c == 0 else (), inc="SC")
            stN[hh] = (ntd, nts)

        def emit_tb(hh):
            r = quad(hh)
            base, mt = r["base"], r["mt"]
            ev = None
            ecop = {}
            for c in range(4):
                PTs = (PT, PT2)[c % 2]
                pTs = pT[:, (c % 2) * 384:(c % 2) * 384 + 384]
                wT = [stN[hh][0], stN[hh][1]]
                if c >= 2:
                    wT.append(ecop[c - 2])
                elif hh >= 1:
                    wT.append(stB2[hh - 1][0])
                tA = p.op("tensor", lambda e, r=r, PTs=PTs, c=c: e.transpose(
                    out=PTs[:, 0:128], in_=r["pbAs"][:, c * 128:(c + 1) * 128],
                    identity=sident[:]), waits=wT, inc="PE")
                p.op("tensor", lambda e, r=r, PTs=PTs, c=c: e.transpose(
                    out=PTs[:, 128:256], in_=r["pbBs"][:, c * 256:c * 256 + 128],
                    identity=sident[:]))
                tB1 = p.op("tensor", lambda e, r=r, PTs=PTs, c=c: e.transpose(
                    out=PTs[:, 256:384], in_=r["pbBs"][:, c * 256 + 128:c * 256 + 256],
                    identity=sident[:]), inc="PE")
                o1 = p.op("scalar", lambda e, pTs=pTs, PTs=PTs: e.copy(
                    out=pTs[:, 0:128], in_=PTs[:, 0:128]), waits=[tA], inc="SC")
                e2last = p.op("scalar", lambda e, pTs=pTs, PTs=PTs: e.copy(
                    out=pTs[:, 128:384], in_=PTs[:, 128:384]), waits=[tB1], inc="SC")
                ecop[c] = e2last
                wpv = [o1, e2last]
                if ev is not None:
                    wpv.append(ev)
                elif hh >= 1:
                    wpv.append(stB2[hh - 1][1])
                h2 = 2 * mt + (base // 64)
                p.op("tensor", lambda e, c=c, h2=h2, base=base, pTs=pTs: e.matmul(
                    out=P4[base:base + 64, 0:128],
                    lhsT=vt[:, 2 * c, h2 * 64:(h2 + 1) * 64],
                    rhs=pTs[:, 0:128], start=True, stop=False), waits=wpv)
                p.op("tensor", lambda e, c=c, h2=h2, base=base, pTs=pTs: e.matmul(
                    out=P4[base:base + 64, 128:256],
                    lhsT=vt[:, 2 * c, h2 * 64:(h2 + 1) * 64],
                    rhs=pTs[:, 128:256], start=False, stop=False))
                pB1 = p.op("tensor", lambda e, c=c, h2=h2, base=base, pTs=pTs: e.matmul(
                    out=P4[base:base + 64, 128:256],
                    lhsT=vt[:, 2 * c + 1, h2 * 64:(h2 + 1) * 64],
                    rhs=pTs[:, 256:384], start=False, stop=True), inc="PE")
                ev = p.op("vector", lambda e, c=c, mt=mt, base=base: e.tensor_copy(
                    out=ob[base:base + 64, mt, c * 256:(c + 1) * 256],
                    in_=P4[base:base + 64, 0:256]), waits=[pB1], inc="DV")
            stB2[hh] = (ecop[3], ev)

        for hh in range(16):
            emit_scores(hh)
            if hh >= 1:
                emit_tb(hh - 1)
            emit_mid(hh)
        emit_tb(15)
        attn_done = stB2[15][1]

        prev_res = {}
        h_done = None
        for gi2 in range(16):
            m, n2 = gi2 // 2, gi2 % 2
            pp = (P0, P1)[gi2 % 2]
            w0 = [attn_done] if gi2 < 2 else [prev_res[gi2 - 2]]
            tok = None
            for kt in range(8):
                tok = p.op("tensor", lambda e, kt=kt, m=m, n2=n2, pp=pp: e.matmul(
                    out=pp[:], lhsT=swo[:, kt, m * 128:(m + 1) * 128],
                    rhs=ob[:, kt, n2 * 512:(n2 + 1) * 512],
                    start=(kt == 0), stop=(kt == 7)),
                    waits=w0 if kt == 0 else (), inc="PE" if kt == 7 else None)
            ev = p.op("vector", lambda e, m=m, n2=n2, pp=pp: e.tensor_add(
                out=sx[:, m, n2 * 512:(n2 + 1) * 512],
                in0=sx[:, m, n2 * 512:(n2 + 1) * 512], in1=pp[:]),
                waits=[tok], inc="DV")
            prev_res[gi2] = ev
            h_done = ev

        prev_apply = emit_ln(p, lnctx, sx, h_done)

        prev_store = p.dma(
            "sync",
            io["outT"].rearrange("(kt p) t -> p kt t", p=128)[:, :, t0:t0 + TC],
            sx[:], waits=[prev_apply], sem="ST")

    p.op("sync", lambda e: e.nop(), waits=[p.last("ST")])


def build_l0attn():
    nc = bass.Bass("TRN2", target_bir_lowering=False, debug=False)
    io = {}
    io["xT"] = nc.dram_tensor("xT", [D, TOK], F32, kind="ExternalInput").ap()
    for w in ("wq", "wk", "wv", "wo"):
        io[w] = nc.dram_tensor(w, [D, D], F32, kind="ExternalInput").ap()
    io["maskA"] = nc.dram_tensor("maskA", [128, 512], BF16, kind="ExternalInput").ap()
    io["maskB"] = nc.dram_tensor("maskB", [128, 1024], BF16, kind="ExternalInput").ap()
    io["ident"] = nc.dram_tensor("ident", [128, 128], BF16, kind="ExternalInput").ap()
    io["lncol"] = nc.dram_tensor("lncol", [128, 1], BF16, kind="ExternalInput").ap()
    io["onesk1"] = nc.dram_tensor("onesk1", [1, 128], F32, kind="ExternalInput").ap()
    io["outT"] = nc.dram_tensor("outT", [D, TOK], F32, kind="ExternalOutput").ap()
    stack = ExitStack()
    with stack:
        p = Prog(nc, stack)
        emit_l0attn(p, nc, stack, io)
        p.build()
    return nc


def common_consts():
    i = np.arange(128)[:, None]
    jA = np.arange(128)[None, :]
    mA = np.where((jA <= i) & (jA >= i - 64), 1.0, 0.0).astype(BF)
    i2 = (np.arange(128) + 128)[:, None]
    jB = np.arange(256)[None, :]
    mB = np.where((jB <= i2) & (jB >= i2 - 64), 1.0, 0.0).astype(BF)
    return dict(
        maskA=np.tile(mA, (1, 4)).astype(BF), maskB=np.tile(mB, (1, 4)).astype(BF),
        ident=np.eye(128, dtype=BF),
        lncol=np.full((128, 1), 1.0 / 1024.0, dtype=BF),
        onesk1=np.ones((1, 128), dtype=np.float32),
    )





def emit_l0ffn(p, nc, stack, io):
    S = 1.0 / 127.0
    TC = 1024
    sx = sb(nc, stack, "fx", [128, 8, TOK], F32)
    xb = sb(nc, stack, "fxb", [128, 8, TOK], BF16)
    hact = sb(nc, stack, "fhact", [128, 24, TC], BF16)
    wgs = sb(nc, stack, "fwgs", [128, 2, 8, 128], BF16)
    wus = sb(nc, stack, "fwus", [128, 2, 8, 128], BF16)
    wds = sb(nc, stack, "fwds", [128, 2, 24, 128], BF16)
    slncol = sb(nc, stack, "flncol", [128, 1], BF16)
    sonesk1 = sb(nc, stack, "fonesk1", [1, 128], F32)
    srows = sb(nc, stack, "fsrows", [1, 2048], F32)
    t1 = sb(nc, stack, "ft1", [128, 512], F32)

    P0 = ps(nc, stack, "fP0", [128, 512])
    P1 = ps(nc, stack, "fP1", [128, 512])
    P2 = ps(nc, stack, "fP2", [128, 512])
    P3 = ps(nc, stack, "fP3", [128, 512])

    p.dma("sync", slncol[:], io["lncol"], sem="C")
    cK = p.dma("sync", sonesk1[:], io["onesk1"], sem="C")
    cL = cK
    lx = p.dma("sync", sx[:], io["xT"].rearrange("(kt p) t -> p kt t", p=128), sem="Lx")
    cast_done = None
    for kt in range(8):
        cast_done = p.op("vector", lambda e, kt=kt: e.tensor_copy(
            out=xb[:, kt, :], in_=sx[:, kt, :]), waits=[lx] if kt == 0 else (),
            inc="DV")

    res_last = None
    for tc in range(TOK // TC):
        t0 = tc * TC
        dma_toks = {}
        fuse_tok = res_last
        for mh in range(2):
            bi = mh % 2
            p.dma("gpsimd", wgs[:, bi, :, :],
                  io["wgT"][:, mh * 128:(mh + 1) * 128]
                  .rearrange("(kt p) m -> p kt m", p=128),
                  waits=[res_last] if res_last else ())
            dma_toks[mh] = p.dma("gpsimd", wus[:, bi, :, :],
                                 io["wuT"][:, mh * 128:(mh + 1) * 128]
                                 .rearrange("(kt p) m -> p kt m", p=128))
        for mh in range(24):
            bi = mh % 2
            for n in range(TC // 512):
                w0 = [dma_toks[max(mh, 1)] if mh < 2 else dma_toks[mh],
                      cast_done, fuse_tok]
                last_pe = None
                for kt in range(8):
                    p.op("tensor", lambda e, kt=kt, bi=bi, n=n, t0=t0: e.matmul(
                        out=P0[:], lhsT=wgs[:, bi, kt, :],
                        rhs=xb[:, kt, t0 + n * 512:t0 + (n + 1) * 512],
                        start=(kt == 0), stop=(kt == 7)),
                        waits=w0 if kt == 0 else ())
                for kt in range(8):
                    last_pe = p.op("tensor", lambda e, kt=kt, bi=bi, n=n, t0=t0: e.matmul(
                        out=P1[:], lhsT=wus[:, bi, kt, :],
                        rhs=xb[:, kt, t0 + n * 512:t0 + (n + 1) * 512],
                        start=(kt == 0), stop=(kt == 7)),
                        inc="PE" if kt == 7 else None)
                a1 = p.op("scalar", lambda e: e.activation(
                    out=t1[:], in_=P0[:], func=AF.Silu, scale=S),
                    waits=[last_pe], inc="SC")
                fuse_tok = p.op("vector", lambda e, mh=mh, n=n: e.tensor_mul(
                    out=hact[:, mh, n * 512:(n + 1) * 512], in0=t1[:], in1=P1[:]),
                    waits=[a1], inc="DV")
            gtok = p.last("PE")
            nxt = mh + 2
            if nxt < 24:
                p.dma("gpsimd", wgs[:, nxt % 2, :, :],
                      io["wgT"][:, nxt * 128:(nxt + 1) * 128]
                      .rearrange("(kt p) m -> p kt m", p=128), waits=[gtok])
                dma_toks[nxt] = p.dma("gpsimd", wus[:, nxt % 2, :, :],
                                      io["wuT"][:, nxt * 128:(nxt + 1) * 128]
                                      .rearrange("(kt p) m -> p kt m", p=128))
        hact_done = fuse_tok

        ddma = {}
        for m in range(2):
            ddma[m] = p.dma("gpsimd", wds[:, m % 2, :, :],
                            io["wdT"][:, m * 128:(m + 1) * 128]
                            .rearrange("(kt p) mm -> p kt mm", p=128),
                            waits=[hact_done] if tc > 0 else ())
        for m in range(8):
            bi = m % 2
            for n in range(TC // 512):
                w0 = [ddma[max(m, 1)] if m < 2 else ddma[m], hact_done, res_last]
                tok = None
                for kt in range(24):
                    tok = p.op("tensor", lambda e, kt=kt, bi=bi, n=n: e.matmul(
                        out=P2[:], lhsT=wds[:, bi, kt, :],
                        rhs=hact[:, kt, n * 512:(n + 1) * 512],
                        start=(kt == 0), stop=(kt == 23)),
                        waits=w0 if kt == 0 else (), inc="PE" if kt == 23 else None)
                res_last = p.op("vector", lambda e, m=m, n=n, t0=t0: e.scalar_tensor_tensor(
                    out=sx[:, m, t0 + n * 512:t0 + (n + 1) * 512], in0=P2[:], scalar=S * S,
                    in1=sx[:, m, t0 + n * 512:t0 + (n + 1) * 512],
                    op0=ALU.mult, op1=ALU.add),
                    waits=[tok], inc="DV")
            dtok = p.last("PE")
            nxt = m + 2
            if nxt < 8:
                ddma[nxt] = p.dma("gpsimd", wds[:, nxt % 2, :, :],
                                  io["wdT"][:, nxt * 128:(nxt + 1) * 128]
                                  .rearrange("(kt p) mm -> p kt mm", p=128),
                                  waits=[dtok])

    lnctx = dict(lncol=slncol, onesk1=sonesk1,
                 hb=lambda kt: hact[:, 2 * kt:2 * kt + 2, :].rearrange("p a t -> p (a t)"),
                 hsq=lambda kt: xb[:, kt, :],
                 srows=srows, PA=P2, PB=P3)
    var_save = io.get("var_sb")
    apl = emit_ln(p, lnctx, sx, res_last, var_save=var_save)
    stt = p.dma("sync", io["outT"].rearrange("(kt p) t -> p kt t", p=128), sx[:],
                waits=[apl], sem="ST")
    if io.get("varT") is not None:
        p.dma("sync", io["varT"], io["var_sb"][:], waits=[p.last("DV")], sem="ST")
    p.op("sync", lambda e: e.nop(), waits=[p.last("ST")])


def build_l0ffn(save_var=False):
    nc = bass.Bass("TRN2", target_bir_lowering=False, debug=False)
    io = {}
    io["xT"] = nc.dram_tensor("xT", [D, TOK], F32, kind="ExternalInput").ap()
    io["wgT"] = nc.dram_tensor("wgT", [D, 3072], I32, kind="ExternalInput").ap()
    io["wuT"] = nc.dram_tensor("wuT", [D, 3072], I32, kind="ExternalInput").ap()
    io["wdT"] = nc.dram_tensor("wdT", [3072, D], I32, kind="ExternalInput").ap()
    io["lncol"] = nc.dram_tensor("lncol", [128, 1], BF16, kind="ExternalInput").ap()
    io["onesk1"] = nc.dram_tensor("onesk1", [1, 128], F32, kind="ExternalInput").ap()
    io["outT"] = nc.dram_tensor("outT", [D, TOK], F32, kind="ExternalOutput").ap()
    stack = ExitStack()
    with stack:
        p = Prog(nc, stack)
        if save_var:
            io["varT"] = nc.dram_tensor("varT", [1, TOK], F32, kind="ExternalOutput").ap()
            io["var_sb"] = sb(nc, stack, "var_sb", [1, TOK], F32)
        emit_l0ffn(p, nc, stack, io)
        p.build()
    return nc




F32EPS = float(np.finfo(np.float32).eps)


def emit_l1a(p, nc, stack, io):
    xb = sb(nc, stack, "axb", [128, 8, TOK], BF16)
    xsq = sb(nc, stack, "axsq", [128, 8, TOK], BF16)
    w1b = sb(nc, stack, "aw1b", [128, 8, 256], BF16)
    w2b = sb(nc, stack, "aw2b", [128, 2, 6144], BF16)
    h2 = sb(nc, stack, "ah2", [128, 2, TOK], BF16)
    vp = sb(nc, stack, "avp", [128, 16, 512], BF16)
    qsq = sb(nc, stack, "aqsq", [128, 16, 512], BF16)
    _xsqf = xsq[:, :, :].rearrange("p a t -> p (a t)")
    qp = _xsqf[:, 0:8192].rearrange("p (r t) -> p r t", r=16)
    kp = _xsqf[:, 8192:16384].rearrange("p (r t) -> p r t", r=16)
    sblkd = sb(nc, stack, "ablkd", [128, 16], BF16)
    sexp16 = sb(nc, stack, "aexp16", [16, 128], F32)
    slncol = sb(nc, stack, "alncol", [128, 1], BF16)
    sonesk1 = sb(nc, stack, "aonesk1", [1, 128], F32)
    r2row = sb(nc, stack, "ar2row", [1, TOK], F32)
    srow = sb(nc, stack, "asrow", [1, 512], F32)
    rbc = sb(nc, stack, "arbc", [128, 512], F32)
    ztmp = sb(nc, stack, "aztmp", [128, 512], F32)
    zr = sb(nc, stack, "azr", [128, 512], BF16)
    rqsc = sb(nc, stack, "arqsc", [16, 512], F32)
    rksc = sb(nc, stack, "arksc", [16, 512], F32)
    cosc = sb(nc, stack, "acosc", [128, 4, 512], BF16)
    sinc = sb(nc, stack, "asinc", [128, 4, 512], BF16)
    tp1 = sb(nc, stack, "atp1", [128, 4, 512], BF16)
    tp2 = sb(nc, stack, "atp2", [128, 4, 512], BF16)
    tp4 = sb(nc, stack, "atp4", [128, 4, 512], BF16)
    ksg = qsq
    scarry = sb(nc, stack, "acarry", [128, 16], F32)
    scred = sb(nc, stack, "acred", [128, 16, 1], F32)

    P0 = ps(nc, stack, "aP0", [128, 512])
    P1 = ps(nc, stack, "aP1", [128, 512])
    Pss = ps(nc, stack, "aPss", [16, 512])
    Pss2 = ps(nc, stack, "aPss2", [16, 512])
    Pbq = ps(nc, stack, "aPbq", [128, 512])
    Pbk = ps(nc, stack, "aPbk", [128, 512])

    p.dma("gpsimd", w1b[:], io["w1T"].rearrange("(kt p) m -> p kt m", p=128))
    p.dma("gpsimd", w2b[:], io["w2Tp"].rearrange("(kt p) m -> p kt m", p=128))
    p.dma("sync", sblkd[:], io["blkd"], sem="C")
    p.dma("sync", sexp16[:], io["exp16"], sem="C")
    p.dma("sync", slncol[:], io["lncol"], sem="C")
    cC = p.dma("sync", sonesk1[:], io["onesk1"], sem="C")
    lb = le = cL = cK = cC
    lx = p.dma("gpsimd", xb[:], io["xT"].rearrange("(kt p) t -> p kt t", p=128))
    lw1 = lw2 = lx

    p.op("vector", lambda e: e.memset(scarry[:], 0.0), inc="DV")
    sq_toks = []
    for kt in range(8):
        sq_toks.append(p.op("vector", lambda e, kt=kt: e.tensor_mul(
            out=xsq[:, kt, :], in0=xb[:, kt, :], in1=xb[:, kt, :]),
            waits=[lx] if kt == 0 else (), inc="DV"))
    guard = sq_toks[-1]
    r2_toks = []
    for n2 in range(4):
        s2 = None
        for kt in range(8):
            s2 = p.op("tensor", lambda e, kt=kt, n2=n2: e.matmul(
                out=P0[0:1, :], lhsT=slncol[:],
                rhs=xsq[:, kt, n2 * 512:(n2 + 1) * 512],
                start=(kt == 0), stop=(kt == 7)),
                waits=[guard, cL] if kt == 0 else (), inc="PE" if kt == 7 else None)
        e1 = p.op("vector", lambda e, n2=n2: e.tensor_scalar_add(
            out=srow[:], in0=P0[0:1, :], scalar1=F32EPS), waits=[s2], inc="DV")
        e2 = p.op("scalar", lambda e: e.activation(out=srow[:], in_=srow[:], func=AF.Sqrt),
                  waits=[e1], inc="SC")
        e3 = p.op("vector", lambda e, n2=n2: e.reciprocal(
            out=r2row[:, n2 * 512:(n2 + 1) * 512], in_=srow[:]), waits=[e2], inc="DV")
        guard = e3
        r2_toks.append(e3)

    h2_tok = None
    for m in range(2):
        for n in range(4):
            tok = None
            for kt in range(8):
                tok = p.op("tensor", lambda e, kt=kt, m=m, n=n: e.matmul(
                    out=P1[:], lhsT=w1b[:, kt, m * 128:(m + 1) * 128],
                    rhs=xb[:, kt, n * 512:(n + 1) * 512],
                    start=(kt == 0), stop=(kt == 7)),
                    waits=[lw1, r2_toks[-1], h2_tok] if kt == 0 else (),
                    inc="PE" if kt == 7 else None)
            b1 = p.op("tensor", lambda e, n=n: e.matmul(
                out=P0[:], lhsT=sonesk1[:], rhs=r2row[0:1, n * 512:(n + 1) * 512],
                start=True, stop=True), waits=[cK], inc="PE")
            v1 = p.op("vector", lambda e: e.tensor_copy(out=rbc[:], in_=P0[:]),
                      waits=[b1], inc="DV")
            v2 = p.op("vector", lambda e: e.tensor_mul(out=ztmp[:], in0=rbc[:], in1=P1[:]),
                      waits=[v1, tok], inc="DV")
            v3 = p.op("vector", lambda e: e.tensor_scalar_max(out=zr[:], in0=ztmp[:], scalar1=0.0),
                      waits=[v2], inc="DV")
            h2_tok = p.op("vector", lambda e, m=m, n=n: e.tensor_mul(
                out=h2[:, m, n * 512:(n + 1) * 512], in0=zr[:], in1=zr[:]),
                waits=[v3], inc="DV")

    prev_chunk_out = None
    prev_rope = None
    for c in range(4):
        t0 = c * 512
        lcos = p.dma("sync", cosc[:], io["cosT"][:, :, t0:t0 + 512],
                     waits=[prev_rope] if prev_rope else (), sem="Lx")
        lsin = p.dma("sync", sinc[:], io["sinT"][:, :, t0:t0 + 512], sem="Lx")
        ev_tok = None
        evs = {}
        gj = 0
        for ti, dst in ((0, qp), (1, kp), (2, vp)):
            for r in range(16):
                j = ti * 16 + r
                pp = (P0, P1)[gj % 2]
                tok = None
                w0 = [lw2, h2_tok, prev_chunk_out] if gj == 0 else []
                if gj >= 2:
                    w0 = [evs[gj - 2]]
                for kt in range(2):
                    tok = p.op("tensor", lambda e, kt=kt, j=j, t0=t0, pp=pp: e.matmul(
                        out=pp[:], lhsT=w2b[:, kt, j * 128:(j + 1) * 128],
                        rhs=h2[:, kt, t0:t0 + 512],
                        start=(kt == 0), stop=(kt == 1)),
                        waits=w0 if kt == 0 else (),
                        inc="PE" if kt == 1 else None)
                ev_tok = p.op("vector", lambda e, dst=dst, r=r, pp=pp: e.tensor_copy(
                    out=dst[:, r, :], in_=pp[:]), waits=[tok], inc="DV")
                evs[gj] = ev_tok
                gj += 1
                if ti < 2:
                    p.op("scalar", lambda e, dst=dst, r=r, sq=qsq: e.activation(
                        out=qsq[:, r, :], in_=dst[:, r, :], func=AF.Square),
                        waits=[ev_tok], inc="SC")
                    if ti == 0:
                        sqtok = p.last("SC")
            if ti < 2:
                pss = Pss if ti == 0 else Pss2
                sst = None
                for r in range(16):
                    sst = p.op("tensor", lambda e, r=r, pss=pss: e.matmul(
                        out=pss[:], lhsT=sblkd[:], rhs=qsq[:, r, :],
                        start=(r == 0), stop=(r == 15)),
                        waits=[p.last("SC"), lb] if r == 0 else (),
                        inc="PE" if r == 15 else None)
                dstc = rqsc if ti == 0 else rksc
                u1 = p.op("vector", lambda e, pss=pss, dstc=dstc: e.tensor_scalar_add(
                    out=dstc[:], in0=pss[:], scalar1=F32EPS), waits=[sst], inc="DV")
                u2 = p.op("scalar", lambda e, dstc=dstc: e.activation(
                    out=dstc[:], in_=dstc[:], func=AF.Sqrt), waits=[u1], inc="SC")
                u3 = p.op("vector", lambda e, dstc=dstc: e.reciprocal(
                    out=dstc[:], in_=dstc[:]), waits=[u2], inc="DV")
                bq = p.op("tensor", lambda e, dstc=dstc, pb=(Pbq if ti == 0 else Pbk):
                          e.matmul(out=pb[:], lhsT=sexp16[:], rhs=dstc[:],
                                   start=True, stop=True),
                          waits=[u3, le], inc="PE")
                for r in range(16):
                    ev_tok = p.op("vector", lambda e, dst=dst, r=r,
                                  pb=(Pbq if ti == 0 else Pbk): e.tensor_mul(
                        out=dst[:, r, :], in0=dst[:, r, :], in1=pb[:]),
                        waits=[bq] if r == 0 else (), inc="DV")
        rope_tok = None
        for dst in (qp, kp):
            z1 = dst[:, 0:4, :]
            z2 = dst[:, 8:12, :]
            w0 = [lcos, lsin, p.last("DV")]
            a = p.op("vector", lambda e, z1=z1: e.tensor_mul(out=tp1[:], in0=z1, in1=cosc[:]),
                     waits=w0, inc="DV")
            b = p.op("vector", lambda e, z2=z2: e.tensor_mul(out=tp2[:], in0=z2, in1=sinc[:]),
                     inc="DV")
            d = p.op("vector", lambda e, z1=z1: e.tensor_mul(out=tp4[:], in0=z1, in1=sinc[:]),
                     inc="DV")
            f = p.op("vector", lambda e, z1=z1: e.tensor_add(out=z1, in0=tp1[:], in1=tp2[:]),
                     waits=[b], inc="DV")
            g = p.op("vector", lambda e, z2=z2: e.tensor_mul(out=tp1[:], in0=z2, in1=cosc[:]),
                     waits=[f], inc="DV")
            rope_tok = p.op("vector", lambda e, z2=z2: e.tensor_tensor(
                out=z2, in0=tp1[:], in1=tp4[:], op=ALU.subtract),
                waits=[g], inc="DV")
            prev_rope = rope_tok
        qf = qp[:, :, :].rearrange("p r t -> p (r t)")
        rq1 = p.op("vector", lambda e, qf=qf: e.tensor_scalar_max(
            out=qf, in0=qf, scalar1=0.0), waits=[rope_tok], inc="DV")
        st1 = p.dma("sync", io["rqT"][:, 0:8, t0:t0 + 512], qp[:, 0:8, :],
                    waits=[rq1], sem="ST")
        st2 = p.dma("sync", io["rqT"][:, 8:16, t0:t0 + 512], qp[:, 8:16, :],
                    waits=[rq1], sem="ST")
        kf = kp[:, :, :].rearrange("p r t -> p (r t)")
        ks1 = p.op("scalar", lambda e, kf=kf: e.activation(
            out=ksg[:, :, :].rearrange("p r t -> p (r t)"), in_=kf, func=AF.Sigmoid),
            waits=[rope_tok, prev_chunk_out], inc="SC")
        sv1 = p.op("vector", lambda e: e.tensor_mul(
            out=ksg[:, :, :].rearrange("p r t -> p (r t)"),
            in0=ksg[:, :, :].rearrange("p r t -> p (r t)"),
            in1=vp[:, :, :].rearrange("p r t -> p (r t)")),
            waits=[ks1], inc="DV")
        sc_tok = sv1
        for r in range(16):
            sc_tok = p.op("vector", lambda e, r=r: e.tensor_tensor_scan(
                out=vp[:, r, :], data0=ksg[:, r, :], data1=ksg[:, r, :],
                initial=scarry[:, r:r + 1], op0=ALU.add, op1=ALU.bypass),
                waits=[sc_tok] if r == 0 else (), inc="DV")
        cr1 = p.op("vector", lambda e: e.tensor_reduce(
            out=scred[:, :, 0:1], in_=ksg[:, :, :], axis=AX.X, op=ALU.add),
            waits=[sc_tok], inc="DV")
        cr2 = p.op("vector", lambda e: e.tensor_add(
            out=scarry[:], in0=scarry[:], in1=scred[:, :, 0]),
            waits=[cr1], inc="DV")
        yp1 = p.op("vector", lambda e: e.tensor_mul(
            out=ksg[:, :, :].rearrange("p r t -> p (r t)"),
            in0=qp[:, :, :].rearrange("p r t -> p (r t)"),
            in1=vp[:, :, :].rearrange("p r t -> p (r t)")),
            waits=[cr2], inc="DV")
        st3 = p.dma("sync", io["ypT"][:, 0:8, t0:t0 + 512], ksg[:, 0:8, :],
                    waits=[yp1], sem="ST")
        st4 = p.dma("sync", io["ypT"][:, 8:16, t0:t0 + 512], ksg[:, 8:16, :],
                    waits=[yp1], sem="ST")
        prev_chunk_out = p.last("ST")
    stc = p.dma("sync", io["TcT"], scarry[:], waits=[p.last("DV")], sem="ST")
    p.op("sync", lambda e: e.nop(), waits=[p.last("ST")])


def build_l1a():
    nc = bass.Bass("TRN2", target_bir_lowering=False, debug=False)
    io = {}
    io["xT"] = nc.dram_tensor("xT", [D, TOK], F32, kind="ExternalInput").ap()
    io["w1T"] = nc.dram_tensor("w1T", [D, 256], F32, kind="ExternalInput").ap()
    io["w2Tp"] = nc.dram_tensor("w2Tp", [256, 6144], F32, kind="ExternalInput").ap()
    io["cosT"] = nc.dram_tensor("cosT", [128, 4, TOK], BF16, kind="ExternalInput").ap()
    io["sinT"] = nc.dram_tensor("sinT", [128, 4, TOK], BF16, kind="ExternalInput").ap()
    io["blkd"] = nc.dram_tensor("blkd", [128, 16], BF16, kind="ExternalInput").ap()
    io["exp16"] = nc.dram_tensor("exp16", [16, 128], F32, kind="ExternalInput").ap()
    io["lncol"] = nc.dram_tensor("lncol", [128, 1], BF16, kind="ExternalInput").ap()
    io["onesk1"] = nc.dram_tensor("onesk1", [1, 128], F32, kind="ExternalInput").ap()
    io["ypT"] = nc.dram_tensor("ypT", [128, 16, TOK], BF16, kind="ExternalOutput").ap()
    io["rqT"] = nc.dram_tensor("rqT", [128, 16, TOK], BF16, kind="ExternalOutput").ap()
    io["TcT"] = nc.dram_tensor("TcT", [128, 16], F32, kind="ExternalOutput").ap()
    stack = ExitStack()
    with stack:
        p = Prog(nc, stack)
        emit_l1a(p, nc, stack, io)
        p.build()
    return nc



def emit_l1b(p, nc, stack, io):
    sv = sb(nc, stack, "bsv", [128, 4, 8192], BF16)
    so = sb(nc, stack, "bso", [128, 4, 8192], BF16)
    lx = p.dma("sync", sv[:], io["svF"])
    toks = []
    for r in range(4):
        toks.append(p.op("vector", lambda e, r=r: e.tensor_tensor_scan(
            out=so[:, r, :], data0=sv[:, r, :], data1=sv[:, r, :],
            initial=0.0, op0=ALU.add, op1=ALU.bypass),
            waits=[lx] if r == 0 else (), inc="DV"))
    st = p.dma("sync", io["SF"], so[:], waits=[toks[-1]], sem="ST")
    p.op("sync", lambda e: e.nop(), waits=[p.last("ST")])


def build_l1b():
    nc = bass.Bass("TRN2", target_bir_lowering=False, debug=False)
    io = {}
    io["svF"] = nc.dram_tensor("svF", [128, 4, 8192], BF16, kind="ExternalInput").ap()
    io["SF"] = nc.dram_tensor("SF", [128, 4, 8192], BF16, kind="ExternalOutput").ap()
    stack = ExitStack()
    with stack:
        p = Prog(nc, stack)
        emit_l1b(p, nc, stack, io)
        p.build()
    return nc



def emit_l1c(p, nc, stack, io):
    sx = sb(nc, stack, "cx", [128, 8, TOK], F32)
    sS = sb(nc, stack, "cS", [128, 16, 512], BF16)
    srq = sb(nc, stack, "crq", [128, 16, 512], BF16)
    sag = sb(nc, stack, "cag", [128, 8, 16], F32)
    smk = sb(nc, stack, "cmk", [128, 8], F32)
    cw1s = sb(nc, stack, "cw1s", [128, 16, 256], BF16)
    cw1b = sb(nc, stack, "cw1b", [128, 16, 256], BF16)
    cw2b = sb(nc, stack, "cw2b", [128, 2, 2048], BF16)
    uv1 = sb(nc, stack, "cuv1", [128, 2, TOK], BF16)
    sg = sb(nc, stack, "csg", [128, 512], F32)
    tu = sb(nc, stack, "ctu", [128, 512], F32)
    hqs = sb(nc, stack, "chqs", [128, 8, TOK], BF16)
    _cSf = None
    slncol = sb(nc, stack, "clncol", [128, 1], BF16)
    sonesk1 = sb(nc, stack, "conesk1", [1, 128], F32)
    srows = sb(nc, stack, "csrows", [1, 2048], F32)
    sWr = sb(nc, stack, "cWr", [128, 8, 8], F32)
    sbr = sb(nc, stack, "cbr", [1, 8], F32)
    sbrb = sb(nc, stack, "cbrb", [128, 8], F32)
    sident = sb(nc, stack, "cident", [128, 128], BF16)
    lg = sb(nc, stack, "clg", [128, 8], F32)
    mxv = sb(nc, stack, "cmxv", [128, 1], F32)
    nmx = sb(nc, stack, "cnmx", [128, 1], F32)
    sexr = sb(nc, stack, "csexr", [128, 8], F32)
    sse = sb(nc, stack, "csse", [128, 1], F32)
    ohb = sb(nc, stack, "cohb", [128, 8], BF16)
    sohT = sb(nc, stack, "csohT", [8, TOK], BF16)
    scs = sb(nc, stack, "cscs", [8, TOK], F32)
    scnt = sb(nc, stack, "ccnt", [8, 1], F32)
    sgate = sb(nc, stack, "cgate", [128, 16], F32)

    P0 = ps(nc, stack, "cP0", [128, 512])
    P1 = ps(nc, stack, "cP1", [128, 512])
    P2 = ps(nc, stack, "cP2", [128, 512])
    P3 = ps(nc, stack, "cP3", [128, 512])
    POH = ps(nc, stack, "cPOH", [8, 128], BF16)

    lx = p.dma("sync", sx[:], io["xT"].rearrange("(kt p) t -> p kt t", p=128), sem="Lx")
    p.dma("gpsimd", cw1b[:], io["cw1p"].rearrange("(r p) m -> p r m", p=128))
    p.dma("gpsimd", cw2b[:], io["cw2T"].rearrange("(kt p) m -> p kt m", p=128))
    lwg = p.dma("gpsimd", sWr[:], io["Wr"].rearrange("(kt p) e -> p kt e", p=128))
    lw1 = lw2 = lwr = lwg
    p.dma("sync", sbr[:], io["br"], sem="C")
    p.dma("sync", sident[:], io["ident"], sem="C")
    p.dma("sync", slncol[:], io["lncol"], sem="C")
    cC = p.dma("sync", sonesk1[:], io["onesk1"], sem="C")
    lbr = cI = cL = cK = cC

    lag = p.dma("sync", sag[:], io["TcAll"].rearrange("(r p) f -> p r f", p=128),
                sem="C2")
    lmk = p.dma("sync", smk[:], io["carrymask"], sem="C2")
    cm0 = None
    lvl = []
    for r in range(8):
        lvl.append(p.op("vector", lambda e, r=r: e.tensor_scalar(
            out=sag[:, r, :], in0=sag[:, r, :], scalar1=smk[:, r:r + 1],
            scalar2=None, op0=ALU.mult), waits=[lag, lmk] if r == 0 else (),
            inc="DV"))
    t4 = []
    for r in range(4):
        t4.append(p.op("vector", lambda e, r=r: e.tensor_add(
            out=sag[:, r, :], in0=sag[:, r, :], in1=sag[:, r + 4, :]),
            waits=[lvl[-1]] if r == 0 else (), inc="DV"))
    t2 = []
    for r in range(2):
        t2.append(p.op("vector", lambda e, r=r: e.tensor_add(
            out=sag[:, r, :], in0=sag[:, r, :], in1=sag[:, r + 2, :]),
            waits=[t4[-1]] if r == 0 else (), inc="DV"))
    carry_tok = p.op("vector", lambda e: e.tensor_add(
        out=sag[:, 0, :], in0=sag[:, 0, :], in1=sag[:, 1, :]),
        waits=[t2[-1]], inc="DV")
    cw1s_tok = None
    for r in range(16):
        cw1s_tok = p.op("vector", lambda e, r=r: e.tensor_scalar(
            out=cw1s[:, r, :], in0=cw1b[:, r, :], scalar1=sag[:, 0, r:r + 1],
            scalar2=None, op0=ALU.mult),
            waits=[carry_tok, lw1] if r == 0 else (), inc="DV")

    uv1_tok = None
    for c in range(4):
        t0 = c * 512
        lS1 = p.dma("sync", sS[:, 0:8, :], io["ypT"][:, 0:8, t0:t0 + 512],
                    waits=[uv1_tok], sem="Lx")
        lS2 = p.dma("sync", sS[:, 8:16, :], io["ypT"][:, 8:16, t0:t0 + 512], sem="Lx")
        lq1 = p.dma("sync", srq[:, 0:8, :], io["rqT"][:, 0:8, t0:t0 + 512], sem="Lx")
        lq2 = p.dma("sync", srq[:, 8:16, :], io["rqT"][:, 8:16, t0:t0 + 512], sem="Lx")
        for m2 in range(2):
            tok = None
            for r in range(16):
                p.op("tensor", lambda e, r=r, m2=m2: e.matmul(
                    out=P0[:], lhsT=cw1b[:, r, m2 * 128:(m2 + 1) * 128],
                    rhs=sS[:, r, :], start=(r == 0), stop=False),
                    waits=[lS1, lS2, lq1, lq2, cw1s_tok, uv1_tok] if r == 0 else ())
            for r in range(16):
                tok = p.op("tensor", lambda e, r=r, m2=m2: e.matmul(
                    out=P0[:], lhsT=cw1s[:, r, m2 * 128:(m2 + 1) * 128],
                    rhs=srq[:, r, :], start=False, stop=(r == 15)),
                    inc="PE" if r == 15 else None)
            uv1_tok = p.op("vector", lambda e, m2=m2, t0=t0: e.tensor_copy(
                out=uv1[:, m2, t0:t0 + 512], in_=P0[:]), waits=[tok], inc="DV")

    res_tok = None
    for n in range(4):
        for m in range(8):
            tokg = None
            toku = None
            for kt in range(2):
                tokg = p.op("tensor", lambda e, kt=kt, m=m, n=n: e.matmul(
                    out=P1[:], lhsT=cw2b[:, kt, (m + 8) * 128:(m + 9) * 128],
                    rhs=uv1[:, kt, n * 512:(n + 1) * 512],
                    start=(kt == 0), stop=(kt == 1)),
                    waits=[uv1_tok, lw2, res_tok] if kt == 0 else (),
                    inc="PE" if kt == 1 else None)
            for kt in range(2):
                toku = p.op("tensor", lambda e, kt=kt, m=m, n=n: e.matmul(
                    out=P0[:], lhsT=cw2b[:, kt, m * 128:(m + 1) * 128],
                    rhs=uv1[:, kt, n * 512:(n + 1) * 512],
                    start=(kt == 0), stop=(kt == 1)),
                    inc="PE" if kt == 1 else None)
            a1 = p.op("scalar", lambda e: e.activation(out=sg[:], in_=P1[:], func=AF.Silu),
                      waits=[tokg], inc="SC")
            a2 = p.op("vector", lambda e: e.tensor_mul(out=tu[:], in0=sg[:], in1=P0[:]),
                      waits=[a1, toku], inc="DV")
            res_tok = p.op("vector", lambda e, m=m, n=n: e.scalar_tensor_tensor(
                out=sx[:, m, n * 512:(n + 1) * 512],
                in0=sx[:, m, n * 512:(n + 1) * 512], scalar=2.0, in1=tu[:],
                op0=ALU.mult, op1=ALU.add),
                waits=[a2, lx] if (n == 0 and m == 0) else [a2], inc="DV")

    _cSf = sS[:, :, :].rearrange("p r t -> p (r t)")
    _crqf = srq[:, :, :].rearrange("p r t -> p (r t)")

    def _hb(kt):
        base = _cSf if kt < 4 else _crqf
        return base[:, (kt % 4) * 2048:(kt % 4 + 1) * 2048]
    lnctx = dict(lncol=slncol, onesk1=sonesk1,
                 hb=_hb, hsq=lambda kt: hqs[:, kt, :],
                 srows=srows, PA=P2, PB=P3)
    apl = emit_ln(p, lnctx, sx, res_tok)
    p.dma("sync", io["outT"].rearrange("(kt p) t -> p kt t", p=128), sx[:],
          waits=[apl], sem="ST")

    bb = p.op("tensor", lambda e: e.matmul(
        out=P2[:, 0:8], lhsT=sonesk1[:], rhs=sbr[:], start=True, stop=True),
        waits=[lbr, cK, apl], inc="PE")
    bbv = p.op("vector", lambda e: e.tensor_copy(out=sbrb[:], in_=P2[:, 0:8]),
               waits=[bb], inc="DV")
    oh_toks = []
    for tt in range(16):
        tok = None
        for kt in range(8):
            tok = p.op("tensor", lambda e, kt=kt, tt=tt: e.matmul(
                out=P3[:, 0:8], lhsT=sx[:, kt, tt * 128:(tt + 1) * 128],
                rhs=sWr[:, kt, :], start=(kt == 0), stop=(kt == 7)),
                waits=[lwr, bbv] if (kt == 0 and tt == 0) else (),
                inc="PE" if kt == 7 else None)
        g1 = p.op("vector", lambda e: e.tensor_add(out=lg[:], in0=sbrb[:], in1=P3[:, 0:8]),
                  waits=[tok], inc="DV")
        g2 = p.op("vector", lambda e: e.reduce_max(out=mxv[:], in_=lg[:], axis=AX.X),
                  waits=[g1], inc="DV")
        g3 = p.op("vector", lambda e: e.tensor_scalar(
            out=ohb[:], in0=lg[:], scalar1=mxv[:], scalar2=None, op0=ALU.is_equal),
            waits=[g2], inc="DV")
        g4 = p.op("vector", lambda e: e.tensor_scalar_mul(out=nmx[:], in0=mxv[:], scalar1=-1.0),
                  waits=[g2], inc="DV")
        g5 = p.op("scalar", lambda e: e.activation(
            out=sexr[:], in_=lg[:], func=AF.Exp, bias=nmx[:], accum_out=sse[:]),
            waits=[g4, g3], inc="SC")
        g6 = p.op("vector", lambda e, tt=tt: e.reciprocal(
            out=sgate[:, tt:tt + 1], in_=sse[:]), waits=[g5], inc="DV")
        tr = p.op("tensor", lambda e: e.transpose(out=POH[:], in_=ohb[:], identity=sident[:]),
                  waits=[g3, cI], inc="PE")
        oh_toks.append(p.op("vector", lambda e, tt=tt: e.tensor_copy(
            out=sohT[:, tt * 128:(tt + 1) * 128], in_=POH[:]), waits=[tr], inc="DV"))
        p.op("tensor", lambda e: e.nop(), waits=[p.last("DV")])

    sc1 = p.op("vector", lambda e: e.tensor_tensor_scan(
        out=scs[:], data0=sohT[:], data1=sohT[:], initial=0.0,
        op0=ALU.add, op1=ALU.bypass), waits=[oh_toks[-1]], inc="DV")
    sc2 = p.op("vector", lambda e: e.tensor_copy(out=scnt[:], in_=scs[:, TOK - 1:TOK]),
               waits=[sc1], inc="DV")
    p.dma("sync", io["ohT"], sohT[:], waits=[oh_toks[-1]], sem="ST")
    p.dma("sync", io["csumT"], scs[:], waits=[sc2], sem="ST")
    p.dma("sync", io["counts"], scnt[:], waits=[sc2], sem="ST")
    p.dma("sync", io["gateT"], sgate[:], waits=[p.last("DV")], sem="ST")
    p.op("sync", lambda e: e.nop(), waits=[p.last("ST")])


def build_l1c():
    nc = bass.Bass("TRN2", target_bir_lowering=False, debug=False)
    io = {}
    io["xT"] = nc.dram_tensor("xT", [D, TOK], F32, kind="ExternalInput").ap()
    io["ypT"] = nc.dram_tensor("ypT", [128, 16, TOK], BF16, kind="ExternalInput").ap()
    io["rqT"] = nc.dram_tensor("rqT", [128, 16, TOK], BF16, kind="ExternalInput").ap()
    io["TcAll"] = nc.dram_tensor("TcAll", [8 * 128, 16], F32, kind="ExternalInput").ap()
    io["carrymask"] = nc.dram_tensor("carrymask", [128, 8], F32, kind="ExternalInput").ap()
    io["cw1p"] = nc.dram_tensor("cw1p", [2048, 256], F32, kind="ExternalInput").ap()
    io["cw2T"] = nc.dram_tensor("cw2T", [256, 2048], F32, kind="ExternalInput").ap()
    io["Wr"] = nc.dram_tensor("Wr", [D, 8], F32, kind="ExternalInput").ap()
    io["br"] = nc.dram_tensor("br", [1, 8], F32, kind="ExternalInput").ap()
    io["ident"] = nc.dram_tensor("ident", [128, 128], BF16, kind="ExternalInput").ap()
    io["lncol"] = nc.dram_tensor("lncol", [128, 1], BF16, kind="ExternalInput").ap()
    io["onesk1"] = nc.dram_tensor("onesk1", [1, 128], F32, kind="ExternalInput").ap()
    io["outT"] = nc.dram_tensor("outT", [D, TOK], F32, kind="ExternalOutput").ap()
    io["ohT"] = nc.dram_tensor("ohT", [8, TOK], BF16, kind="ExternalOutput").ap()
    io["csumT"] = nc.dram_tensor("csumT", [8, TOK], F32, kind="ExternalOutput").ap()
    io["counts"] = nc.dram_tensor("counts", [8, 1], F32, kind="ExternalOutput").ap()
    io["gateT"] = nc.dram_tensor("gateT", [128, 16], F32, kind="ExternalOutput").ap()
    stack = ExitStack()
    with stack:
        p = Prog(nc, stack)
        emit_l1c(p, nc, stack, io)
        p.build()
    return nc





def emit_l4(p, nc, stack, io):
    sx = sb(nc, stack, "m4x", [128, 8, TOK], F32)
    xinb = sb(nc, stack, "mxin", [128, 8, TOK], BF16)
    w1b = sb(nc, stack, "mw1", [128, 2, 8, 1024], BF16)
    w2b = sb(nc, stack, "mw2", [128, 2, 8, 1024], BF16)
    hcb = sb(nc, stack, "mhc", [128, 8, 512], BF16)
    soh = sb(nc, stack, "moh", [8, TOK], BF16)
    scs = sb(nc, stack, "mcs", [8, TOK], F32)
    skept = sb(nc, stack, "mkept", [8, TOK], BF16)
    sWT = sb(nc, stack, "mWT", [8, TOK], BF16)
    sgate = sb(nc, stack, "mgate", [128, 16], F32)
    sgateb = sb(nc, stack, "mgateb", [128, 16], BF16)
    sdiag = sb(nc, stack, "mdiag", [128, 512], BF16)
    sbcdump = sb(nc, stack, "mbcdump", [8, 512], F32)
    bc_dump_tok = None
    scnt = sb(nc, stack, "mcnt", [8, 8], F32)
    sprior = sb(nc, stack, "mprior", [8, 8], F32)
    soffs = sb(nc, stack, "moffs", [8, 1], F32)
    sb1 = sb(nc, stack, "mb1", [128, 8, 8], F32)
    sb2 = sb(nc, stack, "mb2", [8, 1024], BF16)
    ssel = sb(nc, stack, "msel", [8, 1024], BF16)
    sident = sb(nc, stack, "mident", [128, 128], BF16)
    sones128 = sb(nc, stack, "mones128", [128, 128], BF16)
    slncol = sb(nc, stack, "mlncol", [128, 1], BF16)
    sonesk1 = sb(nc, stack, "monesk1", [1, 128], F32)
    srows = sb(nc, stack, "msrows", [1, 2048], F32)

    P0 = ps(nc, stack, "mP0", [128, 512])
    P1 = ps(nc, stack, "mP1", [128, 512])
    P2 = ps(nc, stack, "mP2", [128, 512])
    P3 = ps(nc, stack, "mP3", [128, 512])

    lx = p.dma("sync", sx[:], io["xT"].rearrange("(kt p) t -> p kt t", p=128), sem="Lx")
    p.dma("sync", soh[:], io["ohT"], sem="C")
    p.dma("sync", scs[:], io["csumT"], sem="C")
    p.dma("sync", sgate[:], io["gateT"], sem="C")
    if io.get("cntall") is not None:
        p.dma("sync", scnt[:], io["cntall"], sem="C")
    else:
        for r in range(8):
            p.dma("sync", scnt[:, r:r + 1],
                  io["cntAllCol"][r * 8:(r + 1) * 8, 0:1], sem="C")
    p.dma("sync", sprior[:], io["prior"], sem="C")
    p.dma("sync", sb1[:], io["b1T"].rearrange("(m p) e -> p m e", p=128), sem="C")
    lb2 = p.dma("gpsimd", sb2[:], io["b2"])
    p.dma("sync", ssel[:], io["sel8"], sem="C")
    p.dma("sync", sident[:], io["ident"], sem="C")
    p.dma("sync", sones128[:], io["ones128"], sem="C")
    p.dma("sync", slncol[:], io["lncol"], sem="C")
    cC = p.dma("sync", sonesk1[:], io["onesk1"], sem="C")
    loh = lcs = lg = lcn = lpr = lb1 = lsel = lid = l128 = cL = cK = cC

    of1 = p.op("vector", lambda e: e.tensor_mul(
        out=scnt[:], in0=scnt[:], in1=sprior[:]), waits=[lcn, lpr], inc="DV")
    of2 = p.op("vector", lambda e: e.reduce_sum(
        out=soffs[:], in_=scnt[:], axis=AX.X), waits=[of1], inc="DV")
    k1 = p.op("vector", lambda e: e.tensor_scalar(
        out=skept[:], in0=scs[:], scalar1=soffs[:], scalar2=2048.0,
        op0=ALU.add, op1=ALU.is_lt), waits=[of2, lcs], inc="DV")
    k3 = p.op("vector", lambda e: e.tensor_mul(out=skept[:], in0=skept[:], in1=soh[:]),
              waits=[k1, loh], inc="DV")
    gb = p.op("vector", lambda e: e.tensor_copy(out=sgateb[:], in_=sgate[:]),
              waits=[lg], inc="DV")
    wt_tok = None
    xin_tok = None
    dg_tok = None
    for n in range(4):
        dgs = []
        for j in range(4):
            tt = n * 4 + j
            dgs.append(p.op("vector", lambda e, tt=tt: e.tensor_scalar(
                out=sdiag[:, (tt % 4) * 128:(tt % 4 + 1) * 128], in0=sident[:],
                scalar1=sgate[:, tt:tt + 1], scalar2=None, op0=ALU.mult),
                waits=[lid, lg, dg_tok] if (tt % 4 == 0) else (), inc="DV"))
        bc8s = []
        bc128s = []
        for j in range(4):
            bc8s.append(p.op("tensor", lambda e, j=j: e.matmul(
                out=P1[0:8, j * 128:(j + 1) * 128], lhsT=sones128[:, 0:8],
                rhs=sdiag[:, j * 128:(j + 1) * 128], start=True, stop=True),
                waits=[dgs[-1], l128] if j == 0 else (), inc="PE"))
            bc128s.append(p.op("tensor", lambda e, j=j: e.matmul(
                out=P2[:, j * 128:(j + 1) * 128], lhsT=sones128[:],
                rhs=sdiag[:, j * 128:(j + 1) * 128], start=True, stop=True),
                inc="PE"))
        if n == 0 and io.get("dbgBC") is not None:
            bc_dump_tok = p.op("vector", lambda e: e.tensor_copy(
                out=sbcdump[:], in_=P1[0:8, 0:512]), waits=[bc8s[-1]], inc="DV")
        wt_tok = p.op("vector", lambda e, n=n: e.scalar_tensor_tensor(
            out=sWT[:, n * 512:(n + 1) * 512], in0=skept[:, n * 512:(n + 1) * 512],
            scalar=1.0, in1=P1[0:8, 0:512], op0=ALU.mult, op1=ALU.mult),
            waits=[k3, bc8s[-1]], inc="DV")
        for kt in range(8):
            xin_tok = p.op("vector", lambda e, kt=kt, n=n: e.tensor_mul(
                out=xinb[:, kt, n * 512:(n + 1) * 512],
                in0=sx[:, kt, n * 512:(n + 1) * 512], in1=P2[:, 0:512]),
                waits=[bc128s[-1], lx] if kt == 0 else (), inc="DV")
        dg_tok = xin_tok

    wdma = {}
    for ei in range(2):
        p.dma("gpsimd", w1b[:, ei % 2, :, :],
              io["W1"][ei].rearrange("(kt p) m -> p kt m", p=128))
        t_ = p.dma("gpsimd", w2b[:, ei % 2, :, :],
                   io["W2"][ei].rearrange("(kt p) m -> p kt m", p=128))
        wdma[ei] = (t_, t_)
    wdma[0] = wdma[1]
    acc_tok = None
    for ei in range(8):
        bi = ei % 2
        for n in range(4):
            h_toks = []
            for m in range(8):
                pp = (P0, P1)[m % 2]
                tok = None
                for kt in range(8):
                    tok = p.op("tensor", lambda e, kt=kt, m=m, n=n, bi=bi, pp=pp: e.matmul(
                        out=pp[:], lhsT=w1b[:, bi, kt, m * 128:(m + 1) * 128],
                        rhs=xinb[:, kt, n * 512:(n + 1) * 512],
                        start=(kt == 0), stop=(kt == 7)),
                        waits=([wdma[ei][0], wdma[ei][1], xin_tok, acc_tok]
                               + ([h_toks[m - 2]] if m >= 2 else [])) if kt == 0 else (),
                        inc="PE" if kt == 7 else None)
                ht = p.op("vector", lambda e, m=m, ei=ei, pp=pp: e.tensor_scalar(
                    out=hcb[:, m, :], in0=pp[:], scalar1=sb1[:, m, ei:ei + 1],
                    scalar2=0.0, op0=ALU.add, op1=ALU.max),
                    waits=[tok, lb1], inc="DV")
                h_toks.append(ht)
            wb1 = p.op("tensor", lambda e, ei=ei, n=n: e.matmul(
                out=P2[:, 0:512], lhsT=ssel[:, ei * 128:(ei + 1) * 128],
                rhs=sWT[:, n * 512:(n + 1) * 512], start=True, stop=True),
                waits=[lsel, wt_tok, h_toks[-1]], inc="PE")
            hh_tok = None
            for m in range(8):
                hh_tok = p.op("vector", lambda e, m=m: e.tensor_mul(
                    out=hcb[:, m, :], in0=hcb[:, m, :], in1=P2[:, 0:512]),
                    waits=[wb1] if m == 0 else (), inc="DV")
            accs = []
            for m in range(8):
                pp = (P0, P1)[m % 2]
                tok = None
                w2w = [hh_tok] + ([accs[m - 2]] if m >= 2 else [])
                if ei == 0:
                    p.op("tensor", lambda e, m=m, n=n, pp=pp: e.matmul(
                        out=pp[:], lhsT=sb2[:, m * 128:(m + 1) * 128],
                        rhs=sWT[:, n * 512:(n + 1) * 512],
                        start=True, stop=False),
                        waits=w2w + ([lb2] if m < 2 else []))
                for kt in range(8):
                    tok = p.op("tensor", lambda e, kt=kt, m=m, n=n, bi=bi, pp=pp: e.matmul(
                        out=pp[:], lhsT=w2b[:, bi, kt, m * 128:(m + 1) * 128],
                        rhs=hcb[:, kt, :],
                        start=(kt == 0 and ei != 0), stop=(kt == 7)),
                        waits=(w2w if (kt == 0 and ei != 0) else ()),
                        inc="PE" if kt == 7 else None)
                acc_tok = p.op("vector", lambda e, m=m, n=n, pp=pp: e.tensor_add(
                    out=sx[:, m, n * 512:(n + 1) * 512],
                    in0=sx[:, m, n * 512:(n + 1) * 512], in1=pp[:]),
                    waits=[tok], inc="DV")
                accs.append(acc_tok)
        nxt = ei + 2
        if nxt < 8:
            wdma[nxt] = (p.dma("gpsimd", w1b[:, nxt % 2, :, :],
                               io["W1"][nxt].rearrange("(kt p) m -> p kt m", p=128),
                               waits=[p.last("PE")]),
                         p.dma("gpsimd", w2b[:, nxt % 2, :, :],
                               io["W2"][nxt].rearrange("(kt p) m -> p kt m", p=128)))

    if io.get("dbgDiag") is not None:
        p.dma("sync", io["dbgDiag"], sdiag[:], waits=[p.last("DV")], sem="ST")
        p.dma("sync", io["dbgBC"], sbcdump[:], waits=[bc_dump_tok], sem="ST")
    if io.get("dbgWT") is not None:
        p.dma("sync", io["dbgWT"], sWT[:], waits=[wt_tok], sem="ST")
        p.dma("sync", io["dbgKept"], skept[:], waits=[wt_tok], sem="ST")
        p.dma("sync", io["dbgOffs"], soffs[:], waits=[wt_tok], sem="ST")
        p.dma("sync", io["dbgXin"], xinb[:], waits=[xin_tok], sem="ST")
    _xf = xinb[:, :, :].rearrange("p a t -> p (a t)")
    _wf = w1b[:, :, :, :].rearrange("p a b t -> p (a b t)")
    lnctx = dict(lncol=slncol, onesk1=sonesk1,
                 hb=lambda kt: _xf[:, kt * 2048:(kt + 1) * 2048],
                 hsq=lambda kt: _wf[:, kt * 2048:(kt + 1) * 2048],
                 srows=srows, PA=P2, PB=P3)
    apl = emit_ln(p, lnctx, sx, acc_tok)
    p.dma("sync", io["outT"].rearrange("(kt p) t -> p kt t", p=128), sx[:],
          waits=[apl], sem="ST")
    p.op("sync", lambda e: e.nop(), waits=[p.last("ST")])


def build_l4(debug=False):
    nc = bass.Bass("TRN2", target_bir_lowering=False, debug=False)
    io = {}
    io["xT"] = nc.dram_tensor("xT", [D, TOK], F32, kind="ExternalInput").ap()
    io["ohT"] = nc.dram_tensor("ohT", [8, TOK], BF16, kind="ExternalInput").ap()
    io["csumT"] = nc.dram_tensor("csumT", [8, TOK], F32, kind="ExternalInput").ap()
    io["gateT"] = nc.dram_tensor("gateT", [128, 16], F32, kind="ExternalInput").ap()
    io["cntall"] = nc.dram_tensor("cntall", [8, 8], F32, kind="ExternalInput").ap()
    io["prior"] = nc.dram_tensor("prior", [8, 8], F32, kind="ExternalInput").ap()
    io["W1"] = [nc.dram_tensor(f"W1_{e}", [D, D], F32, kind="ExternalInput").ap()
                for e in range(8)]
    io["W2"] = [nc.dram_tensor(f"W2_{e}", [D, D], F32, kind="ExternalInput").ap()
                for e in range(8)]
    io["b1T"] = nc.dram_tensor("b1T", [D, 8], F32, kind="ExternalInput").ap()
    io["b2"] = nc.dram_tensor("b2", [8, D], F32, kind="ExternalInput").ap()
    io["sel8"] = nc.dram_tensor("sel8", [8, D], BF16, kind="ExternalInput").ap()
    io["ident"] = nc.dram_tensor("ident", [128, 128], BF16, kind="ExternalInput").ap()
    io["ones128"] = nc.dram_tensor("ones128", [128, 128], BF16, kind="ExternalInput").ap()
    io["lncol"] = nc.dram_tensor("lncol", [128, 1], BF16, kind="ExternalInput").ap()
    io["onesk1"] = nc.dram_tensor("onesk1", [1, 128], F32, kind="ExternalInput").ap()
    io["outT"] = nc.dram_tensor("outT", [D, TOK], F32, kind="ExternalOutput").ap()
    if debug:
        io["dbgWT"] = nc.dram_tensor("dbgWT", [8, TOK], BF16, kind="ExternalOutput").ap()
        io["dbgKept"] = nc.dram_tensor("dbgKept", [8, TOK], BF16, kind="ExternalOutput").ap()
        io["dbgOffs"] = nc.dram_tensor("dbgOffs", [8, 1], F32, kind="ExternalOutput").ap()
        io["dbgDiag"] = nc.dram_tensor("dbgDiag", [128, 512], BF16, kind="ExternalOutput").ap()
        io["dbgBC"] = nc.dram_tensor("dbgBC", [8, 512], F32, kind="ExternalOutput").ap()
        io["dbgXin"] = nc.dram_tensor("dbgXin", [128, 8, TOK], BF16, kind="ExternalOutput").ap()
    stack = ExitStack()
    with stack:
        p = Prog(nc, stack)
        emit_l4(p, nc, stack, io)
        p.build()
    return nc





def build_fused():
    nc = bass.Bass("TRN2", target_bir_lowering=False, debug=False)
    ext = {}
    def ein(name, shape, dtype):
        ext[name] = nc.dram_tensor(name, shape, dtype, kind="ExternalInput").ap()
        return ext[name]

    xT = ein("xT", [D, TOK], F32)
    for w in ("wq", "wk", "wv", "wo"):
        ein(w, [D, D], F32)
    ein("maskA", [128, 512], BF16)
    ein("maskB", [128, 1024], BF16)
    ein("ident", [128, 128], BF16)
    ein("lncol", [128, 1], BF16)
    ein("onesk1", [1, 128], F32)
    ein("wgT", [D, 3072], I32)
    ein("wuT", [D, 3072], I32)
    ein("wdT", [3072, D], I32)
    ein("w1T", [D, 256], F32)
    ein("w2Tp", [256, 6144], F32)
    ein("cosT", [128, 4, TOK], BF16)
    ein("sinT", [128, 4, TOK], BF16)
    ein("blkd", [128, 16], BF16)
    ein("exp16", [16, 128], F32)
    ein("cw1p", [2048, 256], F32)
    ein("cw2T", [256, 2048], F32)
    ein("Wr", [D, 8], F32)
    ein("br", [1, 8], F32)
    ein("carrymask", [128, 8], F32)
    ein("prior", [8, 8], F32)
    ein("W1own", [D, D], F32)
    ein("W2own", [D, D], F32)
    ein("b1own", [128, 8], F32)
    ein("b2own", [128, 8], F32)
    ein("e8col", [8, 1], F32)
    ein("ones81", [8, 1], F32)
    ein("ones128", [128, 128], BF16)
    outT = nc.dram_tensor("outT", [D, TOK], F32, kind="ExternalOutput").ap()

    x1T = nc.dram_tensor("x1T", [D, TOK], F32, kind="Internal").ap()
    x2T = nc.dram_tensor("x2T", [D, TOK], F32, kind="Internal").ap()
    x3T = nc.dram_tensor("x3T", [D, TOK], F32, kind="Internal").ap()
    ypT = nc.dram_tensor("ypTi", [128, 16, TOK], BF16, kind="Internal").ap()
    rqT = nc.dram_tensor("rqTi", [128, 16, TOK], BF16, kind="Internal").ap()
    TcT = nc.dram_tensor("TcTi", [128, 16], F32, kind="Internal").ap()
    TcAll = nc.dram_tensor("TcAlli", [8 * 128, 16], F32, kind="Internal", addr_space="Shared").ap()
    ohT = nc.dram_tensor("ohTi", [8, TOK], BF16, kind="Internal").ap()
    csumT = nc.dram_tensor("csumTi", [8, TOK], F32, kind="Internal").ap()
    cntT = nc.dram_tensor("cntTi", [8, 1], F32, kind="Internal").ap()
    cntAll = nc.dram_tensor("cntAlli", [64, 1], F32, kind="Internal", addr_space="Shared").ap()
    gateT = nc.dram_tensor("gateTi", [128, 16], F32, kind="Internal").ap()

    RG = [[0, 1, 2, 3, 4, 5, 6, 7]]
    dram4 = l4r_drams(nc)

    with ExitStack() as outer:
        p = Prog(nc, outer)

        with ExitStack() as ph:
            io = dict(xT=xT, wq=ext["wq"], wk=ext["wk"], wv=ext["wv"],
                      wo=ext["wo"], maskA=ext["maskA"], maskB=ext["maskB"],
                      ident=ext["ident"], lncol=ext["lncol"],
                      onesk1=ext["onesk1"], outT=x1T)
            emit_l4r_zero(p, nc, ph, dram4)
            emit_l0attn(p, nc, ph, io)
        p.barrier()

        with ExitStack() as ph:
            io = dict(xT=x1T, wgT=ext["wgT"], wuT=ext["wuT"], wdT=ext["wdT"],
                      lncol=ext["lncol"], onesk1=ext["onesk1"], outT=x2T)
            emit_l0ffn(p, nc, ph, io)
        p.barrier()

        with ExitStack() as ph:
            io = dict(xT=x2T, w1T=ext["w1T"], w2Tp=ext["w2Tp"],
                      cosT=ext["cosT"], sinT=ext["sinT"], blkd=ext["blkd"],
                      exp16=ext["exp16"], lncol=ext["lncol"],
                      onesk1=ext["onesk1"], ypT=ypT, rqT=rqT, TcT=TcT)
            emit_l1a(p, nc, ph, io)
        cc1 = p.op("gpsimd", lambda e: e.collective_compute(
            "AllGather", ALU.bypass, replica_groups=RG,
            ins=[TcT.opt()], outs=[TcAll.opt()]),
            waits=[p.last("ST")], inc="CC")
        p.barrier()

        with ExitStack() as ph:
            io = dict(xT=x2T, ypT=ypT, rqT=rqT, TcAll=TcAll,
                      carrymask=ext["carrymask"], cw1p=ext["cw1p"],
                      cw2T=ext["cw2T"], Wr=ext["Wr"], br=ext["br"],
                      ident=ext["ident"], lncol=ext["lncol"],
                      onesk1=ext["onesk1"], outT=x3T, ohT=ohT,
                      csumT=csumT, counts=cntT, gateT=gateT)
            emit_l1c(p, nc, ph, io)
        cc2 = p.op("gpsimd", lambda e: e.collective_compute(
            "AllGather", ALU.bypass, replica_groups=RG,
            ins=[cntT.opt()], outs=[cntAll.opt()]),
            waits=[p.last("ST")], inc="CC")
        p.barrier()

        with ExitStack() as ph:
            io = dict(xT=x3T, ohT=ohT, csumT=csumT, gateT=gateT,
                      cntAllCol=cntAll, prior=ext["prior"],
                      W1own=ext["W1own"], W2own=ext["W2own"],
                      b1own=ext["b1own"], b2own=ext["b2own"],
                      e8col=ext["e8col"], ones81=ext["ones81"],
                      ident=ext["ident"], ones128=ext["ones128"],
                      lncol=ext["lncol"], onesk1=ext["onesk1"], outT=outT)
            emit_l4r(p, nc, ph, io, dram4, emit_collectives=True)
        p.build()
    return nc




from concourse.bass_core_rust import IndirectOffsetOnAxis

RG8 = [[0, 1, 2, 3, 4, 5, 6, 7]]

def emit_l4r_zero(p, nc, stack, dram):
    szb = sb(nc, stack, "zzb", [128, 2048], BF16)
    z0 = p.op("vector", lambda e: e.memset(szb[:], 0.0), inc="DV")
    xdisp = dram["xdisp"]
    for k in range(64):
        p.op("scalar", lambda e, k=k: e.dma_start(
            out=xdisp[k * 256:(k + 1) * 256, :].rearrange(
                "(a p) d -> p a d", p=128),
            in_=szb[:].rearrange("p (a d) -> p a d", a=2)),
            waits=[z0, ("Lx", 16)] if k == 0 else (), inc="Z", inc_k=16)


def emit_l4r(p, nc, stack, io, dram, emit_collectives=True):
    U32 = mybir.dt.uint32
    TOKL = 2048

    sx = sb(nc, stack, "r4x", [128, 8, TOKL], F32)
    xinb = sb(nc, stack, "r4xin", [128, 8, TOKL], BF16)
    hcb = sb(nc, stack, "r4hc", [128, 8, TOKL], BF16)
    stg = [sb(nc, stack, f"r4stg{i}", [128, 1024], BF16) for i in range(2)]
    sWT = sb(nc, stack, "r4WT", [8, TOKL], BF16)
    sve = sb(nc, stack, "r4sve", [128, 16, 3], F32)
    sA2 = sb(nc, stack, "r4A2", [128, 16], F32)
    sB2 = sb(nc, stack, "r4B2", [128, 16], F32)
    sD2 = sb(nc, stack, "r4D2", [128, 16], F32)
    sG2 = sb(nc, stack, "r4G2", [128, 16], F32)
    sslotu = sb(nc, stack, "r4slu", [128, 16], U32)
    sgslu = sb(nc, stack, "r4gsu", [128, 16], U32)
    sgate = sb(nc, stack, "r4gate", [128, 16], F32)
    sidentb = sb(nc, stack, "r4id", [128, 128], BF16)
    sones128 = sb(nc, stack, "r4ones", [128, 128], BF16)
    slncol = sb(nc, stack, "r4lnc", [128, 1], BF16)
    sonesk1 = sb(nc, stack, "r4k1", [1, 128], F32)
    scnt = sb(nc, stack, "r4cnt", [8, 8], F32)
    sprior = sb(nc, stack, "r4pr", [8, 8], F32)
    soffs = sb(nc, stack, "r4of", [8, 1], F32)
    se8 = sb(nc, stack, "r4e8", [8, 1], F32)
    s81 = sb(nc, stack, "r4o81", [8, 1], F32)
    sb1o = sb(nc, stack, "r4b1", [128, 8], F32)
    sb2o = sb(nc, stack, "r4b2", [128, 8], F32)

    P0 = ps(nc, stack, "r4P0", [128, 512])
    P1 = ps(nc, stack, "r4P1", [128, 512])
    P2 = ps(nc, stack, "r4P2", [128, 512])
    P3 = ps(nc, stack, "r4P3", [128, 512])
    PS4 = ps(nc, stack, "r4PS", [128, 2048])

    xdisp, xown, eoc, eoAll, svd = (
        dram["xdisp"], dram["xown"], dram["eoc"], dram["eoAll"], dram["svd"])
    z_total = p.last("Z")

    lx = p.dma("sync", sx[:], io["xT"].rearrange("(kt p) t -> p kt t", p=128), sem="Lx")
    p.dma("sync", sgate[:], io["gateT"], sem="C")
    if io.get("cntall") is not None:
        p.dma("sync", scnt[:], io["cntall"], sem="C")
    else:
        for r_ in range(8):
            p.dma("sync", scnt[:, r_:r_ + 1],
                  io["cntAllCol"][r_ * 8:(r_ + 1) * 8, 0:1], sem="C")
    p.dma("sync", sprior[:], io["prior"], sem="C")
    p.dma("sync", se8[:], io["e8col"], sem="C")
    p.dma("sync", s81[:], io["ones81"], sem="C")
    p.dma("sync", sb1o[:], io["b1own"], sem="C")
    p.dma("sync", sb2o[:], io["b2own"], sem="C")
    p.dma("sync", sidentb[:], io["ident"], sem="C")
    p.dma("sync", sones128[:], io["ones128"], sem="C")
    p.dma("sync", slncol[:], io["lncol"], sem="C")
    p.dma("sync", sonesk1[:], io["onesk1"], sem="C")

    w1b = None

    rs_ = ExitStack()
    soh = sb(nc, rs_, "r4oh", [8, TOKL], BF16)
    sohf = sb(nc, rs_, "r4ohf", [8, TOKL], F32)
    scs = sb(nc, rs_, "r4cs", [8, TOKL], F32)
    skept = sb(nc, rs_, "r4kp", [8, TOKL], F32)
    sDr = sb(nc, rs_, "r4Dr", [8, TOKL], F32)
    su32 = sb(nc, rs_, "r4u32", [8, TOKL], U32)
    sv32 = sb(nc, rs_, "r4v32", [8, TOKL], U32)
    srABD = sb(nc, rs_, "r4ABD", [1, 1536], F32)
    sdiag = sb(nc, rs_, "r4diag", [128, 512], BF16)

    p.dma("sync", soh[:], io["ohT"], sem="C")
    cC = p.dma("sync", scs[:], io["csumT"], sem="C")

    of1 = p.op("vector", lambda e: e.tensor_mul(
        out=scnt[:], in0=scnt[:], in1=sprior[:]), waits=[cC], inc="DV")
    of2 = p.op("vector", lambda e: e.reduce_sum(
        out=soffs[:], in_=scnt[:], axis=AX.X), waits=[of1], inc="DV")
    ohc = p.op("vector", lambda e: e.tensor_copy(out=sohf[:], in_=soh[:]),
               waits=[cC], inc="DV")
    k0 = p.op("vector", lambda e: e.tensor_scalar(
        out=scs[:], in0=scs[:], scalar1=soffs[:], scalar2=None, op0=ALU.add),
        waits=[of2], inc="DV")
    k1 = p.op("vector", lambda e: e.tensor_scalar(
        out=skept[:], in0=scs[:], scalar1=2048.0, scalar2=None, op0=ALU.is_lt),
        waits=[k0], inc="DV")
    k2 = p.op("vector", lambda e: e.tensor_mul(
        out=skept[:], in0=skept[:], in1=sohf[:]), waits=[k1, ohc], inc="DV")
    d0 = p.op("vector", lambda e: e.tensor_sub(
        out=sDr[:], in0=sohf[:], in1=skept[:]), waits=[k2], inc="DV")

    wt_tok = None
    xin_toks = {}
    dg_tok = None
    for n in range(4):
        dgs = None
        for j in range(4):
            tt = n * 4 + j
            dgs = p.op("vector", lambda e, tt=tt: e.tensor_scalar(
                out=sdiag[:, (tt % 4) * 128:(tt % 4 + 1) * 128], in0=sidentb[:],
                scalar1=sgate[:, tt:tt + 1], scalar2=None, op0=ALU.mult),
                waits=[cC, dg_tok] if (tt % 4 == 0) else (), inc="DV")
        bc8s = []
        bc128s = []
        for j in range(4):
            bc8s.append(p.op("tensor", lambda e, j=j: e.matmul(
                out=P1[0:8, j * 128:(j + 1) * 128], lhsT=sones128[:, 0:8],
                rhs=sdiag[:, j * 128:(j + 1) * 128], start=True, stop=True),
                waits=[dgs] if j == 0 else (), inc="PE"))
            bc128s.append(p.op("tensor", lambda e, j=j: e.matmul(
                out=P2[:, j * 128:(j + 1) * 128], lhsT=sones128[:],
                rhs=sdiag[:, j * 128:(j + 1) * 128], start=True, stop=True),
                inc="PE"))
        wt_tok = p.op("vector", lambda e, n=n: e.scalar_tensor_tensor(
            out=sWT[:, n * 512:(n + 1) * 512], in0=skept[:, n * 512:(n + 1) * 512],
            scalar=1.0, in1=P1[0:8, 0:512], op0=ALU.mult, op1=ALU.mult),
            waits=[k2, bc8s[-1]], inc="DV")
        tkx = None
        for kt in range(8):
            tkx = p.op("vector", lambda e, kt=kt, n=n: e.tensor_mul(
                out=xinb[:, kt, n * 512:(n + 1) * 512],
                in0=sx[:, kt, n * 512:(n + 1) * 512], in1=P2[:, 0:512]),
                waits=[bc128s[-1], lx] if kt == 0 else (), inc="DV")
        xin_toks[n] = tkx
        dg_tok = tkx

    c0 = p.op("vector", lambda e: e.tensor_scalar(
        out=scs[:], in0=scs[:], scalar1=2047.0, scalar2=None, op0=ALU.min),
        waits=[wt_tok, k2, d0], inc="DV")
    c1 = p.op("vector", lambda e: e.tensor_copy(out=su32[:], in_=scs[:]),
              waits=[c0], inc="DV")
    c2 = p.op("vector", lambda e: e.tensor_scalar(
        out=sv32[:], in0=su32[:], scalar1=255, scalar2=None,
        op0=ALU.bitwise_and), waits=[c1], inc="DV")
    c3 = p.op("vector", lambda e: e.tensor_copy(out=scs[:], in_=sv32[:]),
              waits=[c2], inc="DV")
    c4 = p.op("vector", lambda e: e.tensor_scalar(
        out=sv32[:], in0=su32[:], scalar1=8, scalar2=None,
        op0=ALU.logical_shift_right), waits=[c3], inc="DV")
    c5 = p.op("vector", lambda e: e.tensor_copy(out=skept[:], in_=sv32[:]),
              waits=[c4], inc="DV")
    c6 = p.op("vector", lambda e: e.tensor_scalar(
        out=skept[:], in0=skept[:], scalar1=se8[:], scalar2=None, op0=ALU.add),
        waits=[c5], inc="DV")
    c7 = p.op("vector", lambda e: e.tensor_mul(
        out=skept[:], in0=skept[:], in1=sohf[:]), waits=[c6], inc="DV")
    c8 = p.op("vector", lambda e: e.tensor_mul(
        out=scs[:], in0=scs[:], in1=sohf[:]), waits=[c7], inc="DV")

    chain = c8
    for cch in range(4):
        sl = slice(cch * 512, (cch + 1) * 512)
        p.op("tensor", lambda e, sl=sl: e.matmul(
            out=PS4[0:1, sl], lhsT=s81[:], rhs=skept[:, sl],
            start=True, stop=True), waits=[chain, cC], inc="PE")
        p.op("tensor", lambda e, sl=sl: e.matmul(
            out=PS4[32:33, sl], lhsT=s81[:], rhs=scs[:, sl],
            start=True, stop=True), inc="PE")
        mmE = p.op("tensor", lambda e, sl=sl: e.matmul(
            out=PS4[64:65, sl], lhsT=s81[:], rhs=sDr[:, sl],
            start=True, stop=True), inc="PE")
        vA = srABD.rearrange("a (t three) -> a t three", three=3)
        e0 = p.op("vector", lambda e, sl=sl, vA=vA: e.tensor_copy(
            out=vA[:, 0:512, 0:1],
            in_=PS4[0:1, sl].rearrange("a (t o) -> a t o", o=1)),
            waits=[mmE], inc="DV")
        e1 = p.op("vector", lambda e, sl=sl, vA=vA: e.tensor_copy(
            out=vA[:, 0:512, 1:2],
            in_=PS4[32:33, sl].rearrange("a (t o) -> a t o", o=1)),
            waits=[e0], inc="DV")
        e2 = p.op("vector", lambda e, sl=sl, vA=vA: e.tensor_copy(
            out=vA[:, 0:512, 2:3],
            in_=PS4[64:65, sl].rearrange("a (t o) -> a t o", o=1)),
            waits=[e1], inc="DV")
        chain = p.dma("sync", svd[cch * 512:(cch + 1) * 512, :].rearrange(
            "(a t) three -> a (t three)", a=1), srABD[:],
            waits=[e2], sem="S1")
    sv_done = chain

    lsv = p.dma("sync", sve[:], svd.rearrange("(a p) three -> p a three", p=128),
                waits=[sv_done], sem="S1")
    q0 = p.op("vector", lambda e: e.tensor_copy(
        out=sA2.rearrange("p (t o) -> p t o", o=1), in_=sve[:, :, 0:1]),
        waits=[lsv], inc="DV")
    q1 = p.op("vector", lambda e: e.tensor_copy(
        out=sB2.rearrange("p (t o) -> p t o", o=1), in_=sve[:, :, 1:2]),
        waits=[q0], inc="DV")
    q2 = p.op("vector", lambda e: e.tensor_copy(
        out=sD2.rearrange("p (t o) -> p t o", o=1), in_=sve[:, :, 2:3]),
        waits=[q1], inc="DV")
    q3 = p.op("vector", lambda e: e.tensor_scalar(
        out=sG2[:], in0=sA2[:], scalar1=256.0, scalar2=None, op0=ALU.mult),
        waits=[q2], inc="DV")
    q4 = p.op("vector", lambda e: e.tensor_add(
        out=sG2[:], in0=sG2[:], in1=sB2[:]), waits=[q3], inc="DV")
    cv_g = p.op("vector", lambda e: e.tensor_copy(out=sgslu[:], in_=sG2[:]),
                waits=[q4], inc="DV")
    q6 = p.op("vector", lambda e: e.tensor_scalar(
        out=sD2[:], in0=sD2[:], scalar1=16384.0, scalar2=None, op0=ALU.mult),
        waits=[cv_g], inc="DV")
    q7 = p.op("vector", lambda e: e.tensor_add(
        out=sG2[:], in0=sG2[:], in1=sD2[:]), waits=[q6], inc="DV")
    cv_sl = p.op("vector", lambda e: e.tensor_copy(out=sslotu[:], in_=sG2[:]),
                 waits=[q7], inc="DV")

    rs_.close()

    w1b = sb(nc, stack, "r4w1", [128, 8, 1024], BF16)
    w2b = sb(nc, stack, "r4w2", [128, 8, 1024], BF16)
    srows = sb(nc, stack, "r4srows", [1, 2048], F32)
    lw1 = p.dma("gpsimd", w1b[:], io["W1own"].rearrange("(kt p) m -> p kt m", p=128),
                waits=[cv_sl], sem="L")
    lw2 = p.dma("gpsimd", w2b[:], io["W2own"].rearrange("(kt p) m -> p kt m", p=128),
                sem="L")

    sc_toks = {}
    for t in range(16):
        si = t % 2
        xb_tok = None
        for dt in range(8):
            w = ()
            if dt == 0:
                w = [xin_toks[t // 4]] + ([sc_toks[t - 2]] if t >= 2 else [])
            xb_tok = p.op("sync", lambda e, t=t, dt=dt, si=si: e.dma_start(
                out=stg[si][:, dt * 128:(dt + 1) * 128],
                in_=xinb[:, dt, t * 128:(t + 1) * 128], transpose=True),
                waits=w, inc=("XA" if t % 2 == 0 else "XB"), inc_k=16)
        sc_toks[t] = p.op("gpsimd", lambda e, t=t, si=si: e.indirect_dma_start(
            out=xdisp, out_offset=IndirectOffsetOnAxis(sslotu[:, t:t + 1], 0),
            in_=stg[si][:], in_offset=None,
            bounds_check=16383, oob_is_err=False),
            waits=[xb_tok, cv_sl, z_total],
            inc=("GA" if t % 2 == 0 else "GB"), inc_k=16)
    sc_done = [sc_toks[14], sc_toks[15]]

    if emit_collectives:
        cc_x = p.op("gpsimd", lambda e: e.collective_compute(
            "ReduceScatter", ALU.add, replica_groups=RG8,
            ins=[xdisp.opt()], outs=[xown.opt()]),
            waits=sc_done, inc="CC")
    else:
        cc_x = p.op("gpsimd", lambda e: e.dma_start(out=xown, in_=xdisp[0:2048, :]),
                    waits=sc_done, inc="CC", inc_k=1)

    ld_toks = {}
    xe_toks = {}
    for t in range(16):
        si = t % 2
        w = [cc_x] if t < 2 else [xe_toks[t - 2]]
        ld_toks[t] = p.op("sync", lambda e, t=t, si=si: e.dma_start(
            out=stg[si][:], in_=xown[t * 128:(t + 1) * 128, :]),
            waits=w, inc=("SA" if t % 2 == 0 else "SB"), inc_k=16)
        xe_last = None
        for dt in range(8):
            xe_last = p.op("sync", lambda e, t=t, dt=dt, si=si: e.dma_start(
                out=xinb[:, dt, t * 128:(t + 1) * 128],
                in_=stg[si][:, dt * 128:(dt + 1) * 128], transpose=True),
                waits=[ld_toks[t]] if dt == 0 else (),
                inc=("XA" if t % 2 == 0 else "XB"), inc_k=16)
        xe_toks[t] = xe_last
    xe_done = [xe_toks[14], xe_toks[15]]

    acc = {}
    h_toks = {}
    cC2 = p.last("C")
    for n in range(4):
        for m in range(8):
            pp = (P0, P1)[m % 2]
            w0 = [lw1, lw2, cC2, xe_toks[4 * n + 2], xe_toks[4 * n + 3]]
            if m >= 2:
                w0.append(h_toks[(n, m - 2)])
            elif n > 0:
                w0.append(acc[(n - 1, 6 + m)])
            tok = None
            for kt in range(8):
                tok = p.op("tensor", lambda e, kt=kt, m=m, n=n, pp=pp: e.matmul(
                    out=pp[:], lhsT=w1b[:, kt, m * 128:(m + 1) * 128],
                    rhs=xinb[:, kt, n * 512:(n + 1) * 512],
                    start=(kt == 0), stop=(kt == 7)),
                    waits=w0 if kt == 0 else (), inc="PE" if kt == 7 else None)
            h_toks[(n, m)] = p.op("vector", lambda e, m=m, pp=pp, n=n: e.tensor_scalar(
                out=hcb[:, m, n * 512:(n + 1) * 512], in0=pp[:],
                scalar1=sb1o[:, m:m + 1], scalar2=0.0, op0=ALU.add, op1=ALU.max),
                waits=[tok], inc="DV")
        for m in range(8):
            pp = (P0, P1)[m % 2]
            w0 = [h_toks[(n, 7)]] + ([acc[(n, m - 2)]] if m >= 2 else [])
            tok = None
            for kt in range(8):
                tok = p.op("tensor", lambda e, kt=kt, m=m, n=n, pp=pp: e.matmul(
                    out=pp[:], lhsT=w2b[:, kt, m * 128:(m + 1) * 128],
                    rhs=hcb[:, kt, n * 512:(n + 1) * 512],
                    start=(kt == 0), stop=(kt == 7)),
                    waits=w0 if kt == 0 else (), inc="PE" if kt == 7 else None)
            acc[(n, m)] = p.op("vector", lambda e, m=m, n=n, pp=pp: e.tensor_scalar(
                out=xinb[:, m, n * 512:(n + 1) * 512], in0=pp[:],
                scalar1=sb2o[:, m:m + 1], scalar2=None, op0=ALU.add),
                waits=[tok], inc="DV")
    eo_done = acc[(3, 7)]

    st2 = {}
    for t in range(16):
        si = t % 2
        xb_tok = None
        for dt in range(8):
            w = ()
            if dt == 0:
                w = [acc[(t // 4, 7)]] + ([st2[t - 2]] if t >= 2 else [])
            xb_tok = p.op("sync", lambda e, t=t, dt=dt, si=si: e.dma_start(
                out=stg[si][:, dt * 128:(dt + 1) * 128],
                in_=xinb[:, dt, t * 128:(t + 1) * 128], transpose=True),
                waits=w, inc=("XA" if t % 2 == 0 else "XB"), inc_k=16)
        st2[t] = p.op("sync", lambda e, t=t, si=si: e.dma_start(
            out=eoc[t * 128:(t + 1) * 128, :], in_=stg[si][:]),
            waits=[xb_tok], inc=("SA" if t % 2 == 0 else "SB"), inc_k=16)
    eo_stored = [st2[14], st2[15]]

    if emit_collectives:
        cc_e = p.op("gpsimd", lambda e: e.collective_compute(
            "AllGather", ALU.bypass, replica_groups=RG8,
            ins=[eoc.opt()], outs=[eoAll.opt()]),
            waits=eo_stored, inc="CC")
    else:
        cc_e = p.op("gpsimd", lambda e: e.dma_start(out=eoAll[0:2048, :], in_=eoc),
                    waits=eo_stored, inc="CC", inc_k=1)

    g2_toks = {}
    eg_toks = {}
    for t in range(16):
        si = t % 2
        gw = [cv_g, cc_e] if t < 2 else [eg_toks[t - 2]]
        g2_toks[t] = p.op("gpsimd", lambda e, t=t, si=si: e.indirect_dma_start(
            out=stg[si][:], out_offset=None,
            in_=eoAll, in_offset=IndirectOffsetOnAxis(sgslu[:, t:t + 1], 0),
            bounds_check=16383, oob_is_err=False),
            waits=gw, inc=("GA" if t % 2 == 0 else "GB"), inc_k=16)
        eg_last = None
        for dt in range(8):
            eg_last = p.op("sync", lambda e, t=t, dt=dt, si=si: e.dma_start(
                out=hcb[:, dt, t * 128:(t + 1) * 128],
                in_=stg[si][:, dt * 128:(dt + 1) * 128], transpose=True),
                waits=[g2_toks[t]] if dt == 0 else (),
                inc=("XA" if t % 2 == 0 else "XB"), inc_k=16)
        eg_toks[t] = eg_last
    eg_done = [eg_toks[14], eg_toks[15]]

    acc2 = None
    for n in range(4):
        wb = p.op("tensor", lambda e, n=n: e.matmul(
            out=P2[:, 0:512], lhsT=sones128[0:8, :],
            rhs=sWT[:, n * 512:(n + 1) * 512], start=True, stop=True),
            waits=[wt_tok, eg_toks[4 * n + 2], eg_toks[4 * n + 3]]
            + ([acc2] if acc2 else []), inc="PE")
        for kt in range(8):
            tm1 = p.op("vector", lambda e, kt=kt, n=n: e.tensor_mul(
                out=xinb[:, kt, n * 512:(n + 1) * 512],
                in0=hcb[:, kt, n * 512:(n + 1) * 512], in1=P2[:, 0:512]),
                waits=[wb] if kt == 0 else (), inc="DV")
            acc2 = p.op("vector", lambda e, kt=kt, n=n: e.tensor_add(
                out=sx[:, kt, n * 512:(n + 1) * 512],
                in0=sx[:, kt, n * 512:(n + 1) * 512],
                in1=xinb[:, kt, n * 512:(n + 1) * 512]),
                waits=[tm1], inc="DV")

    lnctx = dict(lncol=slncol, onesk1=sonesk1,
                 hb=lambda kt: xinb[:, kt, :], hsq=lambda kt: hcb[:, kt, :],
                 srows=srows, PA=P2, PB=P3)
    apl = emit_ln(p, lnctx, sx, acc2)
    p.dma("sync", io["outT"].rearrange("(kt p) t -> p kt t", p=128), sx[:],
          waits=[apl], sem="ST")
    p.op("sync", lambda e: e.nop(), waits=[p.last("ST")])


def l4r_drams(nc):
    return dict(
        xdisp=nc.dram_tensor("r_xdisp", [16384, 1024], BF16, kind="Internal").ap(),
        xown=nc.dram_tensor("r_xown", [2048, 1024], BF16, kind="Internal").ap(),
        eoc=nc.dram_tensor("r_eoc", [2048, 1024], BF16, kind="Internal").ap(),
        eoAll=nc.dram_tensor("r_eoAll", [16384, 1024], BF16, kind="Internal", addr_space="Shared").ap(),
        svd=nc.dram_tensor("r_svd", [2048, 3], F32, kind="Internal").ap(),
    )


def l4r_io(nc):
    io = {}
    io["xT"] = nc.dram_tensor("xT", [D, 2048], F32, kind="ExternalInput").ap()
    io["ohT"] = nc.dram_tensor("ohT", [8, 2048], BF16, kind="ExternalInput").ap()
    io["csumT"] = nc.dram_tensor("csumT", [8, 2048], F32, kind="ExternalInput").ap()
    io["gateT"] = nc.dram_tensor("gateT", [128, 16], F32, kind="ExternalInput").ap()
    io["cntall"] = nc.dram_tensor("cntall", [8, 8], F32, kind="ExternalInput").ap()
    io["prior"] = nc.dram_tensor("prior", [8, 8], F32, kind="ExternalInput").ap()
    io["e8col"] = nc.dram_tensor("e8col", [8, 1], F32, kind="ExternalInput").ap()
    io["ones81"] = nc.dram_tensor("ones81", [8, 1], F32, kind="ExternalInput").ap()
    io["b1own"] = nc.dram_tensor("b1own", [128, 8], F32, kind="ExternalInput").ap()
    io["b2own"] = nc.dram_tensor("b2own", [128, 8], F32, kind="ExternalInput").ap()
    io["W1own"] = nc.dram_tensor("W1own", [D, D], F32, kind="ExternalInput").ap()
    io["W2own"] = nc.dram_tensor("W2own", [D, D], F32, kind="ExternalInput").ap()
    io["ident"] = nc.dram_tensor("ident", [128, 128], BF16, kind="ExternalInput").ap()
    io["ones128"] = nc.dram_tensor("ones128", [128, 128], BF16, kind="ExternalInput").ap()
    io["lncol"] = nc.dram_tensor("lncol", [128, 1], BF16, kind="ExternalInput").ap()
    io["onesk1"] = nc.dram_tensor("onesk1", [1, 128], F32, kind="ExternalInput").ap()
    return io


def build_l4r():
    nc = bass.Bass("TRN2", target_bir_lowering=False, debug=False)
    io = l4r_io(nc)
    io["outT"] = nc.dram_tensor("outT", [D, 2048], F32, kind="ExternalOutput").ap()
    stack = ExitStack()
    with stack:
        p = Prog(nc, stack)
        dram = l4r_drams(nc)
        emit_l4r_zero(p, nc, stack, dram)
        emit_l4r(p, nc, stack, io, dram, emit_collectives=True)
        p.build()
    return nc

_BUILD_CACHE = {}


def _get(name, fn):
    if name not in _BUILD_CACHE:
        _BUILD_CACHE[name] = fn()
    return _BUILD_CACHE[name]


def _make_perm():
    pp = np.arange(128)
    perm = np.empty(6144, np.int64)
    for tau in range(3):
        for r in range(16):
            perm[tau * 2048 + r * 128 + pp] = (
                tau * 2048 + (pp % 16) * 128 + 8 * r + pp // 16)
    permy = np.empty(2048, np.int64)
    for r in range(16):
        permy[r * 128 + pp] = (pp % 16) * 128 + 8 * r + pp // 16
    return perm, permy


def _rope_tables(t0):
    lin = np.linspace(0.0, 1.0, 32, dtype=np.float32)
    fr = np.power(np.float32(1.0 / 1024.0), lin).astype(np.float32)
    t = (t0 + np.arange(2048)).astype(np.float32)
    p = np.arange(128)
    r = np.arange(4)
    d = 8 * r[None, :] + (p // 16)[:, None]
    theta = (t[None, None, :] * fr[d][:, :, None]).astype(np.float32).astype(np.float64)
    return (np.cos(theta).astype(np.float32).astype(BF),
            np.sin(theta).astype(np.float32).astype(BF))


def kernel_unfused(L0_attn_x, L0_attn_Wq, L0_attn_Wk, L0_attn_Wv, L0_attn_Wo,
           L0_ffn_w_gate, L0_ffn_w_up, L0_ffn_w_down,
           L1_attn_qkv_w1, L1_attn_qkv_w2, L1_attn_cproj_w1, L1_attn_cproj_w2,
           L1_ffn_Wr, L1_ffn_br, L1_ffn_W1, L1_ffn_b1, L1_ffn_W2, L1_ffn_b2):
    x = np.asarray(L0_attn_x)
    consts = common_consts()
    cores = list(range(NCORE))

    def xslices(arrT_list):
        return arrT_list

    def core_bt(c):
        return c // 4, (c % 4) * 2048

    nc = _get("l0attn", build_l0attn)
    in_maps = []
    for c in cores:
        b, t0 = core_bt(c)
        in_maps.append(dict(
            xT=np.ascontiguousarray(x[b, t0:t0 + TOK, :].T),
            wq=L0_attn_Wq, wk=L0_attn_Wk, wv=L0_attn_Wv, wo=L0_attn_Wo,
            **consts))
    res = run_bass_kernel_spmd(nc, in_maps, core_ids=cores)
    x1T = [np.asarray(res.results[c]["outT"]) for c in cores]

    nc = _get("l0ffn", build_l0ffn)
    wgT = np.ascontiguousarray(L0_ffn_w_gate.T)
    wuT = np.ascontiguousarray(L0_ffn_w_up.T)
    wdT = np.ascontiguousarray(L0_ffn_w_down.T)
    in_maps = [dict(xT=x1T[c], wgT=wgT, wuT=wuT, wdT=wdT,
                    lncol=consts["lncol"], onesk1=consts["onesk1"])
               for c in cores]
    res = run_bass_kernel_spmd(nc, in_maps, core_ids=cores)
    x2T = [np.asarray(res.results[c]["outT"]) for c in cores]

    nc = _get("l1a", build_l1a)
    perm, permy = _make_perm()
    w1T = np.ascontiguousarray(L1_attn_qkv_w1.T)
    w2Tp = np.ascontiguousarray(L1_attn_qkv_w2[perm].T)
    blkd = np.zeros((128, 16), dtype=BF)
    pp = np.arange(128)
    blkd[pp, pp % 16] = BF(1.0 / 128.0)
    exp16 = np.zeros((16, 128), np.float32)
    exp16[pp % 16, pp] = 1.0
    in_maps = []
    for c in cores:
        b, t0 = core_bt(c)
        cosT, sinT = _rope_tables(t0)
        in_maps.append(dict(
            xT=x2T[c], w1T=w1T, w2Tp=w2Tp, cosT=cosT, sinT=sinT,
            blkd=blkd, exp16=exp16,
            lncol=consts["lncol"], onesk1=consts["onesk1"]))
    res = run_bass_kernel_spmd(nc, in_maps, core_ids=cores)
    ypT = [np.asarray(res.results[c]["ypT"]) for c in cores]
    rqT = [np.asarray(res.results[c]["rqT"]) for c in cores]
    TcT = [np.asarray(res.results[c]["TcT"]) for c in cores]

    nc = _get("l1c", build_l1c)
    cw1p = np.ascontiguousarray(L1_attn_cproj_w1.T[permy])
    cw2T = np.ascontiguousarray(L1_attn_cproj_w2.T)
    TcAll = np.ascontiguousarray(np.concatenate(TcT, axis=0))
    in_maps = []
    for c in cores:
        b, t0 = core_bt(c)
        cmask = np.zeros((128, 8), np.float32)
        cmask[:, 4 * b:c] = 1.0
        in_maps.append(dict(
            xT=x2T[c], ypT=ypT[c], rqT=rqT[c], TcAll=TcAll, carrymask=cmask,
            cw1p=cw1p, cw2T=cw2T, Wr=L1_ffn_Wr,
            br=np.ascontiguousarray(L1_ffn_br.reshape(1, 8)),
            ident=consts["ident"], lncol=consts["lncol"],
            onesk1=consts["onesk1"]))
    res = run_bass_kernel_spmd(nc, in_maps, core_ids=cores)
    x3T = [np.asarray(res.results[c]["outT"]) for c in cores]
    ohT = [np.asarray(res.results[c]["ohT"]) for c in cores]
    csumT = [np.asarray(res.results[c]["csumT"]) for c in cores]
    cnt = [np.asarray(res.results[c]["counts"]) for c in cores]
    gateT = [np.asarray(res.results[c]["gateT"]) for c in cores]

    nc = _get("l4", build_l4)
    cnt_et = np.ascontiguousarray(np.concatenate(cnt, axis=1))
    sel8 = np.zeros((8, 1024), dtype=BF)
    for e in range(8):
        sel8[e, e * 128:(e + 1) * 128] = BF(1.0)
    b1T = np.ascontiguousarray(L1_ffn_b1.T)
    in_maps = []
    for c in cores:
        prior = np.zeros((8, 8), np.float32)
        prior[:, :c] = 1.0
        m = dict(xT=x3T[c], ohT=ohT[c], csumT=csumT[c], gateT=gateT[c],
                 cntall=cnt_et, prior=prior, b1T=b1T, b2=L1_ffn_b2,
                 sel8=sel8, ident=consts["ident"],
                 ones128=np.ones((128, 128), dtype=BF),
                 lncol=consts["lncol"], onesk1=consts["onesk1"])
        for e in range(8):
            m[f"W1_{e}"] = np.ascontiguousarray(L1_ffn_W1[e])
            m[f"W2_{e}"] = np.ascontiguousarray(L1_ffn_W2[e])
        in_maps.append(m)
    res = run_bass_kernel_spmd(nc, in_maps, core_ids=cores)

    out = np.empty((2, 8192, D), np.float32)
    for c in cores:
        b, t0 = core_bt(c)
        out[b, t0:t0 + TOK, :] = np.asarray(res.results[c]["outT"]).T
    return out



def kernel(L0_attn_x, L0_attn_Wq, L0_attn_Wk, L0_attn_Wv, L0_attn_Wo,
           L0_ffn_w_gate, L0_ffn_w_up, L0_ffn_w_down,
           L1_attn_qkv_w1, L1_attn_qkv_w2, L1_attn_cproj_w1, L1_attn_cproj_w2,
           L1_ffn_Wr, L1_ffn_br, L1_ffn_W1, L1_ffn_b1, L1_ffn_W2, L1_ffn_b2):
    x = np.asarray(L0_attn_x)
    consts = common_consts()
    cores = list(range(NCORE))
    perm, permy = _make_perm()
    shared = dict(
        wq=L0_attn_Wq, wk=L0_attn_Wk, wv=L0_attn_Wv, wo=L0_attn_Wo,
        wgT=np.ascontiguousarray(L0_ffn_w_gate.T),
        wuT=np.ascontiguousarray(L0_ffn_w_up.T),
        wdT=np.ascontiguousarray(L0_ffn_w_down.T),
        w1T=np.ascontiguousarray(L1_attn_qkv_w1.T),
        w2Tp=np.ascontiguousarray(L1_attn_qkv_w2[perm].T),
        cw1p=np.ascontiguousarray(L1_attn_cproj_w1.T[permy]),
        cw2T=np.ascontiguousarray(L1_attn_cproj_w2.T),
        Wr=L1_ffn_Wr,
        br=np.ascontiguousarray(L1_ffn_br.reshape(1, 8)),
        e8col=(np.arange(8, dtype=np.float32) * 8.0).reshape(8, 1),
        ones81=np.ones((8, 1), np.float32),
        ones128=np.ones((128, 128), dtype=BF),
        **consts)
    pp = np.arange(128)
    blkd = np.zeros((128, 16), dtype=BF)
    blkd[pp, pp % 16] = BF(1.0 / 128.0)
    exp16 = np.zeros((16, 128), np.float32)
    exp16[pp % 16, pp] = 1.0
    shared["blkd"] = blkd
    shared["exp16"] = exp16
    W1a = np.asarray(L1_ffn_W1)
    W2a = np.asarray(L1_ffn_W2)
    b1a = np.asarray(L1_ffn_b1)
    b2a = np.asarray(L1_ffn_b2)

    nc = _get("fused", build_fused)
    in_maps = []
    for c in cores:
        b, t0 = c // 4, (c % 4) * 2048
        cosT, sinT = _rope_tables(t0)
        cmask = np.zeros((128, 8), np.float32)
        cmask[:, 4 * b:c] = 1.0
        prior = np.zeros((8, 8), np.float32)
        prior[:, :c] = 1.0
        in_maps.append(dict(
            xT=np.ascontiguousarray(x[b, t0:t0 + TOK, :].T),
            cosT=cosT, sinT=sinT, carrymask=cmask, prior=prior,
            W1own=np.ascontiguousarray(W1a[c]),
            W2own=np.ascontiguousarray(W2a[c]),
            b1own=np.ascontiguousarray(b1a[c].reshape(8, 128).T),
            b2own=np.ascontiguousarray(b2a[c].reshape(8, 128).T),
            **shared))
    res = run_bass_kernel_spmd(nc, in_maps, core_ids=cores)
    out = np.empty((2, 8192, D), np.float32)
    for c in cores:
        b, t0 = c // 4, (c % 4) * 2048
        out[b, t0:t0 + TOK, :] = np.asarray(res.results[c]["outT"]).T
    return out
